# version 25
# baseline (speedup 1.0000x reference)
import os as _os
import numpy as np
import ml_dtypes
from contextlib import ExitStack

KDBG = _os.environ.get("KDBG", "")

import concourse.bass as bass
import concourse.mybir as mybir
from concourse.bass_utils import run_bass_kernel_spmd

F32 = mybir.dt.float32
BF16 = mybir.dt.bfloat16
I32 = mybir.dt.int32
AF = mybir.ActivationFunctionType
ALU = mybir.AluOpType
AX = mybir.AxisListType


class Cfg:
    def __init__(self, D=1024, TOK=2048, HEADS=16, HID=3072, QB=512, PIECE=256,
                 RANK=8, GROUP=4, N_CORES=8):
        self.D, self.TOK, self.HEADS, self.HID = D, TOK, HEADS, HID
        self.QB, self.PIECE, self.RANK, self.GROUP = QB, PIECE, RANK, GROUP
        self.N_CORES = N_CORES
        self.DH = D // HEADS
        assert self.DH == 64
        self.DC = D // 128
        self.HP = HEADS // 2
        assert self.DC == self.HP
        self.NQB = TOK // QB
        self.HT = HID // 128
        assert self.HT % 2 == 0 and (HID // 2) % 128 == 0
        self.NP = TOK // PIECE
        self.GT = GROUP * PIECE
        self.KCP = self.GT // 128
        self.LORA = 2.0
        self.S8 = 1.0 / 127.0
        self.EPS = 1e-5
        self.ISCALE = 0.125


class W:

    def __init__(self, nc, eng, E, sems):
        self.nc, self.eng, self.E, self.sems = nc, eng, E, sems
        self.c = {k: 0 for k in sems}

    def wait(self, eng, sem, val=None):
        if val is None:
            val = self.c[sem]
        if self.eng == eng and val > 0:
            self.E.wait_ge(self.sems[sem], val)

    def tok(self, sem):
        return (sem, self.c[sem])

    def await_tok(self, eng, t):
        if t is not None:
            self.wait(eng, t[0], t[1])

    def await_toks(self, eng, ts):
        for t in ts:
            self.await_tok(eng, t)

    def dma(self, q, out, in_, sem=None, **kw):
        if sem is None:
            sem = {"sync": "sy", "scalar": "sc", "gpsimd": "gp"}[q]
        if self.eng == q:
            self.E.dma_start(out=out, in_=in_, **kw).then_inc(self.sems[sem], 16)
        self.c[sem] += 16

    def mm(self, out, lhsT, rhs, start=True, stop=True, tp=None, inc=None):
        if self.eng == "tensor":
            ins = self.E.matmul(out, lhsT=lhsT, rhs=rhs, start=start, stop=stop,
                                tile_position=tp)
            if inc:
                ins.then_inc(self.sems[inc], 1)
        if inc:
            self.c[inc] += 1

    def transp(self, out, in_, ident, inc=None):
        if self.eng == "tensor":
            ins = self.E.transpose(out, in_, ident)
            if inc:
                ins.then_inc(self.sems[inc], 1)
        if inc:
            self.c[inc] += 1

    def _fin(self, eng, ins, inc, dr):
        if inc:
            if self.eng == eng:
                ins.then_inc(self.sems[inc], 1)
            self.c[inc] += 1
        if dr and self.eng == eng:
            self.E.drain()

    def act(self, fn, *a, inc="act", dr=False, **kw):
        ins = getattr(self.E, fn)(*a, **kw) if self.eng == "scalar" else None
        self._fin("scalar", ins, inc, dr)

    def dve(self, fn, *a, inc="dve", dr=False, **kw):
        ins = getattr(self.E, fn)(*a, **kw) if self.eng == "vector" else None
        self._fin("vector", ins, inc, dr)

    def cc(self, ins_ap, outs_ap, groups):
        if self.eng == "gpsimd":
            self.E.collective_compute(
                "AllGather", ALU.bypass, replica_groups=groups,
                ins=[ins_ap], outs=[outs_ap],
            ).then_inc(self.sems["cc"])
        self.c["cc"] += 1


def build(cfg: Cfg):
    nc = bass.Bass("TRN2", target_bir_lowering=False)
    C = cfg
    D, TOK, QB, HID = C.D, C.TOK, C.QB, C.HID
    DC, HT, HP, NP = C.DC, C.HT, C.HP, C.NP
    H, R = C.HEADS, C.RANK
    PIECE = C.PIECE
    VW = C.DH + 1
    groups = [list(range(g * C.GROUP, (g + 1) * C.GROUP))
              for g in range(C.N_CORES // C.GROUP)]

    def din(name, shape, dt):
        return nc.dram_tensor(name, shape, dt, kind="ExternalInput").ap()

    io = dict(
        xT=din("xT", [D, TOK], F32),
        ident_in=din("ident", [128, 128], BF16),
        wg_in=din("wgT_t", [HT, 128, D], I32),
        wu_in=din("wuT_t", [HT, 128, D], I32),
        wd_in=din("wdT_t", [DC, 128, HID], I32),
        outT=nc.dram_tensor("outT", [D, TOK], F32, kind="ExternalOutput").ap(),
        s_wg=nc.dram_tensor("s_wg", [HT, 128, 2 * D], BF16).ap(),
        s_wd=nc.dram_tensor("s_wd", [DC, 128, HID], BF16).ap(),
        s_wo=nc.dram_tensor("s_wo", [DC, 128, D], BF16).ap(),
        s_wq=nc.dram_tensor("s_wq", [DC, 128, D], BF16).ap(),
        s_wk=nc.dram_tensor("s_wk", [DC, 128, D], BF16).ap(),
        s_wv=nc.dram_tensor("s_wv", [DC, 128, D], BF16).ap(),
        s_qT=nc.dram_tensor("s_qT", [D, TOK], BF16).ap(),
        x1sp=nc.dram_tensor("x1sp", [D, TOK], F32).ap(),
        kp_own=nc.dram_tensor("kp_own", [NP, HP * 128, PIECE], BF16).ap(),
        vp_own=nc.dram_tensor("vp_own", [NP, PIECE, H * VW], BF16).ap(),
        kp_g=nc.dram_tensor("kp_g", [NP, C.GROUP * HP * 128, PIECE], BF16).ap(),
        vp_g=nc.dram_tensor("vp_g", [NP, C.GROUP * PIECE, H * VW], BF16).ap(),
        nrm_own=nc.dram_tensor("nrm_own", [1, H], F32).ap(),
        nrm_g=nc.dram_tensor("nrm_g", [C.GROUP, H], F32).ap(),
        qn_b=nc.dram_tensor("qn_b", [1, H], F32).ap(),
        groups=groups,
    )
    for p in "qkvo":
        io[f"wT_{p}"] = din(f"w{p}T", [DC, 128, D], F32)
        io[f"bi_{p}"] = din(f"b{p}", [D, 1], F32)
        io[f"aT_{p}"] = din(f"a{p}T", [D, R], F32)
        io[f"bT_{p}"] = din(f"b{p}T", [R, D], F32)

    with ExitStack() as top:
        TE = top.enter_context
        P = {}

        def sb(name, shape, dt, n=None, into=P, E=TE):
            if n is None:
                into[name] = E(nc.sbuf_tensor(name, shape, dt))
            else:
                into[name] = [E(nc.sbuf_tensor(f"{name}{i}", shape, dt))
                              for i in range(n)]

        sb("identS", [128, 128], BF16)
        sb("ones1", [1, 128], BF16)
        sb("onesf", [1, 128], F32)
        sb("onesc", [128, 1], BF16)
        sb("Eind", [128, DC * H], BF16)
        sb("negm", [128, H], F32)
        for p in "qkvo":
            sb(f"bias_{p}", [128, DC], F32)
            sb(f"aTs_{p}", [128, DC * R], BF16)
        sb("bTs_o", [R, D], BF16)
        sb("m_r", [1, QB], F32)
        sb("t_r", [1, QB], F32)
        sb("mr_r", [1, QB], F32)
        sb("qrow", [1, H], F32)
        sb("grow", [1, C.GROUP * H], F32)
        sb("krow", [1, H], F32)
        sb("prow", [1, H], F32)
        sb("nrow", [1, H], F32)
        sb("nrow_b", [1, H], BF16)
        sb("qn2", [H, 1], F32)
        sb("kn2", [H, 1], F32)
        sb("kn_sc", [H, C.NQB], F32)
        sb("qn_sc", [H, C.NQB], F32)
        sb("u_s", [R, QB], BF16, n=2)
        sb("sg", [128, QB], BF16, n=2)
        sb("xb", [128, QB], BF16, n=2)
        sb("xsq", [128, QB], BF16, n=2)
        sb("tmp", [128, QB], F32, n=2)
        sb("wtile", [128, D], BF16, n=4)
        sb("gtile", [128, 2 * D], BF16, n=2)
        sb("dtile", [128, HID], BF16, n=2)
        sb("epsr", [1, 1], F32)

        sems = {}
        for s in ("syc", "syx", "syp0", "syp1", "sps0", "sps1", "spd0", "spd1",
                  "syg0", "syg1", "syu0", "syu1",
                  "syd0", "syd1", "syd2", "syd3", "syw0", "syw1", "syw2", "syw3",
                  "sy1", "syq", "syk0", "syk1", "syvp0", "syvp1",
                  "sysk0", "sysk1", "sysv0", "sysv1",
                  "scs", "sco", "gp", "cc", "pe", "act", "dve"):
            sems[s] = TE(nc.semaphore(s))

        sA = ExitStack()
        A = {}
        sb("x_res", [128, DC * TOK], F32, into=A, E=sA.enter_context)
        sb("xnb", [128, DC * TOK], BF16, into=A, E=sA.enter_context)
        sb("hT_A", [128, HT * QB], BF16, into=A, E=sA.enter_context)
        sb("VT_qb", [128, DC * QB], BF16, into=A, E=sA.enter_context)
        sb("kst", [128, QB], BF16, n=2, into=A, E=sA.enter_context)
        sb("ksq", [128, QB], BF16, n=2, into=A, E=sA.enter_context)
        sb("vrow", [128, H * VW], BF16, n=2, into=A, E=sA.enter_context)
        sb("stg_i", [128, D], I32, n=2, into=A, E=sA.enter_context)
        for _p in "qkv":
            sb(f"bTs_{_p}", [R, D], BF16, into=A, E=sA.enter_context)
        sb("wstg", [128, D], F32, n=1, into=A, E=sA.enter_context)
        A["ps"] = [sA.enter_context(nc.psum_tensor(f"psA{i}", [128, QB], F32))
                   for i in range(7)]
        A["psV"] = sA.enter_context(nc.psum_tensor("psV", [128, 1024], BF16))

        B = {"done": False}

        def alloc_B():
            if B["done"]:
                return
            sA.close()
            KW = C.GROUP * PIECE
            sb("kbuf", [128, HP * KW], BF16, n=2, into=B)
            sb("vbuf", [128, C.KCP * H * VW], BF16, n=2, into=B)
            B["oT"] = [TE(nc.sbuf_tensor(f"oT{i}", [65, QB], F32))
                       for i in range(H)]
            sb("hT_B", [128, HT * QB], BF16, into=B)
            sb("xn_q", [128, DC * QB], BF16, into=B)
            sb("x1_q", [128, DC * QB], F32, into=B)
            sb("qt_q", [128, DC * QB], BF16, into=B)
            sb("Pt", [128, QB], BF16, n=4, into=B)
            B["recl"] = P["u_s"][1][0:1, :]
            B["recf"] = P["m_r"][0:1, :]
            B["ps"] = [TE(nc.psum_tensor(f"psB{i}", [128, QB], F32))
                       for i in range(8)]
            B["done"] = True

        with nc.Block() as block:
            def runner(eng_name):
                def f(E):
                    w = W(nc, eng_name, E, sems)
                    emit_all(w, C, io, P, A, B, alloc_B)
                f.__name__ = "_"
                return f
            block.sync(runner("sync"))
            block.scalar(runner("scalar"))
            block.vector(runner("vector"))
            block.gpsimd(runner("gpsimd"))
            block.tensor(runner("tensor"))

    return nc


def emit_all(w: W, C: Cfg, io, P, A, B, alloc_B):
    D, TOK, QB, HID = C.D, C.TOK, C.QB, C.HID
    DC, HT, HP, NQB, NP = C.DC, C.HT, C.HP, C.NQB, C.NP
    H, R, DH = C.HEADS, C.RANK, C.DH
    PIECE, GT, KCP = C.PIECE, C.GT, C.KCP
    VW = DH + 1
    KW = C.GROUP * PIECE
    HH = HID // 2

    def xch(c, q0=0, n=TOK):
        return A["x_res"][:, c * TOK + q0: c * TOK + q0 + n]

    def xnch(c, q0, n=QB):
        return A["xnb"][:, c * TOK + q0: c * TOK + q0 + n]

    war = {}

    for c in range(DC):
        w.dma("sync", xch(c), io["xT"][c * 128:(c + 1) * 128, :], sem="syx")
    x_tok = w.tok("syx")

    w.wait("sync", "syc")
    w.dma("sync", P["identS"][:], io["ident_in"], sem="syc")
    for p in "qkvo":
        for c in range(DC):
            w.wait("sync", "syc")
            w.dma("sync", P[f"bias_{p}"][:, c:c + 1],
                  io[f"bi_{p}"][c * 128:(c + 1) * 128, 0:1], sem="syc")
        w.wait("sync", "dve")
        w.wait("sync", "syc")
        w.dma("sync", A["wstg"][0][:, 0:DC * R].rearrange("p (c r) -> p c r", r=R),
              io[f"aT_{p}"].rearrange("(c p) r -> p c r", p=128), sem="syc")
        w.wait("vector", "syc")
        w.dve("tensor_copy", P[f"aTs_{p}"][:], A["wstg"][0][:, 0:DC * R])
        w.wait("sync", "dve")
        w.wait("sync", "syc")
        w.dma("sync", A["wstg"][0][0:R, 0:D], io[f"bT_{p}"], sem="syc")
        w.wait("vector", "syc")
        w.dve("tensor_copy", (P if p == "o" else A)[f"bTs_{p}"][:],
              A["wstg"][0][0:R, 0:D])

    w.dve("memset", P["epsr"][:], C.EPS)
    w.dve("memset", P["ones1"][:], 1.0)
    w.dve("memset", P["onesf"][:], 1.0)
    w.dve("memset", P["onesc"][:], 1.0)
    w.dve("memset", P["Eind"][:], 0.0, dr=True)
    for c in range(DC):
        w.dve("memset", P["Eind"][0:64, c * H + 2 * c: c * H + 2 * c + 1], 1.0)
        w.dve("memset", P["Eind"][64:128, c * H + 2 * c + 1: c * H + 2 * c + 2], 1.0)
    for i in range(2):
        vr = A["vrow"][i][:, :].rearrange("p (h v) -> p h v", v=VW)
        w.dve("memset", vr[:, :, DH:DH + 1], 1.0)
    p0_dve = w.tok("dve")

    idx = [0]

    def run_preps(preps, ssem="sps"):
        for nm, srct, dst, nt, ncols, nhalf, stgb, sdt, doff in preps:
            for t in range(nt):
                for half in range(nhalf):
                    sl = idx[0] % 2
                    idx[0] += 1
                    cw = ncols // nhalf
                    stg_in = A["stg_i"][sl] if sdt == I32 else A["wstg"][0]
                    w.await_tok("sync", war.get(("prep_in", id(stg_in))))
                    w.dma("sync", stg_in[:, 0:cw],
                          srct[t][:, half * cw:(half + 1) * cw], sem=f"syp{sl}")
                    w.wait("vector", f"syp{sl}")
                    w.await_tok("vector", war.get(("prep_out", id(stgb[sl]))))
                    w.dve("tensor_copy", stgb[sl][:, 0:cw], stg_in[:, 0:cw])
                    war[("prep_in", id(stg_in))] = w.tok("dve")
                    w.wait("gpsimd", "dve")
                    w.dma("gpsimd",
                          dst[t][:, doff + half * cw: doff + (half + 1) * cw],
                          stgb[sl][:, 0:cw], sem=f"{ssem}{sl}")
                    war[("prep_out", id(stgb[sl]))] = w.tok(f"{ssem}{sl}")

    run_preps([("g", io["wg_in"], io["s_wg"], HT, D, 1, P["gtile"], I32, 0),
               ("u", io["wu_in"], io["s_wg"], HT, D, 1, P["gtile"], I32, D)])
    prep_done = [w.tok(s) for s in ("syp0", "syp1", "sps0", "sps1")]
    run_preps([("d", io["wd_in"], io["s_wd"], DC, HID, HID // D,
                P["dtile"], I32, 0)], ssem="spd")
    war[("dtile", 0)] = w.tok("spd0")
    war[("dtile", 1)] = w.tok("spd1")

    pstok = {}

    def ln(src_of, xn_to, ps4, qbw=QB, src_tok=None):
        ps_sum, ps_ssq, ps_r, ps_mr = ps4
        w.await_tok("vector", src_tok)
        for c in range(DC):
            sl = c % 2
            w.await_toks("vector", war.get(("xb", sl), []))
            w.dve("tensor_copy", P["xb"][sl][:, 0:qbw], src_of(c))
            cast_t = w.tok("dve")
            w.wait("scalar", "dve", cast_t[1])
            w.await_tok("scalar", war.get(("xsq", sl)))
            w.act("square", P["xsq"][sl][:, 0:qbw], P["xb"][sl][:, 0:qbw])
            sq_t = w.tok("act")
            w.wait("tensor", "dve", cast_t[1])
            if c == 0:
                w.await_tok("tensor", pstok.get(id(ps_sum)))
                w.await_tok("tensor", pstok.get(id(ps_ssq)))
            w.mm(ps_sum[0:1, 0:qbw], P["onesc"][:, 0:1], P["xb"][sl][:, 0:qbw],
                 start=(c == 0), stop=(c == DC - 1), inc="pe")
            war[("xb", sl)] = [w.tok("pe"), sq_t]
            w.wait("tensor", "act", sq_t[1])
            w.mm(ps_ssq[0:1, 0:qbw], P["onesc"][:, 0:1], P["xsq"][sl][:, 0:qbw],
                 start=(c == 0), stop=(c == DC - 1), inc="pe")
            war[("xsq", sl)] = w.tok("pe")
        mm_t = w.tok("pe")
        w.wait("vector", "pe", mm_t[1])
        w.dve("tensor_scalar_mul", P["m_r"][0:1, 0:qbw], ps_sum[0:1, 0:qbw], 1.0 / D, dr=True)
        w.dve("tensor_mul", P["mr_r"][0:1, 0:qbw], P["m_r"][0:1, 0:qbw],
              P["m_r"][0:1, 0:qbw], dr=True)
        w.dve("scalar_tensor_tensor", P["t_r"][0:1, 0:qbw], ps_ssq[0:1, 0:qbw],
              1.0 / D, P["mr_r"][0:1, 0:qbw], op0=ALU.mult, op1=ALU.subtract, dr=True)
        pstok[id(ps_sum)] = pstok[id(ps_ssq)] = w.tok("dve")
        w.dve("tensor_scalar_add", P["tmp"][0][0:1, 0:qbw], P["t_r"][0:1, 0:qbw],
              C.EPS, dr=True)
        var_t = w.tok("dve")
        w.wait("scalar", "dve", var_t[1])
        w.act("activation", P["t_r"][0:1, 0:qbw], P["tmp"][0][0:1, 0:qbw], AF.Sqrt,
              bias=0.0, scale=1.0)
        sd_t = w.tok("act")
        w.wait("vector", "act", sd_t[1])
        w.dve("reciprocal", P["t_r"][0:1, 0:qbw], P["t_r"][0:1, 0:qbw], dr=True)
        w.dve("tensor_mul", P["mr_r"][0:1, 0:qbw], P["t_r"][0:1, 0:qbw],
              P["t_r"][0:1, 0:qbw], dr=True)
        w.dve("tensor_mul", P["mr_r"][0:1, 0:qbw], P["mr_r"][0:1, 0:qbw],
              P["tmp"][0][0:1, 0:qbw], dr=True)
        w.dve("tensor_scalar", P["mr_r"][0:1, 0:qbw], P["mr_r"][0:1, 0:qbw],
              -0.5, 1.5, op0=ALU.mult, op1=ALU.add, dr=True)
        w.dve("tensor_mul", P["t_r"][0:1, 0:qbw], P["t_r"][0:1, 0:qbw],
              P["mr_r"][0:1, 0:qbw], dr=True)
        w.dve("tensor_mul", P["mr_r"][0:1, 0:qbw], P["m_r"][0:1, 0:qbw],
              P["t_r"][0:1, 0:qbw], dr=True)
        row_t = w.tok("dve")
        w.wait("tensor", "dve", row_t[1])
        w.await_tok("tensor", pstok.get(id(ps_r)))
        w.await_tok("tensor", pstok.get(id(ps_mr)))
        w.mm(ps_r[:, 0:qbw], P["onesf"][0:1, :], P["t_r"][0:1, 0:qbw], inc="pe")
        w.mm(ps_mr[:, 0:qbw], P["onesf"][0:1, :], P["mr_r"][0:1, 0:qbw], inc="pe")
        bc_t = w.tok("pe")
        w.wait("vector", "pe", bc_t[1])
        for c in range(DC):
            sl = c % 2
            w.dve("tensor_mul", P["tmp"][sl][:, 0:qbw], src_of(c), ps_r[:, 0:qbw],
                  dr=True)
            w.dve("tensor_sub", xn_to(c), P["tmp"][sl][:, 0:qbw], ps_mr[:, 0:qbw])
        pstok[id(ps_r)] = pstok[id(ps_mr)] = w.tok("dve")
        return w.tok("dve")

    def ffn(xn_of, res_of, dst_of, ps4, hT, xn_tok, dst_cb=None,
            res_tok=None):
        ps_g, ps_u, ps_d0, ps_d1 = ps4
        s = C.S8
        last_h_t = None
        w.wait("tensor", xn_tok[0], xn_tok[1])
        w.await_tok("vector", res_tok)
        for ht in range(HT):
            sl = ht % 2
            w.await_tok("sync", war.get(("gtile", sl)))
            w.await_tok("sync", war.get(("utile", sl)))
            w.dma("sync", P["gtile"][sl][:], io["s_wg"][ht], sem=f"syg{sl}")
            w.wait("tensor", f"syg{sl}")
            w.await_tok("tensor", pstok.get(id(ps_g)))
            for c in range(DC):
                w.mm(ps_g[:, :], P["gtile"][sl][:, c * 128:(c + 1) * 128],
                     xn_of(c), start=(c == 0), stop=(c == DC - 1),
                     inc="pe" if c == DC - 1 else None)
            gmm_t = w.tok("pe")
            war[("gtile", sl)] = gmm_t
            w.await_tok("tensor", pstok.get(id(ps_u)))
            for c in range(DC):
                w.mm(ps_u[:, :], P["gtile"][sl][:, D + c * 128:D + (c + 1) * 128],
                     xn_of(c), start=(c == 0), stop=(c == DC - 1),
                     inc="pe" if c == DC - 1 else None)
            umm_t = w.tok("pe")
            war[("utile", sl)] = umm_t
            w.wait("scalar", "pe", gmm_t[1])
            w.await_tok("scalar", war.get(("sg", sl)))
            w.act("activation", P["sg"][sl][:], ps_g[:, :], AF.Silu, scale=s)
            sil_t = w.tok("act")
            pstok[id(ps_g)] = sil_t
            w.wait("vector", "pe", umm_t[1])
            w.wait("vector", "act", sil_t[1])
            w.dve("scalar_tensor_tensor", hT[:, ht * QB:(ht + 1) * QB], ps_u[:, :],
                  s, P["sg"][sl][:], op0=ALU.mult, op1=ALU.mult)
            pstok[id(ps_u)] = war[("sg", sl)] = w.tok("dve")
            last_h_t = w.tok("dve")
        for ot in range(DC):
            ps_d = ps_d0 if ot % 2 == 0 else ps_d1
            sl = ot % 2
            w.await_tok("sync", war.get(("dtile", sl)))
            w.dma("sync", P["dtile"][sl][:], io["s_wd"][ot], sem=f"syd{sl}")
            w.wait("tensor", f"syd{sl}")
            w.await_tok("tensor", pstok.get(id(ps_d)))
            w.wait("tensor", "dve", last_h_t[1])
            for h in range(HT):
                w.mm(ps_d[:, :], P["dtile"][sl][:, h * 128:(h + 1) * 128],
                     hT[:, h * QB:(h + 1) * QB], start=(h == 0),
                     stop=(h == HT - 1), inc="pe" if h == HT - 1 else None)
            dmm_t = w.tok("pe")
            war[("dtile", sl)] = dmm_t
            w.wait("vector", "pe", dmm_t[1])
            w.dve("scalar_tensor_tensor", dst_of(ot), ps_d[:, :], 0.5 * s,
                  res_of(ot), op0=ALU.mult, op1=ALU.add)
            pstok[id(ps_d)] = w.tok("dve")
            if dst_cb is not None:
                dst_cb(ot, w.tok("dve"))

    psA = A["ps"]
    w.wait("vector", "syc")
    w.wait("vector", "syx")
    w.await_toks("sync", prep_done)
    for qb in range(NQB):
        q0 = qb * QB
        ln_t = ln(lambda c: xch(c, q0, QB), lambda c: xnch(c, q0),
                  (psA[3], psA[4], psA[5], psA[6]))
        ffn(lambda c: xnch(c, q0), lambda c: xch(c, q0, QB),
            lambda c: xch(c, q0, QB), (psA[0], psA[1], psA[2], psA[3]),
            A["hT_A"], ln_t)
    x1_tok = w.tok("dve")

    run_preps([(p, io[f"wT_{p}"], io[f"s_w{p}"], DC, D, 1, P["wtile"], F32, 0)
               for p in "qkvo"])
    for sl in range(2):
        war[("wtile", sl)] = war[("prep_out", id(P["wtile"][sl]))]
    w.wait("sync", "sps0")
    w.wait("sync", "sps1")

    w.wait("scalar", "dve", x1_tok[1])
    for c in range(DC):
        w.dma("scalar", io["x1sp"][c * 128:(c + 1) * 128, :], xch(c), sem="scs")
    spill_tok = w.tok("scs")

    if KDBG == "x1":
        w.wait("scalar", "dve", x1_tok[1])
        for c in range(DC):
            w.dma("scalar", io["outT"][c * 128:(c + 1) * 128, :], xch(c),
                  sem="sco")
        w.wait("scalar", "sco")
        alloc_B()
        return

    for qb in range(NQB):
        q0 = qb * QB
        xn2_tok = ln(lambda c, q0=q0: xch(c, q0, QB),
                     lambda c, q0=q0: xnch(c, q0),
                     (psA[3], psA[4], psA[5], psA[6]), src_tok=x1_tok)
    w.wait("tensor", "dve", xn2_tok[1])

    def proj(p, qb, q0, dest_cb):
        usl = qb % 2
        deferred = [None]
        w.await_tok("tensor", pstok.get(id(psA[2])))
        for c in range(DC):
            w.mm(psA[2][0:R, :], P[f"aTs_{p}"][:, c * R:(c + 1) * R],
                 xnch(c, q0), start=(c == 0), stop=(c == DC - 1),
                 inc="pe" if c == DC - 1 else None)
        umm_t = w.tok("pe")
        w.wait("vector", "pe", umm_t[1])
        w.await_tok("vector", war.get(("u_s", usl)))
        w.dve("tensor_scalar_mul", P["u_s"][usl][:, :], psA[2][0:R, :], C.LORA)
        pstok[id(psA[2])] = w.tok("dve")
        u_t = w.tok("dve")
        for ot in range(DC):
            wsl = ot % 4
            q = "sync"
            w.await_tok(q, war.get(("wtile", wsl)))
            w.dma(q, P["wtile"][wsl][:], io[f"s_w{p}"][ot],
                  sem=f"syw{wsl}")
            ps = psA[ot % 2]
            w.wait("tensor", f"syw{wsl}")
            w.await_tok("tensor", pstok.get(id(ps)))
            for c in range(DC):
                w.mm(ps[:, :], P["wtile"][wsl][:, c * 128:(c + 1) * 128],
                     xnch(c, q0), start=(c == 0), stop=False)
            w.wait("tensor", "dve", u_t[1])
            w.mm(ps[:, :], A[f"bTs_{p}"][:, ot * 128:(ot + 1) * 128],
                 P["u_s"][usl][0:R, :], start=False, stop=True, inc="pe")
            mm_t = w.tok("pe")
            war[("wtile", wsl)] = mm_t
            war[("u_s", usl)] = mm_t
            nxt = dest_cb(ot, ps, mm_t)
            if deferred[0] is not None:
                deferred[0]()
            deferred[0] = nxt
        if deferred[0] is not None:
            deferred[0]()
            deferred[0] = None

    kv_cc = {}
    for qb in range(NQB):
        q0 = qb * QB

        def k_dest(ot, ps, mm_t, qb=qb, q0=q0):
            sl = ot % 2
            w.wait("vector", "pe", mm_t[1])
            w.await_toks("vector", war.get(("kst", sl), []))
            w.dve("tensor_scalar", A["kst"][sl][:], ps[:, :],
                  P["bias_k"][:, ot:ot + 1], None, op0=ALU.add)
            pstok[id(ps)] = ke_t = w.tok("dve")
            w.wait("scalar", "dve", ke_t[1])
            w.await_tok("scalar", war.get(("ksq", sl)))
            w.act("square", A["ksq"][sl][:], A["kst"][sl][:])
            ksq_t = w.tok("act")
            w.wait("gpsimd", "dve", ke_t[1])
            pc0 = q0 // PIECE
            nsp = QB // PIECE
            w.dma("gpsimd",
                  io["kp_own"][pc0:pc0 + nsp, ot * 128:(ot + 1) * 128, :]
                  .rearrange("s p t -> p s t"),
                  A["kst"][sl][:, :].rearrange("p (s t) -> p s t", t=PIECE),
                  sem=f"sysk{sl}")
            war[("kst", sl)] = [("act", ksq_t[1]), w.tok(f"sysk{sl}")]

            def knmm(ot=ot, sl=sl, ksq_t=ksq_t):
                w.wait("tensor", "act", ksq_t[1])
                if ot == 0:
                    w.await_tok("tensor", pstok.get(id(psA[3])))
                w.mm(psA[3][0:H, :], P["Eind"][:, ot * H:(ot + 1) * H],
                     A["ksq"][sl][:], start=(ot == 0), stop=(ot == DC - 1),
                     inc="pe")
                war[("ksq", sl)] = w.tok("pe")
            return knmm

        proj("k", qb, q0, k_dest)
        knm_t = w.tok("pe")
        w.wait("vector", "pe", knm_t[1])
        w.dve("reduce_max", P["kn_sc"][:, qb:qb + 1], psA[3][0:H, :], axis=AX.X)
        pstok[id(psA[3])] = w.tok("dve")

        def v_dest(ot, ps, mm_t, qb=qb, q0=q0):
            w.wait("vector", "pe", mm_t[1])
            w.await_tok("vector", war.get("VT"))
            w.dve("tensor_scalar", A["VT_qb"][:, ot * QB:(ot + 1) * QB], ps[:, :],
                  P["bias_v"][:, ot:ot + 1], None, op0=ALU.add)
            pstok[id(ps)] = w.tok("dve")

        proj("v", qb, q0, v_dest)
        vev_t = w.tok("dve")
        for tb in range(QB // 128):
            vsl = tb % 2
            w.wait("tensor", "dve", vev_t[1])
            w.await_tok("tensor", war.get(("vrow", vsl)))
            w.await_tok("tensor", pstok.get(id(A["psV"])))
            for c in range(DC):
                w.transp(A["psV"][:, c * 128:(c + 1) * 128],
                         A["VT_qb"][:, c * QB + tb * 128: c * QB + tb * 128 + 128],
                         P["identS"][:], inc="pe" if c == DC - 1 else None)
            tr_t = w.tok("pe")
            war["VT"] = tr_t
            w.wait("vector", "pe", tr_t[1])
            vr = A["vrow"][vsl][:, :].rearrange("p (h v) -> p h v", v=VW)
            w.await_tok("vector", p0_dve)
            w.dve("tensor_copy", vr[:, :, 0:DH],
                  A["psV"][:, 0:D].rearrange("p (h d) -> p h d", d=DH))
            pstok[id(A["psV"])] = vcp_t = w.tok("dve")
            w.wait("gpsimd", "dve", vcp_t[1])
            pc = (q0 + tb * 128) // PIECE
            pr = ((q0 + tb * 128) % PIECE) // 128
            w.dma("gpsimd", io["vp_own"][pc][pr * 128:(pr + 1) * 128, :],
                  A["vrow"][vsl][:], sem=f"sysv{vsl}")
            war[("vrow", vsl)] = w.tok(f"sysv{vsl}")

        for s in ("sysk0", "sysk1", "sysv0", "sysv1"):
            w.wait("gpsimd", s)
        for sp in range(QB // PIECE):
            pc = (q0 + sp * PIECE) // PIECE
            w.cc(io["kp_own"][pc].opt(), io["kp_g"][pc].opt(), io["groups"])
            w.cc(io["vp_own"][pc].opt(), io["vp_g"][pc].opt(), io["groups"])
            kv_cc[pc] = w.tok("cc")

        def q_dest(ot, ps, mm_t, qb=qb, q0=q0):
            sl = ot % 2
            w.wait("vector", "pe", mm_t[1])
            w.await_toks("vector", war.get(("kst", sl), []))
            w.dve("tensor_scalar", A["kst"][sl][:], ps[:, :],
                  P["bias_q"][:, ot:ot + 1], None, op0=ALU.add)
            pstok[id(ps)] = qe_t = w.tok("dve")
            w.wait("scalar", "dve", qe_t[1])
            w.await_tok("scalar", war.get(("ksq", sl)))
            w.act("square", A["ksq"][sl][:], A["kst"][sl][:])
            qsq_t = w.tok("act")
            w.wait("gpsimd", "dve", qe_t[1])
            w.dma("gpsimd", io["s_qT"][ot * 128:(ot + 1) * 128, q0:q0 + QB],
                  A["kst"][sl][:], sem=f"sysk{sl}")
            war[("kst", sl)] = [("act", qsq_t[1]), w.tok(f"sysk{sl}")]

            def qnmm(ot=ot, sl=sl, qsq_t=qsq_t):
                w.wait("tensor", "act", qsq_t[1])
                if ot == 0:
                    w.await_tok("tensor", pstok.get(id(psA[3])))
                w.mm(psA[3][0:H, :], P["Eind"][:, ot * H:(ot + 1) * H],
                     A["ksq"][sl][:], start=(ot == 0), stop=(ot == DC - 1),
                     inc="pe")
                war[("ksq", sl)] = w.tok("pe")
            return qnmm

        proj("q", qb, q0, q_dest)
        qnm_t = w.tok("pe")
        w.wait("vector", "pe", qnm_t[1])
        w.dve("reduce_max", P["qn_sc"][:, qb:qb + 1], psA[3][0:H, :], axis=AX.X)
        pstok[id(psA[3])] = w.tok("dve")

    w.wait("vector", "dve")
    w.dve("reduce_max", P["kn2"][:, 0:1], P["kn_sc"][:, :], axis=AX.X)
    w.dve("reduce_max", P["qn2"][:, 0:1], P["qn_sc"][:, :], axis=AX.X)
    n_t = w.tok("dve")
    w.wait("gpsimd", "dve", n_t[1])
    w.dma("gpsimd", io["nrm_own"].rearrange("a h -> h a"), P["kn2"][:, 0:1])
    w.dma("gpsimd", io["qn_b"].rearrange("a h -> h a"), P["qn2"][:, 0:1])
    w.wait("gpsimd", "gp")
    w.cc(io["nrm_own"].opt(), io["nrm_g"].opt(), io["groups"])
    nrm_cc = w.tok("cc")
    w.wait("gpsimd", "cc", nrm_cc[1])
    w.dma("gpsimd", P["grow"][0:1, :], io["nrm_g"].rearrange("r h -> (r h)"))
    w.dma("gpsimd", P["qrow"][0:1, :], io["qn_b"])
    row_ld = w.tok("gp")
    w.wait("vector", "gp", row_ld[1])
    w.dve("tensor_max", P["krow"][0:1, :], P["grow"][0:1, 0:H],
          P["grow"][0:1, H:2 * H], dr=True)
    for r in range(2, C.GROUP):
        w.dve("tensor_max", P["krow"][0:1, :], P["krow"][0:1, :],
              P["grow"][0:1, r * H:(r + 1) * H], dr=True)
    w.dve("tensor_mul", P["prow"][0:1, :], P["qrow"][0:1, :], P["krow"][0:1, :], dr=True)
    p_t = w.tok("dve")
    w.wait("scalar", "dve", p_t[1])
    w.act("activation", P["nrow"][0:1, :], P["prow"][0:1, :], AF.Sqrt,
          bias=0.0, scale=1.02 / 64.0)
    sq_t = w.tok("act")
    w.wait("vector", "act", sq_t[1])
    w.dve("tensor_scalar_mul", P["nrow_b"][0:1, :], P["nrow"][0:1, :], -1.0, dr=True)
    nb_t = w.tok("dve")
    w.wait("tensor", "dve", nb_t[1])
    w.await_tok("tensor", pstok.get(id(psA[3])))
    w.mm(psA[3][:, 0:H], P["ones1"][0:1, :], P["nrow_b"][0:1, :], inc="pe")
    nm_t = w.tok("pe")
    w.wait("vector", "pe", nm_t[1])
    w.dve("tensor_copy", P["negm"][:], psA[3][:, 0:H])
    negm_tok = w.tok("dve")

    for sem in list(w.c):
        if sem not in ("cc",):
            for e in ("sync", "scalar", "vector", "gpsimd", "tensor"):
                w.wait(e, sem)
    alloc_B()
    kbuf, vbuf, oT = B["kbuf"], B["vbuf"], B["oT"]
    hT_B, xn_q, x1_q, qt_q = B["hT_B"], B["xn_q"], B["x1_q"], B["qt_q"]
    aoT = hT_B
    psB = B["ps"]

    kv_war = [None, None]

    for qb in range(NQB):
        q0 = qb * QB

        w.wait("sync", "scs", spill_tok[1])
        w.await_tok("sync", war.get("x1q"))
        for c in range(DC):
            w.dma("sync", x1_q[:, c * QB:(c + 1) * QB],
                  io["x1sp"][c * 128:(c + 1) * 128, q0:q0 + QB], sem="sy1")
        x1q_tok = w.tok("sy1")
        w.await_tok("sync", war.get("qt_q"))
        for c in range(DC):
            w.dma("sync", qt_q[:, c * QB:(c + 1) * QB],
                  io["s_qT"][c * 128:(c + 1) * 128, q0:q0 + QB], sem="syq")
        qt_tok = w.tok("syq")

        piece_ld = {}

        def load_piece(pc, qb=qb):
            slot = pc % 2
            w.await_tok("sync", kv_war[slot])
            w.wait("sync", "cc")
            for hp in range(HP):
                w.dma("sync",
                      kbuf[slot][:, hp * KW:(hp + 1) * KW]
                      .rearrange("p (r t) -> p r t", t=PIECE),
                      io["kp_g"][pc].rearrange("(r hh p) t -> p hh r t",
                                               p=128, hh=HP)[:, hp, :, :],
                      sem=f"syk{slot}")
            hk = KCP // 2
            w.await_tok("gpsimd", kv_war[slot])
            w.wait("gpsimd", "cc")
            for half in range(2):
                w.dma("gpsimd",
                      vbuf[slot][:, half * hk * H * VW:(half + 1) * hk * H * VW]
                      .rearrange("p (tc f) -> p tc f", f=H * VW),
                      io["vp_g"][pc].rearrange("(tc p) f -> p tc f", p=128)
                      [:, half * hk:(half + 1) * hk, :], sem=f"syvp{slot}")
            piece_ld[pc] = [w.tok(f"syk{slot}"), w.tok(f"syvp{slot}")]

        load_piece(0)
        if NP > 1:
            load_piece(1)

        units = [(pc, hp, kc) for pc in range(NP) for hp in range(HP)
                 for kc in range(KCP)]
        S_free = [None, None]
        exp_tok = {}
        av_tok = {}
        oT_free = {}

        def emit_av(u):
            pc_, hp_, kc_ = units[u]
            slot_ = pc_ % 2
            sp_ = u % 2
            w.wait("tensor", "act", exp_tok[u][1])
            if kc_ == 0:
                w.await_tok("tensor", oT_free.get(hp_ % 2))
                w.await_tok("tensor", pstok.get(id(psB[4 + 2 * (hp_ % 2)])))
                w.await_tok("tensor", pstok.get(id(psB[5 + 2 * (hp_ % 2)])))
            vb0 = kc_ * H * VW
            w.mm(psB[4 + 2 * (hp_ % 2)][0:65, :],
                 vbuf[slot_][:, vb0 + 2 * hp_ * VW: vb0 + 2 * hp_ * VW + VW],
                 B["Pt"][2 * sp_][:], start=(kc_ == 0), stop=(kc_ == KCP - 1))
            w.mm(psB[5 + 2 * (hp_ % 2)][0:65, :],
                 vbuf[slot_][:, vb0 + (2 * hp_ + 1) * VW: vb0 + (2 * hp_ + 1) * VW + VW],
                 B["Pt"][2 * sp_ + 1][:], start=(kc_ == 0),
                 stop=(kc_ == KCP - 1), inc="pe")
            av_tok[u] = w.tok("pe")
            if kc_ == KCP - 1:
                w.wait("vector", "pe", av_tok[u][1])
                for side in range(2):
                    head = 2 * hp_ + side
                    ps_o = psB[4 + 2 * (hp_ % 2) + side]
                    w.await_tok("vector", war.get(("oT", head)))
                    if pc_ == 0:
                        w.dve("tensor_copy", oT[head][:, :], ps_o[0:65, :])
                    else:
                        w.dve("tensor_add", oT[head][:, :], oT[head][:, :],
                              ps_o[0:65, :])
                    pstok[id(ps_o)] = war[("oT", head)] = w.tok("dve")
                oT_free[hp_ % 2] = w.tok("dve")
                if hp_ == HP - 1:
                    kv_war[slot_] = w.tok("pe")
                    if pc_ + 2 < NP:
                        load_piece(pc_ + 2)

        for ui, (pc, hp, kc) in enumerate(units):
            slot = pc % 2
            sp = ui % 2
            if kc == 0 and hp == 0:
                w.await_toks("tensor", piece_ld[pc])
                w.await_tok("tensor", qt_tok)
            w.await_tok("tensor", S_free[sp])
            w.await_tok("tensor", pstok.get(id(psB[2 * sp])))
            w.await_tok("tensor", pstok.get(id(psB[2 * sp + 1])))
            kb0 = hp * KW + kc * 128
            w.mm(psB[2 * sp][:, :], kbuf[slot][0:64, kb0:kb0 + 128],
                 qt_q[0:64, hp * QB:(hp + 1) * QB], tp=(0, 0))
            w.mm(psB[2 * sp + 1][:, :], kbuf[slot][64:128, kb0:kb0 + 128],
                 qt_q[64:128, hp * QB:(hp + 1) * QB], tp=(64, 0), inc="pe")
            qk_t = w.tok("pe")
            w.wait("scalar", "pe", qk_t[1])
            if ui == 0 and qb == 0:
                w.wait("scalar", "dve", negm_tok[1])
            w.await_tok("scalar", av_tok.get(ui - 2))
            w.act("activation", B["Pt"][2 * sp][:], psB[2 * sp][:, :], AF.Exp,
                  bias=P["negm"][:, 2 * hp:2 * hp + 1], scale=C.ISCALE)
            w.act("activation", B["Pt"][2 * sp + 1][:], psB[2 * sp + 1][:, :],
                  AF.Exp, bias=P["negm"][:, 2 * hp + 1:2 * hp + 2],
                  scale=C.ISCALE)
            S_free[sp] = exp_tok[ui] = w.tok("act")
            pstok[id(psB[2 * sp])] = pstok[id(psB[2 * sp + 1])] = exp_tok[ui]
            if ui >= 1:
                emit_av(ui - 1)
        emit_av(len(units) - 1)

        att_t = w.tok("dve")

        bc_war = [None, None]
        w.wait("vector", "dve", att_t[1])
        for head in range(H):
            hp, side = head // 2, head % 2
            w.dve("reciprocal", B["recf"], oT[head][64:65, :], dr=True)
            w.dve("tensor_copy", B["recl"], B["recf"])
            rc_t = w.tok("dve")
            ps_bc = psB[head % 2]
            w.wait("tensor", "dve", rc_t[1])
            w.await_tok("tensor", bc_war[head % 2])
            w.await_tok("tensor", pstok.get(id(ps_bc)))
            w.mm(ps_bc[0:64, :], P["ones1"][0:1, 0:64], B["recl"],
                 inc="pe")
            bm_t = w.tok("pe")
            w.wait("vector", "pe", bm_t[1])
            w.dve("tensor_mul",
                  aoT[side * 64:(side + 1) * 64, hp * QB:(hp + 1) * QB],
                  oT[head][0:64, :], ps_bc[0:64, :])
            bc_war[head % 2] = pstok[id(ps_bc)] = w.tok("dve")
        aoT_t = w.tok("dve")

        w.await_tok("tensor", pstok.get(id(psB[2])))
        w.wait("tensor", "dve", aoT_t[1])
        for c in range(DC):
            w.mm(psB[2][0:R, :], P["aTs_o"][:, c * R:(c + 1) * R],
                 aoT[:, c * QB:(c + 1) * QB], start=(c == 0), stop=(c == DC - 1),
                 inc="pe" if c == DC - 1 else None)
        uo_t = w.tok("pe")
        w.wait("vector", "pe", uo_t[1])
        w.await_tok("vector", war.get(("u_s", 0)))
        w.dve("tensor_scalar_mul", P["u_s"][0][:, :], psB[2][0:R, :], C.LORA)
        pstok[id(psB[2])] = uo_e = w.tok("dve")
        for ot in range(DC):
            wsl = ot % 2
            q = "sync"
            w.await_tok(q, war.get(("wtile", wsl)))
            w.dma(q, P["wtile"][wsl][:], io["s_wo"][ot], sem=f"syw{wsl}")
            ps = psB[ot % 2]
            w.wait("tensor", f"syw{wsl}")
            w.await_tok("tensor", pstok.get(id(ps)))
            for c in range(DC):
                w.mm(ps[:, :], P["wtile"][wsl][:, c * 128:(c + 1) * 128],
                     aoT[:, c * QB:(c + 1) * QB], start=(c == 0), stop=False)
            w.wait("tensor", "dve", uo_e[1])
            w.mm(ps[:, :], P["bTs_o"][:, ot * 128:(ot + 1) * 128],
                 P["u_s"][0][0:R, :], start=False, stop=True, inc="pe")
            om_t = w.tok("pe")
            war[("wtile", wsl)] = om_t
            war[("u_s", 0)] = om_t
            w.wait("vector", "pe", om_t[1])
            w.await_tok("vector", x1q_tok)
            w.dve("scalar_tensor_tensor", x1_q[:, ot * QB:(ot + 1) * QB],
                  ps[:, :], P["bias_o"][:, ot:ot + 1],
                  x1_q[:, ot * QB:(ot + 1) * QB], op0=ALU.add, op1=ALU.add)
            pstok[id(ps)] = pstok["x2t"] = w.tok("dve")
        war["qt_q"] = w.tok("pe")

        if KDBG == "x2":
            x2_t = w.tok("dve")
            for ot in range(DC):
                w.wait("scalar", "dve", x2_t[1])
                w.dma("scalar", io["outT"][ot * 128:(ot + 1) * 128, q0:q0 + QB],
                      x1_q[:, ot * QB:(ot + 1) * QB], sem="sco")
            war["x1q"] = w.tok("sco")
            continue

        ln3_t = ln(lambda c: x1_q[:, c * QB:(c + 1) * QB],
                   lambda c: xn_q[:, c * QB:(c + 1) * QB],
                   (psB[3], psB[4], psB[5], psB[6]),
                   src_tok=pstok.get("x2t"))

        st_toks = {}
        ffn(lambda c: xn_q[:, c * QB:(c + 1) * QB],
            lambda c: x1_q[:, c * QB:(c + 1) * QB],
            lambda c: x1_q[:, c * QB:(c + 1) * QB],
            (psB[0], psB[1], psB[2], psB[3]), hT_B, ln3_t,
            dst_cb=lambda ot, tok: st_toks.__setitem__(ot, tok),
            res_tok=pstok.get("x2t"))

        for ot in range(DC):
            w.wait("scalar", "dve", st_toks[ot][1])
            w.dma("scalar", io["outT"][ot * 128:(ot + 1) * 128, q0:q0 + QB],
                  x1_q[:, ot * QB:(ot + 1) * QB], sem="sco")
        war["x1q"] = w.tok("sco")

    w.wait("scalar", "sco")
    w.wait("gpsimd", "cc")


def _prep_inputs(C, inputs, prefix="L0_"):
    f = np.float32
    D, TOK, H, R = C.D, C.TOK, C.HEADS, C.RANK
    DC, HT, HID = C.DC, C.HT, C.HID

    x = np.asarray(inputs[prefix + "attn_x"], dtype=f)

    def tile_w(wmat):
        wt = np.asarray(wmat, dtype=f).T
        t = wt.reshape(DC, 128, DC, 128).transpose(2, 1, 0, 3).reshape(DC, 128, D)
        return np.ascontiguousarray(t)

    def tile_in(wmat, nt, ncols):
        wt = np.asarray(wmat).T
        ic = wt.shape[0] // 128
        t = wt.reshape(ic, 128, nt, 128).transpose(2, 1, 0, 3).reshape(nt, 128, ncols)
        return np.ascontiguousarray(t.astype(np.int32))

    common = dict(ident=np.eye(128, dtype=ml_dtypes.bfloat16))
    for p, nm in zip("qkvo", ["q", "k", "v", "o"]):
        common[f"w{p}T"] = tile_w(inputs[f"{prefix}attn_W{nm}"])
        common[f"b{p}"] = np.ascontiguousarray(
            np.asarray(inputs[f"{prefix}attn_b{nm}"], dtype=f).reshape(D, 1))
        common[f"a{p}T"] = np.ascontiguousarray(
            np.asarray(inputs[f"{prefix}attn_A{nm}"], dtype=f).T)
        common[f"b{p}T"] = np.ascontiguousarray(
            np.asarray(inputs[f"{prefix}attn_B{nm}"], dtype=f).T)
    common["wgT_t"] = tile_in(inputs[prefix + "ffn_w_gate"], HT, D)
    common["wuT_t"] = tile_in(inputs[prefix + "ffn_w_up"], HT, D)
    common["wdT_t"] = tile_in(inputs[prefix + "ffn_w_down"], DC, HID)

    B = x.shape[0]
    cores_per_b = C.N_CORES // B
    in_maps = []
    for core in range(C.N_CORES):
        b = core // cores_per_b
        t0 = (core % cores_per_b) * TOK
        m = dict(common)
        m["xT"] = np.ascontiguousarray(x[b, t0:t0 + TOK, :].T)
        in_maps.append(m)
    return in_maps, x.shape


def _gather_out(C, results, shape):
    B, S, D = shape
    cores_per_b = C.N_CORES // B
    out = np.empty(shape, dtype=np.float32)
    for core in range(C.N_CORES):
        b = core // cores_per_b
        t0 = (core % cores_per_b) * C.TOK
        out[b, t0:t0 + C.TOK, :] = results[core]["outT"].T
    return out


def kernel(**inputs):
    C = Cfg()
    in_maps, shape = _prep_inputs(C, inputs)
    nc = build(C)
    res = run_bass_kernel_spmd(nc, in_maps, core_ids=list(range(C.N_CORES)))
    return _gather_out(C, res.results, shape)


# version 26
# speedup vs baseline: 1.0028x; 1.0028x over previous
import os as _os
import numpy as np
import ml_dtypes
from contextlib import ExitStack

KDBG = _os.environ.get("KDBG", "")

import concourse.bass as bass
import concourse.mybir as mybir
from concourse.bass_utils import run_bass_kernel_spmd

F32 = mybir.dt.float32
BF16 = mybir.dt.bfloat16
I32 = mybir.dt.int32
AF = mybir.ActivationFunctionType
ALU = mybir.AluOpType
AX = mybir.AxisListType


class Cfg:
    def __init__(self, D=1024, TOK=2048, HEADS=16, HID=3072, QB=512, PIECE=256,
                 RANK=8, GROUP=4, N_CORES=8):
        self.D, self.TOK, self.HEADS, self.HID = D, TOK, HEADS, HID
        self.QB, self.PIECE, self.RANK, self.GROUP = QB, PIECE, RANK, GROUP
        self.N_CORES = N_CORES
        self.DH = D // HEADS
        assert self.DH == 64
        self.DC = D // 128
        self.HP = HEADS // 2
        assert self.DC == self.HP
        self.NQB = TOK // QB
        self.HT = HID // 128
        assert self.HT % 2 == 0 and (HID // 2) % 128 == 0
        self.NP = TOK // PIECE
        self.GT = GROUP * PIECE
        self.KCP = self.GT // 128
        self.LORA = 2.0
        self.S8 = 1.0 / 127.0
        self.EPS = 1e-5
        self.ISCALE = 0.125


class W:

    def __init__(self, nc, eng, E, sems):
        self.nc, self.eng, self.E, self.sems = nc, eng, E, sems
        self.c = {k: 0 for k in sems}

    def wait(self, eng, sem, val=None):
        if val is None:
            val = self.c[sem]
        if self.eng == eng and val > 0:
            self.E.wait_ge(self.sems[sem], val)

    def tok(self, sem):
        return (sem, self.c[sem])

    def await_tok(self, eng, t):
        if t is not None:
            self.wait(eng, t[0], t[1])

    def await_toks(self, eng, ts):
        for t in ts:
            self.await_tok(eng, t)

    def dma(self, q, out, in_, sem=None, **kw):
        if sem is None:
            sem = {"sync": "sy", "scalar": "sc", "gpsimd": "gp"}[q]
        if self.eng == q:
            self.E.dma_start(out=out, in_=in_, **kw).then_inc(self.sems[sem], 16)
        self.c[sem] += 16

    def mm(self, out, lhsT, rhs, start=True, stop=True, tp=None, inc=None):
        if self.eng == "tensor":
            ins = self.E.matmul(out, lhsT=lhsT, rhs=rhs, start=start, stop=stop,
                                tile_position=tp)
            if inc:
                ins.then_inc(self.sems[inc], 1)
        if inc:
            self.c[inc] += 1

    def transp(self, out, in_, ident, inc=None):
        if self.eng == "tensor":
            ins = self.E.transpose(out, in_, ident)
            if inc:
                ins.then_inc(self.sems[inc], 1)
        if inc:
            self.c[inc] += 1

    def _fin(self, eng, ins, inc, dr):
        if inc:
            if self.eng == eng:
                ins.then_inc(self.sems[inc], 1)
            self.c[inc] += 1
        if dr and self.eng == eng:
            self.E.drain()

    def act(self, fn, *a, inc="act", dr=False, **kw):
        ins = getattr(self.E, fn)(*a, **kw) if self.eng == "scalar" else None
        self._fin("scalar", ins, inc, dr)

    def dve(self, fn, *a, inc="dve", dr=False, **kw):
        ins = getattr(self.E, fn)(*a, **kw) if self.eng == "vector" else None
        self._fin("vector", ins, inc, dr)

    def cc(self, ins_ap, outs_ap, groups):
        if self.eng == "gpsimd":
            self.E.collective_compute(
                "AllGather", ALU.bypass, replica_groups=groups,
                ins=[ins_ap], outs=[outs_ap],
            ).then_inc(self.sems["cc"])
        self.c["cc"] += 1


def build(cfg: Cfg):
    nc = bass.Bass("TRN2", target_bir_lowering=False)
    C = cfg
    D, TOK, QB, HID = C.D, C.TOK, C.QB, C.HID
    DC, HT, HP, NP = C.DC, C.HT, C.HP, C.NP
    H, R = C.HEADS, C.RANK
    PIECE = C.PIECE
    VW = C.DH + 1
    groups = [list(range(g * C.GROUP, (g + 1) * C.GROUP))
              for g in range(C.N_CORES // C.GROUP)]

    def din(name, shape, dt):
        return nc.dram_tensor(name, shape, dt, kind="ExternalInput").ap()

    io = dict(
        xT=din("xT", [D, TOK], F32),
        ident_in=din("ident", [128, 128], BF16),
        wg_in=din("wgT_t", [HT, 128, D], I32),
        wu_in=din("wuT_t", [HT, 128, D], I32),
        wd_in=din("wdT_t", [DC, 128, HID], I32),
        outT=nc.dram_tensor("outT", [D, TOK], F32, kind="ExternalOutput").ap(),
        s_wg=nc.dram_tensor("s_wg", [HT, 128, 2 * D], BF16).ap(),
        s_wd=nc.dram_tensor("s_wd", [DC, 128, HID], BF16).ap(),
        s_wo=nc.dram_tensor("s_wo", [DC, 128, D], BF16).ap(),
        s_wq=nc.dram_tensor("s_wq", [DC, 128, D], BF16).ap(),
        s_wk=nc.dram_tensor("s_wk", [DC, 128, D], BF16).ap(),
        s_wv=nc.dram_tensor("s_wv", [DC, 128, D], BF16).ap(),
        s_qT=nc.dram_tensor("s_qT", [D, TOK], BF16).ap(),
        x1sp=nc.dram_tensor("x1sp", [D, TOK], F32).ap(),
        kp_own=nc.dram_tensor("kp_own", [NP, HP * 128, PIECE], BF16).ap(),
        vp_own=nc.dram_tensor("vp_own", [NP, PIECE, H * VW], BF16).ap(),
        kp_g=nc.dram_tensor("kp_g", [NP, C.GROUP * HP * 128, PIECE], BF16).ap(),
        vp_g=nc.dram_tensor("vp_g", [NP, C.GROUP * PIECE, H * VW], BF16).ap(),
        nrm_own=nc.dram_tensor("nrm_own", [1, H], F32).ap(),
        nrm_g=nc.dram_tensor("nrm_g", [C.GROUP, H], F32).ap(),
        qn_b=nc.dram_tensor("qn_b", [1, H], F32).ap(),
        groups=groups,
    )
    for p in "qkvo":
        io[f"wT_{p}"] = din(f"w{p}T", [DC, 128, D], F32)
        io[f"bi_{p}"] = din(f"b{p}", [D, 1], F32)
        io[f"aT_{p}"] = din(f"a{p}T", [D, R], F32)
        io[f"bT_{p}"] = din(f"b{p}T", [R, D], F32)

    with ExitStack() as top:
        TE = top.enter_context
        P = {}

        def sb(name, shape, dt, n=None, into=P, E=TE):
            if n is None:
                into[name] = E(nc.sbuf_tensor(name, shape, dt))
            else:
                into[name] = [E(nc.sbuf_tensor(f"{name}{i}", shape, dt))
                              for i in range(n)]

        sb("identS", [128, 128], BF16)
        sb("ones1", [1, 128], BF16)
        sb("onesf", [1, 128], F32)
        sb("onesc", [128, 1], BF16)
        sb("Eind", [128, DC * H], BF16)
        sb("negm", [128, H], F32)
        for p in "qkvo":
            sb(f"bias_{p}", [128, DC], F32)
            sb(f"aTs_{p}", [128, DC * R], BF16)
        sb("bTs_o", [R, D], BF16)
        sb("m_r", [1, QB], F32)
        sb("t_r", [1, QB], F32)
        sb("mr_r", [1, QB], F32)
        sb("qrow", [1, H], F32)
        sb("grow", [1, C.GROUP * H], F32)
        sb("krow", [1, H], F32)
        sb("prow", [1, H], F32)
        sb("nrow", [1, H], F32)
        sb("nrow_b", [1, H], BF16)
        sb("qn2", [H, 1], F32)
        sb("kn2", [H, 1], F32)
        sb("kn_sc", [H, C.NQB], F32)
        sb("qn_sc", [H, C.NQB], F32)
        sb("u_s", [R, QB], BF16, n=2)
        sb("sg", [128, QB], BF16, n=2)
        sb("xb", [128, QB], BF16, n=2)
        sb("xsq", [128, QB], BF16, n=2)
        sb("tmp", [128, QB], F32, n=2)
        sb("wtile", [128, D], BF16, n=4)
        sb("gtile", [128, 2 * D], BF16, n=2)
        sb("dtile", [128, HID], BF16, n=2)
        sb("epsr", [1, 1], F32)

        sems = {}
        for s in ("syc", "syx", "syp0", "syp1", "sps0", "sps1", "spd0", "spd1",
                  "syg0", "syg1", "syu0", "syu1",
                  "syd0", "syd1", "syd2", "syd3", "syw0", "syw1", "syw2", "syw3",
                  "sy1", "syq", "syk0", "syk1", "syvp0", "syvp1",
                  "sysk0", "sysk1", "sysv0", "sysv1",
                  "scs", "sco", "gp", "cc", "pe", "act", "dve"):
            sems[s] = TE(nc.semaphore(s))

        sA = ExitStack()
        A = {}
        sb("x_res", [128, DC * TOK], F32, into=A, E=sA.enter_context)
        sb("xnb", [128, DC * TOK], BF16, into=A, E=sA.enter_context)
        sb("hT_A", [128, HT * QB], BF16, into=A, E=sA.enter_context)
        sb("VT_qb", [128, DC * QB], BF16, into=A, E=sA.enter_context)
        sb("kst", [128, QB], BF16, n=2, into=A, E=sA.enter_context)
        sb("ksq", [128, QB], BF16, n=2, into=A, E=sA.enter_context)
        sb("vrow", [128, H * VW], BF16, n=2, into=A, E=sA.enter_context)
        sb("stg_i", [128, D], I32, n=2, into=A, E=sA.enter_context)
        for _p in "qkv":
            sb(f"bTs_{_p}", [R, D], BF16, into=A, E=sA.enter_context)
        sb("wstg", [128, D], F32, n=1, into=A, E=sA.enter_context)
        A["ps"] = [sA.enter_context(nc.psum_tensor(f"psA{i}", [128, QB], F32))
                   for i in range(7)]
        A["psV"] = sA.enter_context(nc.psum_tensor("psV", [128, 1024], BF16))

        B = {"done": False}

        def alloc_B():
            if B["done"]:
                return
            sA.close()
            KW = C.GROUP * PIECE
            sb("kbuf", [128, HP * KW], BF16, n=2, into=B)
            sb("vbuf", [128, C.KCP * H * VW], BF16, n=2, into=B)
            B["oT"] = [TE(nc.sbuf_tensor(f"oT{i}", [65, QB], F32))
                       for i in range(H)]
            sb("hT_B", [128, HT * QB], BF16, into=B)
            sb("xn_q", [128, DC * QB], BF16, into=B)
            sb("x1_q", [128, DC * QB], F32, into=B)
            sb("qt_q", [128, DC * QB], BF16, into=B)
            sb("Pt", [128, QB], BF16, n=4, into=B)
            B["recl"] = P["u_s"][1][0:1, :]
            B["recf"] = P["m_r"][0:1, :]
            B["ps"] = [TE(nc.psum_tensor(f"psB{i}", [128, QB], F32))
                       for i in range(8)]
            B["done"] = True

        with nc.Block() as block:
            def runner(eng_name):
                def f(E):
                    w = W(nc, eng_name, E, sems)
                    emit_all(w, C, io, P, A, B, alloc_B)
                f.__name__ = "_"
                return f
            block.sync(runner("sync"))
            block.scalar(runner("scalar"))
            block.vector(runner("vector"))
            block.gpsimd(runner("gpsimd"))
            block.tensor(runner("tensor"))

    return nc


def emit_all(w: W, C: Cfg, io, P, A, B, alloc_B):
    D, TOK, QB, HID = C.D, C.TOK, C.QB, C.HID
    DC, HT, HP, NQB, NP = C.DC, C.HT, C.HP, C.NQB, C.NP
    H, R, DH = C.HEADS, C.RANK, C.DH
    PIECE, GT, KCP = C.PIECE, C.GT, C.KCP
    VW = DH + 1
    KW = C.GROUP * PIECE
    HH = HID // 2

    def xch(c, q0=0, n=TOK):
        return A["x_res"][:, c * TOK + q0: c * TOK + q0 + n]

    def xnch(c, q0, n=QB):
        return A["xnb"][:, c * TOK + q0: c * TOK + q0 + n]

    war = {}

    for c in range(DC):
        w.dma("sync", xch(c), io["xT"][c * 128:(c + 1) * 128, :], sem="syx")
    x_tok = w.tok("syx")

    w.wait("sync", "syc")
    w.dma("sync", P["identS"][:], io["ident_in"], sem="syc")
    for p in "qkvo":
        for c in range(DC):
            w.wait("sync", "syc")
            w.dma("sync", P[f"bias_{p}"][:, c:c + 1],
                  io[f"bi_{p}"][c * 128:(c + 1) * 128, 0:1], sem="syc")
        w.wait("sync", "dve")
        w.wait("sync", "syc")
        w.dma("sync", A["wstg"][0][:, 0:DC * R].rearrange("p (c r) -> p c r", r=R),
              io[f"aT_{p}"].rearrange("(c p) r -> p c r", p=128), sem="syc")
        w.wait("vector", "syc")
        w.dve("tensor_copy", P[f"aTs_{p}"][:], A["wstg"][0][:, 0:DC * R])
        w.wait("sync", "dve")
        w.wait("sync", "syc")
        w.dma("sync", A["wstg"][0][0:R, 0:D], io[f"bT_{p}"], sem="syc")
        w.wait("vector", "syc")
        w.dve("tensor_copy", (P if p == "o" else A)[f"bTs_{p}"][:],
              A["wstg"][0][0:R, 0:D])

    w.dve("memset", P["epsr"][:], C.EPS)
    w.dve("memset", P["ones1"][:], 1.0)
    w.dve("memset", P["onesf"][:], 1.0)
    w.dve("memset", P["onesc"][:], 1.0)
    w.dve("memset", P["Eind"][:], 0.0, dr=True)
    for c in range(DC):
        w.dve("memset", P["Eind"][0:64, c * H + 2 * c: c * H + 2 * c + 1], 1.0)
        w.dve("memset", P["Eind"][64:128, c * H + 2 * c + 1: c * H + 2 * c + 2], 1.0)
    for i in range(2):
        vr = A["vrow"][i][:, :].rearrange("p (h v) -> p h v", v=VW)
        w.dve("memset", vr[:, :, DH:DH + 1], 1.0)
    p0_dve = w.tok("dve")

    idx = [0]

    def run_preps(preps, ssem="sps"):
        for nm, srct, dst, nt, ncols, nhalf, stgb, sdt, doff in preps:
            for t in range(nt):
                for half in range(nhalf):
                    sl = idx[0] % 2
                    idx[0] += 1
                    cw = ncols // nhalf
                    stg_in = A["stg_i"][sl] if sdt == I32 else A["wstg"][0]
                    w.await_tok("sync", war.get(("prep_in", id(stg_in))))
                    w.dma("sync", stg_in[:, 0:cw],
                          srct[t][:, half * cw:(half + 1) * cw], sem=f"syp{sl}")
                    w.wait("vector", f"syp{sl}")
                    w.await_tok("vector", war.get(("prep_out", id(stgb[sl]))))
                    w.dve("tensor_copy", stgb[sl][:, 0:cw], stg_in[:, 0:cw])
                    war[("prep_in", id(stg_in))] = w.tok("dve")
                    w.wait("gpsimd", "dve")
                    w.dma("gpsimd",
                          dst[t][:, doff + half * cw: doff + (half + 1) * cw],
                          stgb[sl][:, 0:cw], sem=f"{ssem}{sl}")
                    war[("prep_out", id(stgb[sl]))] = w.tok(f"{ssem}{sl}")

    run_preps([("g", io["wg_in"], io["s_wg"], HT, D, 1, P["gtile"], I32, 0),
               ("u", io["wu_in"], io["s_wg"], HT, D, 1, P["gtile"], I32, D)])
    prep_done = [w.tok(s) for s in ("syp0", "syp1", "sps0", "sps1")]
    run_preps([("d", io["wd_in"], io["s_wd"], DC, HID, HID // D,
                P["dtile"], I32, 0)], ssem="spd")
    war[("dtile", 0)] = w.tok("spd0")
    war[("dtile", 1)] = w.tok("spd1")

    pstok = {}

    def ln(src_of, xn_to, ps4, qbw=QB, src_tok=None):
        ps_sum, ps_ssq, ps_r, ps_mr = ps4
        w.await_tok("vector", src_tok)
        for c in range(DC):
            sl = c % 2
            w.await_toks("vector", war.get(("xb", sl), []))
            w.dve("tensor_copy", P["xb"][sl][:, 0:qbw], src_of(c))
            cast_t = w.tok("dve")
            w.wait("scalar", "dve", cast_t[1])
            w.await_tok("scalar", war.get(("xsq", sl)))
            w.act("square", P["xsq"][sl][:, 0:qbw], P["xb"][sl][:, 0:qbw])
            sq_t = w.tok("act")
            w.wait("tensor", "dve", cast_t[1])
            if c == 0:
                w.await_tok("tensor", pstok.get(id(ps_sum)))
                w.await_tok("tensor", pstok.get(id(ps_ssq)))
            w.mm(ps_sum[0:1, 0:qbw], P["onesc"][:, 0:1], P["xb"][sl][:, 0:qbw],
                 start=(c == 0), stop=(c == DC - 1), inc="pe")
            war[("xb", sl)] = [w.tok("pe"), sq_t]
            w.wait("tensor", "act", sq_t[1])
            w.mm(ps_ssq[0:1, 0:qbw], P["onesc"][:, 0:1], P["xsq"][sl][:, 0:qbw],
                 start=(c == 0), stop=(c == DC - 1), inc="pe")
            war[("xsq", sl)] = w.tok("pe")
        mm_t = w.tok("pe")
        w.wait("vector", "pe", mm_t[1])
        w.dve("tensor_scalar_mul", P["m_r"][0:1, 0:qbw], ps_sum[0:1, 0:qbw], 1.0 / D, dr=True)
        w.dve("tensor_mul", P["mr_r"][0:1, 0:qbw], P["m_r"][0:1, 0:qbw],
              P["m_r"][0:1, 0:qbw], dr=True)
        w.dve("scalar_tensor_tensor", P["t_r"][0:1, 0:qbw], ps_ssq[0:1, 0:qbw],
              1.0 / D, P["mr_r"][0:1, 0:qbw], op0=ALU.mult, op1=ALU.subtract, dr=True)
        pstok[id(ps_sum)] = pstok[id(ps_ssq)] = w.tok("dve")
        w.dve("tensor_scalar_add", P["tmp"][0][0:1, 0:qbw], P["t_r"][0:1, 0:qbw],
              C.EPS, dr=True)
        var_t = w.tok("dve")
        w.wait("scalar", "dve", var_t[1])
        w.act("activation", P["t_r"][0:1, 0:qbw], P["tmp"][0][0:1, 0:qbw], AF.Sqrt,
              bias=0.0, scale=1.0)
        sd_t = w.tok("act")
        w.wait("vector", "act", sd_t[1])
        w.dve("reciprocal", P["t_r"][0:1, 0:qbw], P["t_r"][0:1, 0:qbw], dr=True)
        w.dve("tensor_mul", P["mr_r"][0:1, 0:qbw], P["t_r"][0:1, 0:qbw],
              P["t_r"][0:1, 0:qbw], dr=True)
        w.dve("tensor_mul", P["mr_r"][0:1, 0:qbw], P["mr_r"][0:1, 0:qbw],
              P["tmp"][0][0:1, 0:qbw], dr=True)
        w.dve("tensor_scalar", P["mr_r"][0:1, 0:qbw], P["mr_r"][0:1, 0:qbw],
              -0.5, 1.5, op0=ALU.mult, op1=ALU.add, dr=True)
        w.dve("tensor_mul", P["t_r"][0:1, 0:qbw], P["t_r"][0:1, 0:qbw],
              P["mr_r"][0:1, 0:qbw], dr=True)
        w.dve("tensor_mul", P["mr_r"][0:1, 0:qbw], P["m_r"][0:1, 0:qbw],
              P["t_r"][0:1, 0:qbw], dr=True)
        row_t = w.tok("dve")
        w.wait("tensor", "dve", row_t[1])
        w.await_tok("tensor", pstok.get(id(ps_r)))
        w.await_tok("tensor", pstok.get(id(ps_mr)))
        w.mm(ps_r[:, 0:qbw], P["onesf"][0:1, :], P["t_r"][0:1, 0:qbw], inc="pe")
        w.mm(ps_mr[:, 0:qbw], P["onesf"][0:1, :], P["mr_r"][0:1, 0:qbw], inc="pe")
        bc_t = w.tok("pe")
        w.wait("vector", "pe", bc_t[1])
        for c in range(DC):
            sl = c % 2
            w.dve("tensor_mul", P["tmp"][sl][:, 0:qbw], src_of(c), ps_r[:, 0:qbw],
                  dr=True)
            w.dve("tensor_sub", xn_to(c), P["tmp"][sl][:, 0:qbw], ps_mr[:, 0:qbw])
        pstok[id(ps_r)] = pstok[id(ps_mr)] = w.tok("dve")
        return w.tok("dve")

    def ffn(xn_of, res_of, dst_of, ps4, hT, xn_tok, dst_cb=None,
            res_tok=None):
        ps_g, ps_u, ps_d0, ps_d1 = ps4
        s = C.S8
        last_h_t = None
        w.wait("tensor", xn_tok[0], xn_tok[1])
        w.await_tok("vector", res_tok)
        for ht in range(HT):
            sl = ht % 2
            w.await_tok("sync", war.get(("gtile", sl)))
            w.await_tok("sync", war.get(("utile", sl)))
            w.dma("sync", P["gtile"][sl][:], io["s_wg"][ht], sem=f"syg{sl}")
            w.wait("tensor", f"syg{sl}")
            w.await_tok("tensor", pstok.get(id(ps_g)))
            for c in range(DC):
                w.mm(ps_g[:, :], P["gtile"][sl][:, c * 128:(c + 1) * 128],
                     xn_of(c), start=(c == 0), stop=(c == DC - 1),
                     inc="pe" if c == DC - 1 else None)
            gmm_t = w.tok("pe")
            war[("gtile", sl)] = gmm_t
            w.await_tok("tensor", pstok.get(id(ps_u)))
            for c in range(DC):
                w.mm(ps_u[:, :], P["gtile"][sl][:, D + c * 128:D + (c + 1) * 128],
                     xn_of(c), start=(c == 0), stop=(c == DC - 1),
                     inc="pe" if c == DC - 1 else None)
            umm_t = w.tok("pe")
            war[("utile", sl)] = umm_t
            w.wait("scalar", "pe", gmm_t[1])
            w.await_tok("scalar", war.get(("sg", sl)))
            w.act("activation", P["sg"][sl][:], ps_g[:, :], AF.Silu, scale=s)
            sil_t = w.tok("act")
            pstok[id(ps_g)] = sil_t
            w.wait("vector", "pe", umm_t[1])
            w.wait("vector", "act", sil_t[1])
            w.dve("scalar_tensor_tensor", hT[:, ht * QB:(ht + 1) * QB], ps_u[:, :],
                  s, P["sg"][sl][:], op0=ALU.mult, op1=ALU.mult)
            pstok[id(ps_u)] = war[("sg", sl)] = w.tok("dve")
            last_h_t = w.tok("dve")
        for ot in range(DC):
            ps_d = ps_d0 if ot % 2 == 0 else ps_d1
            sl = ot % 2
            w.await_tok("sync", war.get(("dtile", sl)))
            w.dma("sync", P["dtile"][sl][:], io["s_wd"][ot], sem=f"syd{sl}")
            w.wait("tensor", f"syd{sl}")
            w.await_tok("tensor", pstok.get(id(ps_d)))
            w.wait("tensor", "dve", last_h_t[1])
            for h in range(HT):
                w.mm(ps_d[:, :], P["dtile"][sl][:, h * 128:(h + 1) * 128],
                     hT[:, h * QB:(h + 1) * QB], start=(h == 0),
                     stop=(h == HT - 1), inc="pe" if h == HT - 1 else None)
            dmm_t = w.tok("pe")
            war[("dtile", sl)] = dmm_t
            w.wait("vector", "pe", dmm_t[1])
            w.dve("scalar_tensor_tensor", dst_of(ot), ps_d[:, :], 0.5 * s,
                  res_of(ot), op0=ALU.mult, op1=ALU.add)
            pstok[id(ps_d)] = w.tok("dve")
            if dst_cb is not None:
                dst_cb(ot, w.tok("dve"))

    psA = A["ps"]
    w.wait("vector", "syc")
    w.wait("vector", "syx")
    w.await_toks("sync", prep_done)
    for qb in range(NQB):
        q0 = qb * QB
        ln_t = ln(lambda c: xch(c, q0, QB), lambda c: xnch(c, q0),
                  (psA[3], psA[4], psA[5], psA[6]))
        ffn(lambda c: xnch(c, q0), lambda c: xch(c, q0, QB),
            lambda c: xch(c, q0, QB), (psA[0], psA[1], psA[2], psA[3]),
            A["hT_A"], ln_t)
    x1_tok = w.tok("dve")

    run_preps([(p, io[f"wT_{p}"], io[f"s_w{p}"], DC, D, 1, P["wtile"], F32, 0)
               for p in "qkvo"])
    for sl in range(2):
        war[("wtile", sl)] = war[("prep_out", id(P["wtile"][sl]))]
    w.wait("sync", "sps0")
    w.wait("sync", "sps1")

    w.wait("scalar", "dve", x1_tok[1])
    for c in range(DC):
        w.dma("scalar", io["x1sp"][c * 128:(c + 1) * 128, :], xch(c), sem="scs")
    spill_tok = w.tok("scs")

    if KDBG == "x1":
        w.wait("scalar", "dve", x1_tok[1])
        for c in range(DC):
            w.dma("scalar", io["outT"][c * 128:(c + 1) * 128, :], xch(c),
                  sem="sco")
        w.wait("scalar", "sco")
        alloc_B()
        return

    for qb in range(NQB):
        q0 = qb * QB
        xn2_tok = ln(lambda c, q0=q0: xch(c, q0, QB),
                     lambda c, q0=q0: xnch(c, q0),
                     (psA[3], psA[4], psA[5], psA[6]), src_tok=x1_tok)
    w.wait("tensor", "dve", xn2_tok[1])

    def proj(p, qb, q0, dest_cb):
        usl = qb % 2
        deferred = [None]
        w.await_tok("tensor", pstok.get(id(psA[2])))
        for c in range(DC):
            w.mm(psA[2][0:R, :], P[f"aTs_{p}"][:, c * R:(c + 1) * R],
                 xnch(c, q0), start=(c == 0), stop=(c == DC - 1),
                 inc="pe" if c == DC - 1 else None)
        umm_t = w.tok("pe")
        w.wait("vector", "pe", umm_t[1])
        w.await_tok("vector", war.get(("u_s", usl)))
        w.dve("tensor_scalar_mul", P["u_s"][usl][:, :], psA[2][0:R, :], C.LORA)
        pstok[id(psA[2])] = w.tok("dve")
        u_t = w.tok("dve")
        for ot in range(DC):
            wsl = ot % 4
            q = "sync"
            w.await_tok(q, war.get(("wtile", wsl)))
            w.dma(q, P["wtile"][wsl][:], io[f"s_w{p}"][ot],
                  sem=f"syw{wsl}")
            ps = psA[ot % 2]
            w.wait("tensor", f"syw{wsl}")
            w.await_tok("tensor", pstok.get(id(ps)))
            for c in range(DC):
                w.mm(ps[:, :], P["wtile"][wsl][:, c * 128:(c + 1) * 128],
                     xnch(c, q0), start=(c == 0), stop=False)
            w.wait("tensor", "dve", u_t[1])
            w.mm(ps[:, :], A[f"bTs_{p}"][:, ot * 128:(ot + 1) * 128],
                 P["u_s"][usl][0:R, :], start=False, stop=True, inc="pe")
            mm_t = w.tok("pe")
            war[("wtile", wsl)] = mm_t
            war[("u_s", usl)] = mm_t
            nxt = dest_cb(ot, ps, mm_t)
            if deferred[0] is not None:
                deferred[0]()
            deferred[0] = nxt
        if deferred[0] is not None:
            deferred[0]()
            deferred[0] = None

    kv_cc = {}
    for qb in range(NQB):
        q0 = qb * QB

        def k_dest(ot, ps, mm_t, qb=qb, q0=q0):
            sl = ot % 2
            w.wait("vector", "pe", mm_t[1])
            w.await_toks("vector", war.get(("kst", sl), []))
            w.dve("tensor_scalar", A["kst"][sl][:], ps[:, :],
                  P["bias_k"][:, ot:ot + 1], None, op0=ALU.add)
            pstok[id(ps)] = ke_t = w.tok("dve")
            w.wait("scalar", "dve", ke_t[1])
            w.await_tok("scalar", war.get(("ksq", sl)))
            w.act("square", A["ksq"][sl][:], A["kst"][sl][:])
            ksq_t = w.tok("act")
            w.wait("gpsimd", "dve", ke_t[1])
            pc0 = q0 // PIECE
            nsp = QB // PIECE
            w.dma("gpsimd",
                  io["kp_own"][pc0:pc0 + nsp, ot * 128:(ot + 1) * 128, :]
                  .rearrange("s p t -> p s t"),
                  A["kst"][sl][:, :].rearrange("p (s t) -> p s t", t=PIECE),
                  sem=f"sysk{sl}")
            war[("kst", sl)] = [("act", ksq_t[1]), w.tok(f"sysk{sl}")]

            def knmm(ot=ot, sl=sl, ksq_t=ksq_t):
                w.wait("tensor", "act", ksq_t[1])
                if ot == 0:
                    w.await_tok("tensor", pstok.get(id(psA[3])))
                w.mm(psA[3][0:H, :], P["Eind"][:, ot * H:(ot + 1) * H],
                     A["ksq"][sl][:], start=(ot == 0), stop=(ot == DC - 1),
                     inc="pe")
                war[("ksq", sl)] = w.tok("pe")
            return knmm

        proj("k", qb, q0, k_dest)
        knm_t = w.tok("pe")
        w.wait("vector", "pe", knm_t[1])
        w.dve("reduce_max", P["kn_sc"][:, qb:qb + 1], psA[3][0:H, :], axis=AX.X)
        pstok[id(psA[3])] = w.tok("dve")

        def v_dest(ot, ps, mm_t, qb=qb, q0=q0):
            w.wait("vector", "pe", mm_t[1])
            w.await_tok("vector", war.get("VT"))
            w.dve("tensor_scalar", A["VT_qb"][:, ot * QB:(ot + 1) * QB], ps[:, :],
                  P["bias_v"][:, ot:ot + 1], None, op0=ALU.add)
            pstok[id(ps)] = w.tok("dve")

        proj("v", qb, q0, v_dest)
        vev_t = w.tok("dve")
        for tb in range(QB // 128):
            vsl = tb % 2
            w.wait("tensor", "dve", vev_t[1])
            w.await_tok("tensor", war.get(("vrow", vsl)))
            w.await_tok("tensor", pstok.get(id(A["psV"])))
            for c in range(DC):
                w.transp(A["psV"][:, c * 128:(c + 1) * 128],
                         A["VT_qb"][:, c * QB + tb * 128: c * QB + tb * 128 + 128],
                         P["identS"][:], inc="pe" if c == DC - 1 else None)
            tr_t = w.tok("pe")
            war["VT"] = tr_t
            w.wait("vector", "pe", tr_t[1])
            vr = A["vrow"][vsl][:, :].rearrange("p (h v) -> p h v", v=VW)
            w.await_tok("vector", p0_dve)
            w.dve("tensor_copy", vr[:, :, 0:DH],
                  A["psV"][:, 0:D].rearrange("p (h d) -> p h d", d=DH))
            pstok[id(A["psV"])] = vcp_t = w.tok("dve")
            w.wait("gpsimd", "dve", vcp_t[1])
            pc = (q0 + tb * 128) // PIECE
            pr = ((q0 + tb * 128) % PIECE) // 128
            w.dma("gpsimd", io["vp_own"][pc][pr * 128:(pr + 1) * 128, :],
                  A["vrow"][vsl][:], sem=f"sysv{vsl}")
            war[("vrow", vsl)] = w.tok(f"sysv{vsl}")

        for s in ("sysk0", "sysk1", "sysv0", "sysv1"):
            w.wait("gpsimd", s)
        for sp in range(QB // PIECE):
            pc = (q0 + sp * PIECE) // PIECE
            w.cc(io["kp_own"][pc].opt(), io["kp_g"][pc].opt(), io["groups"])
            w.cc(io["vp_own"][pc].opt(), io["vp_g"][pc].opt(), io["groups"])
            kv_cc[pc] = w.tok("cc")

        def q_dest(ot, ps, mm_t, qb=qb, q0=q0):
            sl = ot % 2
            w.wait("vector", "pe", mm_t[1])
            w.await_toks("vector", war.get(("kst", sl), []))
            w.dve("tensor_scalar", A["kst"][sl][:], ps[:, :],
                  P["bias_q"][:, ot:ot + 1], None, op0=ALU.add)
            pstok[id(ps)] = qe_t = w.tok("dve")
            w.wait("scalar", "dve", qe_t[1])
            w.await_tok("scalar", war.get(("ksq", sl)))
            w.act("square", A["ksq"][sl][:], A["kst"][sl][:])
            qsq_t = w.tok("act")
            w.wait("gpsimd", "dve", qe_t[1])
            w.dma("gpsimd", io["s_qT"][ot * 128:(ot + 1) * 128, q0:q0 + QB],
                  A["kst"][sl][:], sem=f"sysk{sl}")
            war[("kst", sl)] = [("act", qsq_t[1]), w.tok(f"sysk{sl}")]

            def qnmm(ot=ot, sl=sl, qsq_t=qsq_t):
                w.wait("tensor", "act", qsq_t[1])
                if ot == 0:
                    w.await_tok("tensor", pstok.get(id(psA[3])))
                w.mm(psA[3][0:H, :], P["Eind"][:, ot * H:(ot + 1) * H],
                     A["ksq"][sl][:], start=(ot == 0), stop=(ot == DC - 1),
                     inc="pe")
                war[("ksq", sl)] = w.tok("pe")
            return qnmm

        proj("q", qb, q0, q_dest)
        qnm_t = w.tok("pe")
        w.wait("vector", "pe", qnm_t[1])
        w.dve("reduce_max", P["qn_sc"][:, qb:qb + 1], psA[3][0:H, :], axis=AX.X)
        pstok[id(psA[3])] = w.tok("dve")

    w.wait("vector", "dve")
    w.dve("reduce_max", P["kn2"][:, 0:1], P["kn_sc"][:, :], axis=AX.X)
    w.dve("reduce_max", P["qn2"][:, 0:1], P["qn_sc"][:, :], axis=AX.X)
    n_t = w.tok("dve")
    w.wait("gpsimd", "dve", n_t[1])
    w.dma("gpsimd", io["nrm_own"].rearrange("a h -> h a"), P["kn2"][:, 0:1])
    w.dma("gpsimd", io["qn_b"].rearrange("a h -> h a"), P["qn2"][:, 0:1])
    w.wait("gpsimd", "gp")
    w.cc(io["nrm_own"].opt(), io["nrm_g"].opt(), io["groups"])
    nrm_cc = w.tok("cc")
    w.wait("gpsimd", "cc", nrm_cc[1])
    w.dma("gpsimd", P["grow"][0:1, :], io["nrm_g"].rearrange("r h -> (r h)"))
    w.dma("gpsimd", P["qrow"][0:1, :], io["qn_b"])
    row_ld = w.tok("gp")
    w.wait("vector", "gp", row_ld[1])
    w.dve("tensor_max", P["krow"][0:1, :], P["grow"][0:1, 0:H],
          P["grow"][0:1, H:2 * H], dr=True)
    for r in range(2, C.GROUP):
        w.dve("tensor_max", P["krow"][0:1, :], P["krow"][0:1, :],
              P["grow"][0:1, r * H:(r + 1) * H], dr=True)
    w.dve("tensor_mul", P["prow"][0:1, :], P["qrow"][0:1, :], P["krow"][0:1, :], dr=True)
    p_t = w.tok("dve")
    w.wait("scalar", "dve", p_t[1])
    w.act("activation", P["nrow"][0:1, :], P["prow"][0:1, :], AF.Sqrt,
          bias=0.0, scale=1.02 / 64.0)
    sq_t = w.tok("act")
    w.wait("vector", "act", sq_t[1])
    w.dve("tensor_scalar_mul", P["nrow_b"][0:1, :], P["nrow"][0:1, :], -1.0, dr=True)
    nb_t = w.tok("dve")
    w.wait("tensor", "dve", nb_t[1])
    w.await_tok("tensor", pstok.get(id(psA[3])))
    w.mm(psA[3][:, 0:H], P["ones1"][0:1, :], P["nrow_b"][0:1, :], inc="pe")
    nm_t = w.tok("pe")
    w.wait("vector", "pe", nm_t[1])
    w.dve("tensor_copy", P["negm"][:], psA[3][:, 0:H])
    negm_tok = w.tok("dve")

    for sem in list(w.c):
        if sem not in ("cc",):
            for e in ("sync", "scalar", "vector", "gpsimd", "tensor"):
                w.wait(e, sem)
    alloc_B()
    kbuf, vbuf, oT = B["kbuf"], B["vbuf"], B["oT"]
    hT_B, xn_q, x1_q, qt_q = B["hT_B"], B["xn_q"], B["x1_q"], B["qt_q"]
    aoT = hT_B
    psB = B["ps"]

    kv_war = [None, None]
    piece_ld = {}

    def load_piece(qb_, pc):
        slot = pc % 2
        w.await_tok("sync", kv_war[slot])
        w.wait("sync", "cc")
        for hp in range(HP):
            w.dma("sync",
                  kbuf[slot][:, hp * KW:(hp + 1) * KW]
                  .rearrange("p (r t) -> p r t", t=PIECE),
                  io["kp_g"][pc].rearrange("(r hh p) t -> p hh r t",
                                           p=128, hh=HP)[:, hp, :, :],
                  sem=f"syk{slot}")
        hk = KCP // 2
        w.await_tok("gpsimd", kv_war[slot])
        w.wait("gpsimd", "cc")
        for half in range(2):
            w.dma("gpsimd",
                  vbuf[slot][:, half * hk * H * VW:(half + 1) * hk * H * VW]
                  .rearrange("p (tc f) -> p tc f", f=H * VW),
                  io["vp_g"][pc].rearrange("(tc p) f -> p tc f", p=128)
                  [:, half * hk:(half + 1) * hk, :], sem=f"syvp{slot}")
        piece_ld[(qb_, pc)] = [w.tok(f"syk{slot}"), w.tok(f"syvp{slot}")]

    for qb in range(NQB):
        q0 = qb * QB

        w.wait("sync", "scs", spill_tok[1])
        w.await_tok("sync", war.get("x1q"))
        for c in range(DC):
            w.dma("sync", x1_q[:, c * QB:(c + 1) * QB],
                  io["x1sp"][c * 128:(c + 1) * 128, q0:q0 + QB], sem="sy1")
        x1q_tok = w.tok("sy1")
        w.await_tok("sync", war.get("qt_q"))
        for c in range(DC):
            w.dma("sync", qt_q[:, c * QB:(c + 1) * QB],
                  io["s_qT"][c * 128:(c + 1) * 128, q0:q0 + QB], sem="syq")
        qt_tok = w.tok("syq")

        if qb == 0:
            load_piece(0, 0)
            if NP > 1:
                load_piece(0, 1)

        units = [(pc, hp, kc) for pc in range(NP) for hp in range(HP)
                 for kc in range(KCP)]
        S_free = [None, None]
        exp_tok = {}
        av_tok = {}
        oT_free = {}

        def emit_av(u):
            pc_, hp_, kc_ = units[u]
            slot_ = pc_ % 2
            sp_ = u % 2
            w.wait("tensor", "act", exp_tok[u][1])
            if kc_ == 0:
                w.await_tok("tensor", oT_free.get(hp_ % 2))
                w.await_tok("tensor", pstok.get(id(psB[4 + 2 * (hp_ % 2)])))
                w.await_tok("tensor", pstok.get(id(psB[5 + 2 * (hp_ % 2)])))
            vb0 = kc_ * H * VW
            w.mm(psB[4 + 2 * (hp_ % 2)][0:65, :],
                 vbuf[slot_][:, vb0 + 2 * hp_ * VW: vb0 + 2 * hp_ * VW + VW],
                 B["Pt"][2 * sp_][:], start=(kc_ == 0), stop=(kc_ == KCP - 1))
            w.mm(psB[5 + 2 * (hp_ % 2)][0:65, :],
                 vbuf[slot_][:, vb0 + (2 * hp_ + 1) * VW: vb0 + (2 * hp_ + 1) * VW + VW],
                 B["Pt"][2 * sp_ + 1][:], start=(kc_ == 0),
                 stop=(kc_ == KCP - 1), inc="pe")
            av_tok[u] = w.tok("pe")
            if kc_ == KCP - 1:
                w.wait("vector", "pe", av_tok[u][1])
                for side in range(2):
                    head = 2 * hp_ + side
                    ps_o = psB[4 + 2 * (hp_ % 2) + side]
                    w.await_tok("vector", war.get(("oT", head)))
                    if pc_ == 0:
                        w.dve("tensor_copy", oT[head][:, :], ps_o[0:65, :])
                    else:
                        w.dve("tensor_add", oT[head][:, :], oT[head][:, :],
                              ps_o[0:65, :])
                    pstok[id(ps_o)] = war[("oT", head)] = w.tok("dve")
                oT_free[hp_ % 2] = w.tok("dve")
                if hp_ == HP - 1:
                    kv_war[slot_] = w.tok("pe")
                    if pc_ + 2 < NP:
                        load_piece(qb, pc_ + 2)
                    elif qb + 1 < NQB:
                        load_piece(qb + 1, pc_ + 2 - NP)

        for ui, (pc, hp, kc) in enumerate(units):
            slot = pc % 2
            sp = ui % 2
            if kc == 0 and hp == 0:
                w.await_toks("tensor", piece_ld[(qb, pc)])
                w.await_tok("tensor", qt_tok)
            w.await_tok("tensor", S_free[sp])
            w.await_tok("tensor", pstok.get(id(psB[2 * sp])))
            w.await_tok("tensor", pstok.get(id(psB[2 * sp + 1])))
            kb0 = hp * KW + kc * 128
            w.mm(psB[2 * sp][:, :], kbuf[slot][0:64, kb0:kb0 + 128],
                 qt_q[0:64, hp * QB:(hp + 1) * QB], tp=(0, 0))
            w.mm(psB[2 * sp + 1][:, :], kbuf[slot][64:128, kb0:kb0 + 128],
                 qt_q[64:128, hp * QB:(hp + 1) * QB], tp=(64, 0), inc="pe")
            qk_t = w.tok("pe")
            w.wait("scalar", "pe", qk_t[1])
            if ui == 0 and qb == 0:
                w.wait("scalar", "dve", negm_tok[1])
            w.await_tok("scalar", av_tok.get(ui - 2))
            w.act("activation", B["Pt"][2 * sp][:], psB[2 * sp][:, :], AF.Exp,
                  bias=P["negm"][:, 2 * hp:2 * hp + 1], scale=C.ISCALE)
            w.act("activation", B["Pt"][2 * sp + 1][:], psB[2 * sp + 1][:, :],
                  AF.Exp, bias=P["negm"][:, 2 * hp + 1:2 * hp + 2],
                  scale=C.ISCALE)
            S_free[sp] = exp_tok[ui] = w.tok("act")
            pstok[id(psB[2 * sp])] = pstok[id(psB[2 * sp + 1])] = exp_tok[ui]
            if ui >= 1:
                emit_av(ui - 1)
        emit_av(len(units) - 1)

        att_t = w.tok("dve")

        bc_war = [None, None]
        w.wait("vector", "dve", att_t[1])
        for head in range(H):
            hp, side = head // 2, head % 2
            w.dve("reciprocal", B["recf"], oT[head][64:65, :], dr=True)
            w.dve("tensor_copy", B["recl"], B["recf"])
            rc_t = w.tok("dve")
            ps_bc = psB[head % 2]
            w.wait("tensor", "dve", rc_t[1])
            w.await_tok("tensor", bc_war[head % 2])
            w.await_tok("tensor", pstok.get(id(ps_bc)))
            w.mm(ps_bc[0:64, :], P["ones1"][0:1, 0:64], B["recl"],
                 inc="pe")
            bm_t = w.tok("pe")
            w.wait("vector", "pe", bm_t[1])
            w.dve("tensor_mul",
                  aoT[side * 64:(side + 1) * 64, hp * QB:(hp + 1) * QB],
                  oT[head][0:64, :], ps_bc[0:64, :])
            bc_war[head % 2] = pstok[id(ps_bc)] = w.tok("dve")
        aoT_t = w.tok("dve")

        w.await_tok("tensor", pstok.get(id(psB[2])))
        w.wait("tensor", "dve", aoT_t[1])
        for c in range(DC):
            w.mm(psB[2][0:R, :], P["aTs_o"][:, c * R:(c + 1) * R],
                 aoT[:, c * QB:(c + 1) * QB], start=(c == 0), stop=(c == DC - 1),
                 inc="pe" if c == DC - 1 else None)
        uo_t = w.tok("pe")
        w.wait("vector", "pe", uo_t[1])
        w.await_tok("vector", war.get(("u_s", 0)))
        w.dve("tensor_scalar_mul", P["u_s"][0][:, :], psB[2][0:R, :], C.LORA)
        pstok[id(psB[2])] = uo_e = w.tok("dve")
        for ot in range(DC):
            wsl = ot % 2
            q = "sync"
            w.await_tok(q, war.get(("wtile", wsl)))
            w.dma(q, P["wtile"][wsl][:], io["s_wo"][ot], sem=f"syw{wsl}")
            ps = psB[ot % 2]
            w.wait("tensor", f"syw{wsl}")
            w.await_tok("tensor", pstok.get(id(ps)))
            for c in range(DC):
                w.mm(ps[:, :], P["wtile"][wsl][:, c * 128:(c + 1) * 128],
                     aoT[:, c * QB:(c + 1) * QB], start=(c == 0), stop=False)
            w.wait("tensor", "dve", uo_e[1])
            w.mm(ps[:, :], P["bTs_o"][:, ot * 128:(ot + 1) * 128],
                 P["u_s"][0][0:R, :], start=False, stop=True, inc="pe")
            om_t = w.tok("pe")
            war[("wtile", wsl)] = om_t
            war[("u_s", 0)] = om_t
            w.wait("vector", "pe", om_t[1])
            w.await_tok("vector", x1q_tok)
            w.dve("scalar_tensor_tensor", x1_q[:, ot * QB:(ot + 1) * QB],
                  ps[:, :], P["bias_o"][:, ot:ot + 1],
                  x1_q[:, ot * QB:(ot + 1) * QB], op0=ALU.add, op1=ALU.add)
            pstok[id(ps)] = pstok["x2t"] = w.tok("dve")
        war["qt_q"] = w.tok("pe")

        if KDBG == "x2":
            x2_t = w.tok("dve")
            for ot in range(DC):
                w.wait("scalar", "dve", x2_t[1])
                w.dma("scalar", io["outT"][ot * 128:(ot + 1) * 128, q0:q0 + QB],
                      x1_q[:, ot * QB:(ot + 1) * QB], sem="sco")
            war["x1q"] = w.tok("sco")
            continue

        ln3_t = ln(lambda c: x1_q[:, c * QB:(c + 1) * QB],
                   lambda c: xn_q[:, c * QB:(c + 1) * QB],
                   (psB[3], psB[4], psB[5], psB[6]),
                   src_tok=pstok.get("x2t"))

        st_toks = {}
        ffn(lambda c: xn_q[:, c * QB:(c + 1) * QB],
            lambda c: x1_q[:, c * QB:(c + 1) * QB],
            lambda c: x1_q[:, c * QB:(c + 1) * QB],
            (psB[0], psB[1], psB[2], psB[3]), hT_B, ln3_t,
            dst_cb=lambda ot, tok: st_toks.__setitem__(ot, tok),
            res_tok=pstok.get("x2t"))

        for ot in range(DC):
            w.wait("scalar", "dve", st_toks[ot][1])
            w.dma("scalar", io["outT"][ot * 128:(ot + 1) * 128, q0:q0 + QB],
                  x1_q[:, ot * QB:(ot + 1) * QB], sem="sco")
        war["x1q"] = w.tok("sco")

    w.wait("scalar", "sco")
    w.wait("gpsimd", "cc")


def _prep_inputs(C, inputs, prefix="L0_"):
    f = np.float32
    D, TOK, H, R = C.D, C.TOK, C.HEADS, C.RANK
    DC, HT, HID = C.DC, C.HT, C.HID

    x = np.asarray(inputs[prefix + "attn_x"], dtype=f)

    def tile_w(wmat):
        wt = np.asarray(wmat, dtype=f).T
        t = wt.reshape(DC, 128, DC, 128).transpose(2, 1, 0, 3).reshape(DC, 128, D)
        return np.ascontiguousarray(t)

    def tile_in(wmat, nt, ncols):
        wt = np.asarray(wmat).T
        ic = wt.shape[0] // 128
        t = wt.reshape(ic, 128, nt, 128).transpose(2, 1, 0, 3).reshape(nt, 128, ncols)
        return np.ascontiguousarray(t.astype(np.int32))

    common = dict(ident=np.eye(128, dtype=ml_dtypes.bfloat16))
    for p, nm in zip("qkvo", ["q", "k", "v", "o"]):
        common[f"w{p}T"] = tile_w(inputs[f"{prefix}attn_W{nm}"])
        common[f"b{p}"] = np.ascontiguousarray(
            np.asarray(inputs[f"{prefix}attn_b{nm}"], dtype=f).reshape(D, 1))
        common[f"a{p}T"] = np.ascontiguousarray(
            np.asarray(inputs[f"{prefix}attn_A{nm}"], dtype=f).T)
        common[f"b{p}T"] = np.ascontiguousarray(
            np.asarray(inputs[f"{prefix}attn_B{nm}"], dtype=f).T)
    common["wgT_t"] = tile_in(inputs[prefix + "ffn_w_gate"], HT, D)
    common["wuT_t"] = tile_in(inputs[prefix + "ffn_w_up"], HT, D)
    common["wdT_t"] = tile_in(inputs[prefix + "ffn_w_down"], DC, HID)

    B = x.shape[0]
    cores_per_b = C.N_CORES // B
    in_maps = []
    for core in range(C.N_CORES):
        b = core // cores_per_b
        t0 = (core % cores_per_b) * TOK
        m = dict(common)
        m["xT"] = np.ascontiguousarray(x[b, t0:t0 + TOK, :].T)
        in_maps.append(m)
    return in_maps, x.shape


def _gather_out(C, results, shape):
    B, S, D = shape
    cores_per_b = C.N_CORES // B
    out = np.empty(shape, dtype=np.float32)
    for core in range(C.N_CORES):
        b = core // cores_per_b
        t0 = (core % cores_per_b) * C.TOK
        out[b, t0:t0 + C.TOK, :] = results[core]["outT"].T
    return out


def kernel(**inputs):
    C = Cfg()
    in_maps, shape = _prep_inputs(C, inputs)
    nc = build(C)
    res = run_bass_kernel_spmd(nc, in_maps, core_ids=list(range(C.N_CORES)))
    return _gather_out(C, res.results, shape)


# version 27
# speedup vs baseline: 1.0060x; 1.0032x over previous
import os as _os
import numpy as np
import ml_dtypes
from contextlib import ExitStack

KDBG = _os.environ.get("KDBG", "")

import concourse.bass as bass
import concourse.mybir as mybir
from concourse.bass_utils import run_bass_kernel_spmd

F32 = mybir.dt.float32
BF16 = mybir.dt.bfloat16
I32 = mybir.dt.int32
AF = mybir.ActivationFunctionType
ALU = mybir.AluOpType
AX = mybir.AxisListType


class Cfg:
    def __init__(self, D=1024, TOK=2048, HEADS=16, HID=3072, QB=512, PIECE=256,
                 RANK=8, GROUP=4, N_CORES=8):
        self.D, self.TOK, self.HEADS, self.HID = D, TOK, HEADS, HID
        self.QB, self.PIECE, self.RANK, self.GROUP = QB, PIECE, RANK, GROUP
        self.N_CORES = N_CORES
        self.DH = D // HEADS
        assert self.DH == 64
        self.DC = D // 128
        self.HP = HEADS // 2
        assert self.DC == self.HP
        self.NQB = TOK // QB
        self.HT = HID // 128
        assert self.HT % 2 == 0 and (HID // 2) % 128 == 0
        self.NP = TOK // PIECE
        self.GT = GROUP * PIECE
        self.KCP = self.GT // 128
        self.LORA = 2.0
        self.S8 = 1.0 / 127.0
        self.EPS = 1e-5
        self.ISCALE = 0.125


class W:

    def __init__(self, nc, eng, E, sems):
        self.nc, self.eng, self.E, self.sems = nc, eng, E, sems
        self.c = {k: 0 for k in sems}

    def wait(self, eng, sem, val=None):
        if val is None:
            val = self.c[sem]
        if self.eng == eng and val > 0:
            self.E.wait_ge(self.sems[sem], val)

    def tok(self, sem):
        return (sem, self.c[sem])

    def await_tok(self, eng, t):
        if t is not None:
            self.wait(eng, t[0], t[1])

    def await_toks(self, eng, ts):
        for t in ts:
            self.await_tok(eng, t)

    def dma(self, q, out, in_, sem=None, **kw):
        if sem is None:
            sem = {"sync": "sy", "scalar": "sc", "gpsimd": "gp"}[q]
        if self.eng == q:
            self.E.dma_start(out=out, in_=in_, **kw).then_inc(self.sems[sem], 16)
        self.c[sem] += 16

    def mm(self, out, lhsT, rhs, start=True, stop=True, tp=None, inc=None):
        if self.eng == "tensor":
            ins = self.E.matmul(out, lhsT=lhsT, rhs=rhs, start=start, stop=stop,
                                tile_position=tp)
            if inc:
                ins.then_inc(self.sems[inc], 1)
        if inc:
            self.c[inc] += 1

    def transp(self, out, in_, ident, inc=None):
        if self.eng == "tensor":
            ins = self.E.transpose(out, in_, ident)
            if inc:
                ins.then_inc(self.sems[inc], 1)
        if inc:
            self.c[inc] += 1

    def _fin(self, eng, ins, inc, dr):
        if inc:
            if self.eng == eng:
                ins.then_inc(self.sems[inc], 1)
            self.c[inc] += 1
        if dr and self.eng == eng:
            self.E.drain()

    def act(self, fn, *a, inc="act", dr=False, **kw):
        ins = getattr(self.E, fn)(*a, **kw) if self.eng == "scalar" else None
        self._fin("scalar", ins, inc, dr)

    def dve(self, fn, *a, inc="dve", dr=False, **kw):
        ins = getattr(self.E, fn)(*a, **kw) if self.eng == "vector" else None
        self._fin("vector", ins, inc, dr)

    def cc(self, ins_ap, outs_ap, groups):
        if self.eng == "gpsimd":
            self.E.collective_compute(
                "AllGather", ALU.bypass, replica_groups=groups,
                ins=[ins_ap], outs=[outs_ap],
            ).then_inc(self.sems["cc"])
        self.c["cc"] += 1


def build(cfg: Cfg):
    nc = bass.Bass("TRN2", target_bir_lowering=False)
    C = cfg
    D, TOK, QB, HID = C.D, C.TOK, C.QB, C.HID
    DC, HT, HP, NP = C.DC, C.HT, C.HP, C.NP
    H, R = C.HEADS, C.RANK
    PIECE = C.PIECE
    VW = C.DH + 1
    groups = [list(range(g * C.GROUP, (g + 1) * C.GROUP))
              for g in range(C.N_CORES // C.GROUP)]

    def din(name, shape, dt):
        return nc.dram_tensor(name, shape, dt, kind="ExternalInput").ap()

    io = dict(
        xT=din("xT", [D, TOK], F32),
        ident_in=din("ident", [128, 128], BF16),
        wg_in=din("wgT_t", [HT, 128, D], I32),
        wu_in=din("wuT_t", [HT, 128, D], I32),
        wd_in=din("wdT_t", [DC, 128, HID], I32),
        outT=nc.dram_tensor("outT", [D, TOK], F32, kind="ExternalOutput").ap(),
        s_wg=nc.dram_tensor("s_wg", [HT, 128, 2 * D], BF16).ap(),
        s_wd=nc.dram_tensor("s_wd", [DC, 128, HID], BF16).ap(),
        s_wo=nc.dram_tensor("s_wo", [DC, 128, D], BF16).ap(),
        s_wq=nc.dram_tensor("s_wq", [DC, 128, D], BF16).ap(),
        s_wk=nc.dram_tensor("s_wk", [DC, 128, D], BF16).ap(),
        s_wv=nc.dram_tensor("s_wv", [DC, 128, D], BF16).ap(),
        s_qT=nc.dram_tensor("s_qT", [D, TOK], BF16).ap(),
        x1sp=nc.dram_tensor("x1sp", [D, TOK], F32).ap(),
        kp_own=nc.dram_tensor("kp_own", [NP, HP * 128, PIECE], BF16).ap(),
        vp_own=nc.dram_tensor("vp_own", [NP, PIECE, H * VW], BF16).ap(),
        kp_g=nc.dram_tensor("kp_g", [NP, C.GROUP * HP * 128, PIECE], BF16).ap(),
        vp_g=nc.dram_tensor("vp_g", [NP, C.GROUP * PIECE, H * VW], BF16).ap(),
        nrm_own=nc.dram_tensor("nrm_own", [1, H], F32).ap(),
        nrm_g=nc.dram_tensor("nrm_g", [C.GROUP, H], F32).ap(),
        qn_b=nc.dram_tensor("qn_b", [1, H], F32).ap(),
        groups=groups,
    )
    for p in "qkvo":
        io[f"wT_{p}"] = din(f"w{p}T", [DC, 128, D], F32)
        io[f"bi_{p}"] = din(f"b{p}", [D, 1], F32)
        io[f"aT_{p}"] = din(f"a{p}T", [D, R], F32)
        io[f"bT_{p}"] = din(f"b{p}T", [R, D], F32)

    with ExitStack() as top:
        TE = top.enter_context
        P = {}

        def sb(name, shape, dt, n=None, into=P, E=TE):
            if n is None:
                into[name] = E(nc.sbuf_tensor(name, shape, dt))
            else:
                into[name] = [E(nc.sbuf_tensor(f"{name}{i}", shape, dt))
                              for i in range(n)]

        sb("identS", [128, 128], BF16)
        sb("ones1", [1, 128], BF16)
        sb("onesf", [1, 128], F32)
        sb("onesc", [128, 1], BF16)
        sb("Eind", [128, DC * H], BF16)
        sb("negm", [128, H], F32)
        for p in "qkvo":
            sb(f"bias_{p}", [128, DC], F32)
            sb(f"aTs_{p}", [128, DC * R], BF16)
        sb("bTs_o", [R, D], BF16)
        sb("m_r", [1, QB], F32)
        sb("t_r", [1, QB], F32)
        sb("mr_r", [1, QB], F32)
        sb("qrow", [1, H], F32)
        sb("grow", [1, C.GROUP * H], F32)
        sb("krow", [1, H], F32)
        sb("prow", [1, H], F32)
        sb("nrow", [1, H], F32)
        sb("nrow_b", [1, H], BF16)
        sb("qn2", [H, 1], F32)
        sb("kn2", [H, 1], F32)
        sb("kn_sc", [H, C.NQB], F32)
        sb("qn_sc", [H, C.NQB], F32)
        sb("u_s", [R, QB], BF16, n=2)
        sb("sg", [128, QB], BF16, n=2)
        sb("xb", [128, QB], BF16, n=2)
        sb("xsq", [128, QB], BF16, n=2)
        sb("tmp", [128, QB], F32, n=2)
        sb("wtile", [128, D], BF16, n=4)
        sb("gtile", [128, 2 * D], BF16, n=2)
        sb("dtile", [128, HID], BF16, n=2)
        sb("epsr", [1, 1], F32)

        sems = {}
        for s in ("syc", "syx", "syp0", "syp1", "sps0", "sps1", "spd0", "spd1",
                  "syg0", "syg1", "syu0", "syu1",
                  "syd0", "syd1", "syd2", "syd3", "syw0", "syw1", "syw2", "syw3",
                  "sy1", "syq", "syk0", "syk1", "syvp0", "syvp1",
                  "sysk0", "sysk1", "sysv0", "sysv1",
                  "scs", "sco", "gp", "cc", "pe", "act", "dve"):
            sems[s] = TE(nc.semaphore(s))

        sA = ExitStack()
        A = {}
        sb("x_res", [128, DC * TOK], F32, into=A, E=sA.enter_context)
        sb("xnb", [128, DC * TOK], BF16, into=A, E=sA.enter_context)
        sb("hT_A", [128, HT * QB], BF16, into=A, E=sA.enter_context)
        sb("VT_qb", [128, DC * QB], BF16, into=A, E=sA.enter_context)
        sb("kst", [128, QB], BF16, n=2, into=A, E=sA.enter_context)
        sb("ksq", [128, QB], BF16, n=2, into=A, E=sA.enter_context)
        sb("vrow", [128, H * VW], BF16, n=2, into=A, E=sA.enter_context)
        sb("stg_i", [128, D], I32, n=2, into=A, E=sA.enter_context)
        for _p in "qkv":
            sb(f"bTs_{_p}", [R, D], BF16, into=A, E=sA.enter_context)
        sb("wstg", [128, D], F32, n=1, into=A, E=sA.enter_context)
        A["ps"] = [sA.enter_context(nc.psum_tensor(f"psA{i}", [128, QB], F32))
                   for i in range(7)]
        A["psV"] = sA.enter_context(nc.psum_tensor("psV", [128, 1024], BF16))

        B = {"done": False}

        def alloc_B():
            if B["done"]:
                return
            sA.close()
            KW = C.GROUP * PIECE
            sb("kbuf", [128, HP * KW], BF16, n=2, into=B)
            sb("vbuf", [128, C.KCP * H * VW], BF16, n=2, into=B)
            B["oT"] = [TE(nc.sbuf_tensor(f"oT{i}", [65, QB], F32))
                       for i in range(H)]
            sb("hT_B", [128, HT * QB], BF16, into=B)
            sb("xn_q", [128, DC * QB], BF16, into=B)
            sb("x1_q", [128, DC * QB], F32, into=B)
            sb("qt_q", [128, DC * QB], BF16, into=B)
            sb("Pt", [128, QB], BF16, n=4, into=B)
            B["recl"] = P["u_s"][1][0:1, :]
            B["recf"] = P["m_r"][0:1, :]
            B["ps"] = [TE(nc.psum_tensor(f"psB{i}", [128, QB], F32))
                       for i in range(8)]
            B["done"] = True

        with nc.Block() as block:
            def runner(eng_name):
                def f(E):
                    w = W(nc, eng_name, E, sems)
                    emit_all(w, C, io, P, A, B, alloc_B)
                f.__name__ = "_"
                return f
            block.sync(runner("sync"))
            block.scalar(runner("scalar"))
            block.vector(runner("vector"))
            block.gpsimd(runner("gpsimd"))
            block.tensor(runner("tensor"))

    return nc


def emit_all(w: W, C: Cfg, io, P, A, B, alloc_B):
    D, TOK, QB, HID = C.D, C.TOK, C.QB, C.HID
    DC, HT, HP, NQB, NP = C.DC, C.HT, C.HP, C.NQB, C.NP
    H, R, DH = C.HEADS, C.RANK, C.DH
    PIECE, GT, KCP = C.PIECE, C.GT, C.KCP
    VW = DH + 1
    KW = C.GROUP * PIECE
    HH = HID // 2

    def xch(c, q0=0, n=TOK):
        return A["x_res"][:, c * TOK + q0: c * TOK + q0 + n]

    def xnch(c, q0, n=QB):
        return A["xnb"][:, c * TOK + q0: c * TOK + q0 + n]

    war = {}

    for c in range(DC):
        w.dma("sync", xch(c), io["xT"][c * 128:(c + 1) * 128, :], sem="syx")
    x_tok = w.tok("syx")

    w.wait("sync", "syc")
    w.dma("sync", P["identS"][:], io["ident_in"], sem="syc")
    for p in "qkvo":
        for c in range(DC):
            w.wait("sync", "syc")
            w.dma("sync", P[f"bias_{p}"][:, c:c + 1],
                  io[f"bi_{p}"][c * 128:(c + 1) * 128, 0:1], sem="syc")
        w.wait("sync", "dve")
        w.wait("sync", "syc")
        w.dma("sync", A["wstg"][0][:, 0:DC * R].rearrange("p (c r) -> p c r", r=R),
              io[f"aT_{p}"].rearrange("(c p) r -> p c r", p=128), sem="syc")
        w.wait("vector", "syc")
        w.dve("tensor_copy", P[f"aTs_{p}"][:], A["wstg"][0][:, 0:DC * R])
        w.wait("sync", "dve")
        w.wait("sync", "syc")
        w.dma("sync", A["wstg"][0][0:R, 0:D], io[f"bT_{p}"], sem="syc")
        w.wait("vector", "syc")
        w.dve("tensor_copy", (P if p == "o" else A)[f"bTs_{p}"][:],
              A["wstg"][0][0:R, 0:D])

    w.dve("memset", P["epsr"][:], C.EPS)
    w.dve("memset", P["ones1"][:], 1.0)
    w.dve("memset", P["onesf"][:], 1.0)
    w.dve("memset", P["onesc"][:], 1.0)
    w.dve("memset", P["Eind"][:], 0.0, dr=True)
    for c in range(DC):
        w.dve("memset", P["Eind"][0:64, c * H + 2 * c: c * H + 2 * c + 1], 1.0)
        w.dve("memset", P["Eind"][64:128, c * H + 2 * c + 1: c * H + 2 * c + 2], 1.0)
    for i in range(2):
        vr = A["vrow"][i][:, :].rearrange("p (h v) -> p h v", v=VW)
        w.dve("memset", vr[:, :, DH:DH + 1], 1.0)
    p0_dve = w.tok("dve")

    idx = [0]

    def run_preps(preps, ssem="sps"):
        for nm, srct, dst, nt, ncols, nhalf, stgb, sdt, doff in preps:
            for t in range(nt):
                for half in range(nhalf):
                    sl = idx[0] % 2
                    idx[0] += 1
                    cw = ncols // nhalf
                    stg_in = A["stg_i"][sl] if sdt == I32 else A["wstg"][0]
                    w.await_tok("sync", war.get(("prep_in", id(stg_in))))
                    w.dma("sync", stg_in[:, 0:cw],
                          srct[t][:, half * cw:(half + 1) * cw], sem=f"syp{sl}")
                    w.wait("vector", f"syp{sl}")
                    w.await_tok("vector", war.get(("prep_out", id(stgb[sl]))))
                    w.dve("tensor_copy", stgb[sl][:, 0:cw], stg_in[:, 0:cw])
                    war[("prep_in", id(stg_in))] = w.tok("dve")
                    w.wait("gpsimd", "dve")
                    w.dma("gpsimd",
                          dst[t][:, doff + half * cw: doff + (half + 1) * cw],
                          stgb[sl][:, 0:cw], sem=f"{ssem}{sl}")
                    war[("prep_out", id(stgb[sl]))] = w.tok(f"{ssem}{sl}")

    run_preps([("g", io["wg_in"], io["s_wg"], HT, D, 1, P["gtile"], I32, 0),
               ("u", io["wu_in"], io["s_wg"], HT, D, 1, P["gtile"], I32, D)])
    prep_done = [w.tok(s) for s in ("syp0", "syp1", "sps0", "sps1")]
    run_preps([("d", io["wd_in"], io["s_wd"], DC, HID, HID // D,
                P["dtile"], I32, 0)], ssem="spd")
    war[("dtile", 0)] = w.tok("spd0")
    war[("dtile", 1)] = w.tok("spd1")

    pstok = {}

    def ln(src_of, xn_to, ps4, qbw=QB, src_tok=None):
        ps_sum, ps_ssq, ps_r, ps_mr = ps4
        w.await_tok("vector", src_tok)
        for c in range(DC):
            sl = c % 2
            w.await_toks("vector", war.get(("xb", sl), []))
            w.dve("tensor_copy", P["xb"][sl][:, 0:qbw], src_of(c))
            cast_t = w.tok("dve")
            w.wait("scalar", "dve", cast_t[1])
            w.await_tok("scalar", war.get(("xsq", sl)))
            w.act("square", P["xsq"][sl][:, 0:qbw], P["xb"][sl][:, 0:qbw])
            sq_t = w.tok("act")
            w.wait("tensor", "dve", cast_t[1])
            if c == 0:
                w.await_tok("tensor", pstok.get(id(ps_sum)))
                w.await_tok("tensor", pstok.get(id(ps_ssq)))
            w.mm(ps_sum[0:1, 0:qbw], P["onesc"][:, 0:1], P["xb"][sl][:, 0:qbw],
                 start=(c == 0), stop=(c == DC - 1), inc="pe")
            war[("xb", sl)] = [w.tok("pe"), sq_t]
            w.wait("tensor", "act", sq_t[1])
            w.mm(ps_ssq[0:1, 0:qbw], P["onesc"][:, 0:1], P["xsq"][sl][:, 0:qbw],
                 start=(c == 0), stop=(c == DC - 1), inc="pe")
            war[("xsq", sl)] = w.tok("pe")
        mm_t = w.tok("pe")
        w.wait("vector", "pe", mm_t[1])
        w.dve("tensor_scalar_mul", P["m_r"][0:1, 0:qbw], ps_sum[0:1, 0:qbw], 1.0 / D, dr=True)
        w.dve("tensor_mul", P["mr_r"][0:1, 0:qbw], P["m_r"][0:1, 0:qbw],
              P["m_r"][0:1, 0:qbw], dr=True)
        w.dve("scalar_tensor_tensor", P["t_r"][0:1, 0:qbw], ps_ssq[0:1, 0:qbw],
              1.0 / D, P["mr_r"][0:1, 0:qbw], op0=ALU.mult, op1=ALU.subtract, dr=True)
        pstok[id(ps_sum)] = pstok[id(ps_ssq)] = w.tok("dve")
        w.dve("tensor_scalar_add", P["tmp"][0][0:1, 0:qbw], P["t_r"][0:1, 0:qbw],
              C.EPS, dr=True)
        var_t = w.tok("dve")
        w.wait("scalar", "dve", var_t[1])
        w.act("activation", P["t_r"][0:1, 0:qbw], P["tmp"][0][0:1, 0:qbw], AF.Sqrt,
              bias=0.0, scale=1.0)
        sd_t = w.tok("act")
        w.wait("vector", "act", sd_t[1])
        w.dve("reciprocal", P["t_r"][0:1, 0:qbw], P["t_r"][0:1, 0:qbw], dr=True)
        w.dve("tensor_mul", P["mr_r"][0:1, 0:qbw], P["t_r"][0:1, 0:qbw],
              P["t_r"][0:1, 0:qbw], dr=True)
        w.dve("tensor_mul", P["mr_r"][0:1, 0:qbw], P["mr_r"][0:1, 0:qbw],
              P["tmp"][0][0:1, 0:qbw], dr=True)
        w.dve("tensor_scalar", P["mr_r"][0:1, 0:qbw], P["mr_r"][0:1, 0:qbw],
              -0.5, 1.5, op0=ALU.mult, op1=ALU.add, dr=True)
        w.dve("tensor_mul", P["t_r"][0:1, 0:qbw], P["t_r"][0:1, 0:qbw],
              P["mr_r"][0:1, 0:qbw], dr=True)
        w.dve("tensor_mul", P["mr_r"][0:1, 0:qbw], P["m_r"][0:1, 0:qbw],
              P["t_r"][0:1, 0:qbw], dr=True)
        row_t = w.tok("dve")
        w.wait("tensor", "dve", row_t[1])
        w.await_tok("tensor", pstok.get(id(ps_r)))
        w.await_tok("tensor", pstok.get(id(ps_mr)))
        w.mm(ps_r[:, 0:qbw], P["onesf"][0:1, :], P["t_r"][0:1, 0:qbw], inc="pe")
        w.mm(ps_mr[:, 0:qbw], P["onesf"][0:1, :], P["mr_r"][0:1, 0:qbw], inc="pe")
        bc_t = w.tok("pe")
        w.wait("vector", "pe", bc_t[1])
        for c in range(DC):
            sl = c % 2
            w.dve("tensor_mul", P["tmp"][sl][:, 0:qbw], src_of(c), ps_r[:, 0:qbw],
                  dr=True)
            w.dve("tensor_sub", xn_to(c), P["tmp"][sl][:, 0:qbw], ps_mr[:, 0:qbw])
        pstok[id(ps_r)] = pstok[id(ps_mr)] = w.tok("dve")
        return w.tok("dve")

    def ffn(xn_of, res_of, dst_of, ps4, hT, xn_tok, dst_cb=None,
            res_tok=None):
        ps_g, ps_u, ps_d0, ps_d1 = ps4
        s = C.S8
        last_h_t = None
        w.wait("tensor", xn_tok[0], xn_tok[1])
        w.await_tok("vector", res_tok)
        for ht in range(HT):
            sl = ht % 2
            w.await_tok("sync", war.get(("gtile", sl)))
            w.await_tok("sync", war.get(("utile", sl)))
            w.dma("sync", P["gtile"][sl][:], io["s_wg"][ht], sem=f"syg{sl}")
            w.wait("tensor", f"syg{sl}")
            w.await_tok("tensor", pstok.get(id(ps_g)))
            for c in range(DC):
                w.mm(ps_g[:, :], P["gtile"][sl][:, c * 128:(c + 1) * 128],
                     xn_of(c), start=(c == 0), stop=(c == DC - 1),
                     inc="pe" if c == DC - 1 else None)
            gmm_t = w.tok("pe")
            war[("gtile", sl)] = gmm_t
            w.await_tok("tensor", pstok.get(id(ps_u)))
            for c in range(DC):
                w.mm(ps_u[:, :], P["gtile"][sl][:, D + c * 128:D + (c + 1) * 128],
                     xn_of(c), start=(c == 0), stop=(c == DC - 1),
                     inc="pe" if c == DC - 1 else None)
            umm_t = w.tok("pe")
            war[("utile", sl)] = umm_t
            w.wait("scalar", "pe", gmm_t[1])
            w.await_tok("scalar", war.get(("sg", sl)))
            w.act("activation", P["sg"][sl][:], ps_g[:, :], AF.Silu, scale=s)
            sil_t = w.tok("act")
            pstok[id(ps_g)] = sil_t
            w.wait("vector", "pe", umm_t[1])
            w.wait("vector", "act", sil_t[1])
            w.dve("scalar_tensor_tensor", hT[:, ht * QB:(ht + 1) * QB], ps_u[:, :],
                  s, P["sg"][sl][:], op0=ALU.mult, op1=ALU.mult)
            pstok[id(ps_u)] = war[("sg", sl)] = w.tok("dve")
            last_h_t = w.tok("dve")
        for ot in range(DC):
            ps_d = ps_d0 if ot % 2 == 0 else ps_d1
            sl = ot % 2
            w.await_tok("sync", war.get(("dtile", sl)))
            w.dma("sync", P["dtile"][sl][:], io["s_wd"][ot], sem=f"syd{sl}")
            w.wait("tensor", f"syd{sl}")
            w.await_tok("tensor", pstok.get(id(ps_d)))
            w.wait("tensor", "dve", last_h_t[1])
            for h in range(HT):
                w.mm(ps_d[:, :], P["dtile"][sl][:, h * 128:(h + 1) * 128],
                     hT[:, h * QB:(h + 1) * QB], start=(h == 0),
                     stop=(h == HT - 1), inc="pe" if h == HT - 1 else None)
            dmm_t = w.tok("pe")
            war[("dtile", sl)] = dmm_t
            w.wait("vector", "pe", dmm_t[1])
            w.dve("scalar_tensor_tensor", dst_of(ot), ps_d[:, :], 0.5 * s,
                  res_of(ot), op0=ALU.mult, op1=ALU.add)
            pstok[id(ps_d)] = w.tok("dve")
            if dst_cb is not None:
                dst_cb(ot, w.tok("dve"))

    psA = A["ps"]
    w.wait("vector", "syc")
    w.wait("vector", "syx")
    w.await_toks("sync", prep_done)
    for qb in range(NQB):
        q0 = qb * QB
        ln_t = ln(lambda c: xch(c, q0, QB), lambda c: xnch(c, q0),
                  (psA[3], psA[4], psA[5], psA[6]))
        ffn(lambda c: xnch(c, q0), lambda c: xch(c, q0, QB),
            lambda c: xch(c, q0, QB), (psA[0], psA[1], psA[2], psA[3]),
            A["hT_A"], ln_t)
    x1_tok = w.tok("dve")

    run_preps([(p, io[f"wT_{p}"], io[f"s_w{p}"], DC, D, 1, P["wtile"], F32, 0)
               for p in "qkvo"])
    for sl in range(2):
        war[("wtile", sl)] = war[("prep_out", id(P["wtile"][sl]))]
    w.wait("sync", "sps0")
    w.wait("sync", "sps1")

    w.wait("scalar", "dve", x1_tok[1])
    for c in range(DC):
        w.dma("scalar", io["x1sp"][c * 128:(c + 1) * 128, :], xch(c), sem="scs")
    spill_tok = w.tok("scs")

    if KDBG == "x1":
        w.wait("scalar", "dve", x1_tok[1])
        for c in range(DC):
            w.dma("scalar", io["outT"][c * 128:(c + 1) * 128, :], xch(c),
                  sem="sco")
        w.wait("scalar", "sco")
        alloc_B()
        return

    for qb in range(NQB):
        q0 = qb * QB
        xn2_tok = ln(lambda c, q0=q0: xch(c, q0, QB),
                     lambda c, q0=q0: xnch(c, q0),
                     (psA[3], psA[4], psA[5], psA[6]), src_tok=x1_tok)
    w.wait("tensor", "dve", xn2_tok[1])

    def proj(p, qb, q0, dest_cb):
        usl = qb % 2
        deferred = [None]
        w.await_tok("tensor", pstok.get(id(psA[2])))
        for c in range(DC):
            w.mm(psA[2][0:R, :], P[f"aTs_{p}"][:, c * R:(c + 1) * R],
                 xnch(c, q0), start=(c == 0), stop=(c == DC - 1),
                 inc="pe" if c == DC - 1 else None)
        umm_t = w.tok("pe")
        w.wait("vector", "pe", umm_t[1])
        w.await_tok("vector", war.get(("u_s", usl)))
        w.dve("tensor_scalar_mul", P["u_s"][usl][:, :], psA[2][0:R, :], C.LORA)
        pstok[id(psA[2])] = w.tok("dve")
        u_t = w.tok("dve")
        for ot in range(DC):
            wsl = ot % 4
            q = "sync"
            w.await_tok(q, war.get(("wtile", wsl)))
            w.dma(q, P["wtile"][wsl][:], io[f"s_w{p}"][ot],
                  sem=f"syw{wsl}")
            ps = psA[ot % 2]
            w.wait("tensor", f"syw{wsl}")
            w.await_tok("tensor", pstok.get(id(ps)))
            for c in range(DC):
                w.mm(ps[:, :], P["wtile"][wsl][:, c * 128:(c + 1) * 128],
                     xnch(c, q0), start=(c == 0), stop=False)
            w.wait("tensor", "dve", u_t[1])
            w.mm(ps[:, :], A[f"bTs_{p}"][:, ot * 128:(ot + 1) * 128],
                 P["u_s"][usl][0:R, :], start=False, stop=True, inc="pe")
            mm_t = w.tok("pe")
            war[("wtile", wsl)] = mm_t
            war[("u_s", usl)] = mm_t
            nxt = dest_cb(ot, ps, mm_t)
            if deferred[0] is not None:
                deferred[0]()
            deferred[0] = nxt
        if deferred[0] is not None:
            deferred[0]()
            deferred[0] = None

    kv_cc = {}
    for qb in range(NQB):
        q0 = qb * QB

        def k_dest(ot, ps, mm_t, qb=qb, q0=q0):
            sl = ot % 2
            w.wait("vector", "pe", mm_t[1])
            w.await_toks("vector", war.get(("kst", sl), []))
            w.dve("tensor_scalar", A["kst"][sl][:], ps[:, :],
                  P["bias_k"][:, ot:ot + 1], None, op0=ALU.add)
            pstok[id(ps)] = ke_t = w.tok("dve")
            w.wait("scalar", "dve", ke_t[1])
            w.await_tok("scalar", war.get(("ksq", sl)))
            w.act("square", A["ksq"][sl][:], A["kst"][sl][:])
            ksq_t = w.tok("act")
            w.wait("gpsimd", "dve", ke_t[1])
            pc0 = q0 // PIECE
            nsp = QB // PIECE
            w.dma("gpsimd",
                  io["kp_own"][pc0:pc0 + nsp, ot * 128:(ot + 1) * 128, :]
                  .rearrange("s p t -> p s t"),
                  A["kst"][sl][:, :].rearrange("p (s t) -> p s t", t=PIECE),
                  sem=f"sysk{sl}")
            war[("kst", sl)] = [("act", ksq_t[1]), w.tok(f"sysk{sl}")]

            def knmm(ot=ot, sl=sl, ksq_t=ksq_t):
                w.wait("tensor", "act", ksq_t[1])
                if ot == 0:
                    w.await_tok("tensor", pstok.get(id(psA[3])))
                w.mm(psA[3][0:H, :], P["Eind"][:, ot * H:(ot + 1) * H],
                     A["ksq"][sl][:], start=(ot == 0), stop=(ot == DC - 1),
                     inc="pe")
                war[("ksq", sl)] = w.tok("pe")
            return knmm

        proj("k", qb, q0, k_dest)
        knm_t = w.tok("pe")
        w.wait("vector", "pe", knm_t[1])
        w.dve("reduce_max", P["kn_sc"][:, qb:qb + 1], psA[3][0:H, :], axis=AX.X)
        pstok[id(psA[3])] = w.tok("dve")

        def v_dest(ot, ps, mm_t, qb=qb, q0=q0):
            w.wait("vector", "pe", mm_t[1])
            w.await_tok("vector", war.get("VT"))
            w.dve("tensor_scalar", A["VT_qb"][:, ot * QB:(ot + 1) * QB], ps[:, :],
                  P["bias_v"][:, ot:ot + 1], None, op0=ALU.add)
            pstok[id(ps)] = w.tok("dve")

        proj("v", qb, q0, v_dest)
        vev_t = w.tok("dve")
        for tb in range(QB // 128):
            vsl = tb % 2
            w.wait("tensor", "dve", vev_t[1])
            w.await_tok("tensor", war.get(("vrow", vsl)))
            w.await_tok("tensor", pstok.get(id(A["psV"])))
            for c in range(DC):
                w.transp(A["psV"][:, c * 128:(c + 1) * 128],
                         A["VT_qb"][:, c * QB + tb * 128: c * QB + tb * 128 + 128],
                         P["identS"][:], inc="pe" if c == DC - 1 else None)
            tr_t = w.tok("pe")
            war["VT"] = tr_t
            w.wait("vector", "pe", tr_t[1])
            vr = A["vrow"][vsl][:, :].rearrange("p (h v) -> p h v", v=VW)
            w.await_tok("vector", p0_dve)
            w.dve("tensor_copy", vr[:, :, 0:DH],
                  A["psV"][:, 0:D].rearrange("p (h d) -> p h d", d=DH))
            pstok[id(A["psV"])] = vcp_t = w.tok("dve")
            w.wait("gpsimd", "dve", vcp_t[1])
            pc = (q0 + tb * 128) // PIECE
            pr = ((q0 + tb * 128) % PIECE) // 128
            w.dma("gpsimd", io["vp_own"][pc][pr * 128:(pr + 1) * 128, :],
                  A["vrow"][vsl][:], sem=f"sysv{vsl}")
            war[("vrow", vsl)] = w.tok(f"sysv{vsl}")

        for s in ("sysk0", "sysk1", "sysv0", "sysv1"):
            w.wait("gpsimd", s)
        for sp in range(QB // PIECE):
            pc = (q0 + sp * PIECE) // PIECE
            w.cc(io["kp_own"][pc].opt(), io["kp_g"][pc].opt(), io["groups"])
            w.cc(io["vp_own"][pc].opt(), io["vp_g"][pc].opt(), io["groups"])
            kv_cc[pc] = w.tok("cc")

        def q_dest(ot, ps, mm_t, qb=qb, q0=q0):
            sl = ot % 2
            w.wait("vector", "pe", mm_t[1])
            w.await_toks("vector", war.get(("kst", sl), []))
            w.dve("tensor_scalar", A["kst"][sl][:], ps[:, :],
                  P["bias_q"][:, ot:ot + 1], None, op0=ALU.add)
            pstok[id(ps)] = qe_t = w.tok("dve")
            w.wait("scalar", "dve", qe_t[1])
            w.await_tok("scalar", war.get(("ksq", sl)))
            w.act("square", A["ksq"][sl][:], A["kst"][sl][:])
            qsq_t = w.tok("act")
            w.wait("gpsimd", "dve", qe_t[1])
            w.dma("gpsimd", io["s_qT"][ot * 128:(ot + 1) * 128, q0:q0 + QB],
                  A["kst"][sl][:], sem=f"sysk{sl}")
            war[("kst", sl)] = [("act", qsq_t[1]), w.tok(f"sysk{sl}")]

            def qnmm(ot=ot, sl=sl, qsq_t=qsq_t):
                w.wait("tensor", "act", qsq_t[1])
                if ot == 0:
                    w.await_tok("tensor", pstok.get(id(psA[3])))
                w.mm(psA[3][0:H, :], P["Eind"][:, ot * H:(ot + 1) * H],
                     A["ksq"][sl][:], start=(ot == 0), stop=(ot == DC - 1),
                     inc="pe")
                war[("ksq", sl)] = w.tok("pe")
            return qnmm

        proj("q", qb, q0, q_dest)
        qnm_t = w.tok("pe")
        w.wait("vector", "pe", qnm_t[1])
        w.dve("reduce_max", P["qn_sc"][:, qb:qb + 1], psA[3][0:H, :], axis=AX.X)
        pstok[id(psA[3])] = w.tok("dve")

    w.wait("vector", "dve")
    w.dve("reduce_max", P["kn2"][:, 0:1], P["kn_sc"][:, :], axis=AX.X)
    w.dve("reduce_max", P["qn2"][:, 0:1], P["qn_sc"][:, :], axis=AX.X)
    n_t = w.tok("dve")
    w.wait("gpsimd", "dve", n_t[1])
    w.dma("gpsimd", io["nrm_own"].rearrange("a h -> h a"), P["kn2"][:, 0:1])
    w.dma("gpsimd", io["qn_b"].rearrange("a h -> h a"), P["qn2"][:, 0:1])
    w.wait("gpsimd", "gp")
    w.cc(io["nrm_own"].opt(), io["nrm_g"].opt(), io["groups"])
    nrm_cc = w.tok("cc")
    w.wait("gpsimd", "cc", nrm_cc[1])
    w.dma("gpsimd", P["grow"][0:1, :], io["nrm_g"].rearrange("r h -> (r h)"))
    w.dma("gpsimd", P["qrow"][0:1, :], io["qn_b"])
    row_ld = w.tok("gp")
    w.wait("vector", "gp", row_ld[1])
    w.dve("tensor_max", P["krow"][0:1, :], P["grow"][0:1, 0:H],
          P["grow"][0:1, H:2 * H], dr=True)
    for r in range(2, C.GROUP):
        w.dve("tensor_max", P["krow"][0:1, :], P["krow"][0:1, :],
              P["grow"][0:1, r * H:(r + 1) * H], dr=True)
    w.dve("tensor_mul", P["prow"][0:1, :], P["qrow"][0:1, :], P["krow"][0:1, :], dr=True)
    p_t = w.tok("dve")
    w.wait("scalar", "dve", p_t[1])
    w.act("activation", P["nrow"][0:1, :], P["prow"][0:1, :], AF.Sqrt,
          bias=0.0, scale=1.02 / 64.0)
    sq_t = w.tok("act")
    w.wait("vector", "act", sq_t[1])
    w.dve("tensor_scalar_mul", P["nrow_b"][0:1, :], P["nrow"][0:1, :], -1.0, dr=True)
    nb_t = w.tok("dve")
    w.wait("tensor", "dve", nb_t[1])
    w.await_tok("tensor", pstok.get(id(psA[3])))
    w.mm(psA[3][:, 0:H], P["ones1"][0:1, :], P["nrow_b"][0:1, :], inc="pe")
    nm_t = w.tok("pe")
    w.wait("vector", "pe", nm_t[1])
    w.dve("tensor_copy", P["negm"][:], psA[3][:, 0:H])
    negm_tok = w.tok("dve")

    for sem in list(w.c):
        if sem not in ("cc",):
            for e in ("sync", "scalar", "vector", "gpsimd", "tensor"):
                w.wait(e, sem)
    alloc_B()
    kbuf, vbuf, oT = B["kbuf"], B["vbuf"], B["oT"]
    hT_B, xn_q, x1_q, qt_q = B["hT_B"], B["xn_q"], B["x1_q"], B["qt_q"]
    aoT = hT_B
    psB = B["ps"]

    kv_war = [None, None]
    piece_ld = {}

    def load_k_hp(pc, hp, gate):
        slot = pc % 2
        q, fam = (("sync", f"syk{slot}") if hp % 2 == 0
                  else ("gpsimd", f"syvp{slot}"))
        w.await_tok(q, gate)
        w.wait(q, "cc")
        w.dma(q,
              kbuf[slot][:, hp * KW:(hp + 1) * KW]
              .rearrange("p (r t) -> p r t", t=PIECE),
              io["kp_g"][pc].rearrange("(r hh p) t -> p hh r t",
                                       p=128, hh=HP)[:, hp, :, :],
              sem=fam)

    def load_v(qb_, pc, gate):
        slot = pc % 2
        hk = KCP // 2
        for half in range(2):
            q, fam = (("sync", f"syk{slot}") if half == 0
                      else ("gpsimd", f"syvp{slot}"))
            w.await_tok(q, gate)
            w.wait(q, "cc")
            w.dma(q,
                  vbuf[slot][:, half * hk * H * VW:(half + 1) * hk * H * VW]
                  .rearrange("p (tc f) -> p tc f", f=H * VW),
                  io["vp_g"][pc].rearrange("(tc p) f -> p tc f", p=128)
                  [:, half * hk:(half + 1) * hk, :], sem=fam)
        piece_ld[(qb_, pc)] = [w.tok(f"syk{slot}"), w.tok(f"syvp{slot}")]

    def load_piece(qb_, pc):
        for hp in range(HP):
            load_k_hp(pc, hp, kv_war[pc % 2])
        load_v(qb_, pc, kv_war[pc % 2])

    for qb in range(NQB):
        q0 = qb * QB

        w.wait("sync", "scs", spill_tok[1])
        w.await_tok("sync", war.get("x1q"))
        for c in range(DC):
            w.dma("sync", x1_q[:, c * QB:(c + 1) * QB],
                  io["x1sp"][c * 128:(c + 1) * 128, q0:q0 + QB], sem="sy1")
        x1q_tok = w.tok("sy1")
        w.await_tok("sync", war.get("qt_q"))
        for c in range(DC):
            w.dma("sync", qt_q[:, c * QB:(c + 1) * QB],
                  io["s_qT"][c * 128:(c + 1) * 128, q0:q0 + QB], sem="syq")
        qt_tok = w.tok("syq")

        if qb == 0:
            load_piece(0, 0)
            if NP > 1:
                load_piece(0, 1)

        units = [(pc, hp, kc) for pc in range(NP) for hp in range(HP)
                 for kc in range(KCP)]
        S_free = [None, None]
        exp_tok = {}
        av_tok = {}
        oT_free = {}

        def emit_av(u):
            pc_, hp_, kc_ = units[u]
            slot_ = pc_ % 2
            sp_ = u % 2
            w.wait("tensor", "act", exp_tok[u][1])
            if kc_ == 0:
                w.await_tok("tensor", oT_free.get(hp_ % 2))
                w.await_tok("tensor", pstok.get(id(psB[4 + 2 * (hp_ % 2)])))
                w.await_tok("tensor", pstok.get(id(psB[5 + 2 * (hp_ % 2)])))
            vb0 = kc_ * H * VW
            w.mm(psB[4 + 2 * (hp_ % 2)][0:65, :],
                 vbuf[slot_][:, vb0 + 2 * hp_ * VW: vb0 + 2 * hp_ * VW + VW],
                 B["Pt"][2 * sp_][:], start=(kc_ == 0), stop=(kc_ == KCP - 1))
            w.mm(psB[5 + 2 * (hp_ % 2)][0:65, :],
                 vbuf[slot_][:, vb0 + (2 * hp_ + 1) * VW: vb0 + (2 * hp_ + 1) * VW + VW],
                 B["Pt"][2 * sp_ + 1][:], start=(kc_ == 0),
                 stop=(kc_ == KCP - 1), inc="pe")
            av_tok[u] = w.tok("pe")
            if kc_ == KCP - 1:
                w.wait("vector", "pe", av_tok[u][1])
                for side in range(2):
                    head = 2 * hp_ + side
                    ps_o = psB[4 + 2 * (hp_ % 2) + side]
                    w.await_tok("vector", war.get(("oT", head)))
                    if pc_ == 0:
                        w.dve("tensor_copy", oT[head][:, :], ps_o[0:65, :])
                    else:
                        w.dve("tensor_add", oT[head][:, :], oT[head][:, :],
                              ps_o[0:65, :])
                    pstok[id(ps_o)] = war[("oT", head)] = w.tok("dve")
                oT_free[hp_ % 2] = w.tok("dve")
                if hp_ == HP - 1:
                    kv_war[slot_] = w.tok("pe")
                    if pc_ + 2 < NP:
                        load_v(qb, pc_ + 2, kv_war[slot_])
                    elif qb + 1 < NQB:
                        load_v(qb + 1, pc_ + 2 - NP, kv_war[slot_])

        for ui, (pc, hp, kc) in enumerate(units):
            slot = pc % 2
            sp = ui % 2
            if kc == 0 and hp == 0:
                w.await_toks("tensor", piece_ld[(qb, pc)])
                w.await_tok("tensor", qt_tok)
            w.await_tok("tensor", S_free[sp])
            w.await_tok("tensor", pstok.get(id(psB[2 * sp])))
            w.await_tok("tensor", pstok.get(id(psB[2 * sp + 1])))
            kb0 = hp * KW + kc * 128
            w.mm(psB[2 * sp][:, :], kbuf[slot][0:64, kb0:kb0 + 128],
                 qt_q[0:64, hp * QB:(hp + 1) * QB], tp=(0, 0))
            w.mm(psB[2 * sp + 1][:, :], kbuf[slot][64:128, kb0:kb0 + 128],
                 qt_q[64:128, hp * QB:(hp + 1) * QB], tp=(64, 0), inc="pe")
            qk_t = w.tok("pe")
            if kc == KCP - 1:
                if pc + 2 < NP:
                    load_k_hp(pc + 2, hp, qk_t)
                elif qb + 1 < NQB:
                    load_k_hp(pc + 2 - NP, hp, qk_t)
            w.wait("scalar", "pe", qk_t[1])
            if ui == 0 and qb == 0:
                w.wait("scalar", "dve", negm_tok[1])
            w.await_tok("scalar", av_tok.get(ui - 2))
            w.act("activation", B["Pt"][2 * sp][:], psB[2 * sp][:, :], AF.Exp,
                  bias=P["negm"][:, 2 * hp:2 * hp + 1], scale=C.ISCALE)
            w.act("activation", B["Pt"][2 * sp + 1][:], psB[2 * sp + 1][:, :],
                  AF.Exp, bias=P["negm"][:, 2 * hp + 1:2 * hp + 2],
                  scale=C.ISCALE)
            S_free[sp] = exp_tok[ui] = w.tok("act")
            pstok[id(psB[2 * sp])] = pstok[id(psB[2 * sp + 1])] = exp_tok[ui]
            if ui >= 1:
                emit_av(ui - 1)
        emit_av(len(units) - 1)

        att_t = w.tok("dve")

        bc_war = [None, None]
        w.wait("vector", "dve", att_t[1])
        for head in range(H):
            hp, side = head // 2, head % 2
            w.dve("reciprocal", B["recf"], oT[head][64:65, :], dr=True)
            w.dve("tensor_copy", B["recl"], B["recf"])
            rc_t = w.tok("dve")
            ps_bc = psB[head % 2]
            w.wait("tensor", "dve", rc_t[1])
            w.await_tok("tensor", bc_war[head % 2])
            w.await_tok("tensor", pstok.get(id(ps_bc)))
            w.mm(ps_bc[0:64, :], P["ones1"][0:1, 0:64], B["recl"],
                 inc="pe")
            bm_t = w.tok("pe")
            w.wait("vector", "pe", bm_t[1])
            w.dve("tensor_mul",
                  aoT[side * 64:(side + 1) * 64, hp * QB:(hp + 1) * QB],
                  oT[head][0:64, :], ps_bc[0:64, :])
            bc_war[head % 2] = pstok[id(ps_bc)] = w.tok("dve")
        aoT_t = w.tok("dve")

        w.await_tok("tensor", pstok.get(id(psB[2])))
        w.wait("tensor", "dve", aoT_t[1])
        for c in range(DC):
            w.mm(psB[2][0:R, :], P["aTs_o"][:, c * R:(c + 1) * R],
                 aoT[:, c * QB:(c + 1) * QB], start=(c == 0), stop=(c == DC - 1),
                 inc="pe" if c == DC - 1 else None)
        uo_t = w.tok("pe")
        w.wait("vector", "pe", uo_t[1])
        w.await_tok("vector", war.get(("u_s", 0)))
        w.dve("tensor_scalar_mul", P["u_s"][0][:, :], psB[2][0:R, :], C.LORA)
        pstok[id(psB[2])] = uo_e = w.tok("dve")
        for ot in range(DC):
            wsl = ot % 2
            q = "sync"
            w.await_tok(q, war.get(("wtile", wsl)))
            w.dma(q, P["wtile"][wsl][:], io["s_wo"][ot], sem=f"syw{wsl}")
            ps = psB[ot % 2]
            w.wait("tensor", f"syw{wsl}")
            w.await_tok("tensor", pstok.get(id(ps)))
            for c in range(DC):
                w.mm(ps[:, :], P["wtile"][wsl][:, c * 128:(c + 1) * 128],
                     aoT[:, c * QB:(c + 1) * QB], start=(c == 0), stop=False)
            w.wait("tensor", "dve", uo_e[1])
            w.mm(ps[:, :], P["bTs_o"][:, ot * 128:(ot + 1) * 128],
                 P["u_s"][0][0:R, :], start=False, stop=True, inc="pe")
            om_t = w.tok("pe")
            war[("wtile", wsl)] = om_t
            war[("u_s", 0)] = om_t
            w.wait("vector", "pe", om_t[1])
            w.await_tok("vector", x1q_tok)
            w.dve("scalar_tensor_tensor", x1_q[:, ot * QB:(ot + 1) * QB],
                  ps[:, :], P["bias_o"][:, ot:ot + 1],
                  x1_q[:, ot * QB:(ot + 1) * QB], op0=ALU.add, op1=ALU.add)
            pstok[id(ps)] = pstok["x2t"] = w.tok("dve")
        war["qt_q"] = w.tok("pe")

        if KDBG == "x2":
            x2_t = w.tok("dve")
            for ot in range(DC):
                w.wait("scalar", "dve", x2_t[1])
                w.dma("scalar", io["outT"][ot * 128:(ot + 1) * 128, q0:q0 + QB],
                      x1_q[:, ot * QB:(ot + 1) * QB], sem="sco")
            war["x1q"] = w.tok("sco")
            continue

        ln3_t = ln(lambda c: x1_q[:, c * QB:(c + 1) * QB],
                   lambda c: xn_q[:, c * QB:(c + 1) * QB],
                   (psB[3], psB[4], psB[5], psB[6]),
                   src_tok=pstok.get("x2t"))

        st_toks = {}
        ffn(lambda c: xn_q[:, c * QB:(c + 1) * QB],
            lambda c: x1_q[:, c * QB:(c + 1) * QB],
            lambda c: x1_q[:, c * QB:(c + 1) * QB],
            (psB[0], psB[1], psB[2], psB[3]), hT_B, ln3_t,
            dst_cb=lambda ot, tok: st_toks.__setitem__(ot, tok),
            res_tok=pstok.get("x2t"))

        for ot in range(DC):
            w.wait("scalar", "dve", st_toks[ot][1])
            w.dma("scalar", io["outT"][ot * 128:(ot + 1) * 128, q0:q0 + QB],
                  x1_q[:, ot * QB:(ot + 1) * QB], sem="sco")
        war["x1q"] = w.tok("sco")

    w.wait("scalar", "sco")
    w.wait("gpsimd", "cc")


def _prep_inputs(C, inputs, prefix="L0_"):
    f = np.float32
    D, TOK, H, R = C.D, C.TOK, C.HEADS, C.RANK
    DC, HT, HID = C.DC, C.HT, C.HID

    x = np.asarray(inputs[prefix + "attn_x"], dtype=f)

    def tile_w(wmat):
        wt = np.asarray(wmat, dtype=f).T
        t = wt.reshape(DC, 128, DC, 128).transpose(2, 1, 0, 3).reshape(DC, 128, D)
        return np.ascontiguousarray(t)

    def tile_in(wmat, nt, ncols):
        wt = np.asarray(wmat).T
        ic = wt.shape[0] // 128
        t = wt.reshape(ic, 128, nt, 128).transpose(2, 1, 0, 3).reshape(nt, 128, ncols)
        return np.ascontiguousarray(t.astype(np.int32))

    common = dict(ident=np.eye(128, dtype=ml_dtypes.bfloat16))
    for p, nm in zip("qkvo", ["q", "k", "v", "o"]):
        common[f"w{p}T"] = tile_w(inputs[f"{prefix}attn_W{nm}"])
        common[f"b{p}"] = np.ascontiguousarray(
            np.asarray(inputs[f"{prefix}attn_b{nm}"], dtype=f).reshape(D, 1))
        common[f"a{p}T"] = np.ascontiguousarray(
            np.asarray(inputs[f"{prefix}attn_A{nm}"], dtype=f).T)
        common[f"b{p}T"] = np.ascontiguousarray(
            np.asarray(inputs[f"{prefix}attn_B{nm}"], dtype=f).T)
    common["wgT_t"] = tile_in(inputs[prefix + "ffn_w_gate"], HT, D)
    common["wuT_t"] = tile_in(inputs[prefix + "ffn_w_up"], HT, D)
    common["wdT_t"] = tile_in(inputs[prefix + "ffn_w_down"], DC, HID)

    B = x.shape[0]
    cores_per_b = C.N_CORES // B
    in_maps = []
    for core in range(C.N_CORES):
        b = core // cores_per_b
        t0 = (core % cores_per_b) * TOK
        m = dict(common)
        m["xT"] = np.ascontiguousarray(x[b, t0:t0 + TOK, :].T)
        in_maps.append(m)
    return in_maps, x.shape


def _gather_out(C, results, shape):
    B, S, D = shape
    cores_per_b = C.N_CORES // B
    out = np.empty(shape, dtype=np.float32)
    for core in range(C.N_CORES):
        b = core // cores_per_b
        t0 = (core % cores_per_b) * C.TOK
        out[b, t0:t0 + C.TOK, :] = results[core]["outT"].T
    return out


def kernel(**inputs):
    C = Cfg()
    in_maps, shape = _prep_inputs(C, inputs)
    nc = build(C)
    res = run_bass_kernel_spmd(nc, in_maps, core_ids=list(range(C.N_CORES)))
    return _gather_out(C, res.results, shape)


# version 28 (final)
# speedup vs baseline: 1.0122x; 1.0061x over previous
import os as _os
import numpy as np
import ml_dtypes
from contextlib import ExitStack

KDBG = _os.environ.get("KDBG", "")

import concourse.bass as bass
import concourse.mybir as mybir
from concourse.bass_utils import run_bass_kernel_spmd

F32 = mybir.dt.float32
BF16 = mybir.dt.bfloat16
I32 = mybir.dt.int32
AF = mybir.ActivationFunctionType
ALU = mybir.AluOpType
AX = mybir.AxisListType


class Cfg:
    def __init__(self, D=1024, TOK=2048, HEADS=16, HID=3072, QB=512, PIECE=256,
                 RANK=8, GROUP=4, N_CORES=8):
        self.D, self.TOK, self.HEADS, self.HID = D, TOK, HEADS, HID
        self.QB, self.PIECE, self.RANK, self.GROUP = QB, PIECE, RANK, GROUP
        self.N_CORES = N_CORES
        self.DH = D // HEADS
        assert self.DH == 64
        self.DC = D // 128
        self.HP = HEADS // 2
        assert self.DC == self.HP
        self.NQB = TOK // QB
        self.HT = HID // 128
        assert self.HT % 2 == 0 and (HID // 2) % 128 == 0
        self.NP = TOK // PIECE
        self.GT = GROUP * PIECE
        self.KCP = self.GT // 128
        self.LORA = 2.0
        self.S8 = 1.0 / 127.0
        self.EPS = 1e-5
        self.ISCALE = 0.125


class W:

    def __init__(self, nc, eng, E, sems):
        self.nc, self.eng, self.E, self.sems = nc, eng, E, sems
        self.c = {k: 0 for k in sems}

    def wait(self, eng, sem, val=None):
        if val is None:
            val = self.c[sem]
        if self.eng == eng and val > 0:
            self.E.wait_ge(self.sems[sem], val)

    def tok(self, sem):
        return (sem, self.c[sem])

    def await_tok(self, eng, t):
        if t is not None:
            self.wait(eng, t[0], t[1])

    def await_toks(self, eng, ts):
        for t in ts:
            self.await_tok(eng, t)

    def dma(self, q, out, in_, sem=None, **kw):
        if sem is None:
            sem = {"sync": "sy", "scalar": "sc", "gpsimd": "gp"}[q]
        if self.eng == q:
            self.E.dma_start(out=out, in_=in_, **kw).then_inc(self.sems[sem], 16)
        self.c[sem] += 16

    def mm(self, out, lhsT, rhs, start=True, stop=True, tp=None, inc=None):
        if self.eng == "tensor":
            ins = self.E.matmul(out, lhsT=lhsT, rhs=rhs, start=start, stop=stop,
                                tile_position=tp)
            if inc:
                ins.then_inc(self.sems[inc], 1)
        if inc:
            self.c[inc] += 1

    def transp(self, out, in_, ident, inc=None):
        if self.eng == "tensor":
            ins = self.E.transpose(out, in_, ident)
            if inc:
                ins.then_inc(self.sems[inc], 1)
        if inc:
            self.c[inc] += 1

    def _fin(self, eng, ins, inc, dr):
        if inc:
            if self.eng == eng:
                ins.then_inc(self.sems[inc], 1)
            self.c[inc] += 1
        if dr and self.eng == eng:
            self.E.drain()

    def act(self, fn, *a, inc="act", dr=False, **kw):
        ins = getattr(self.E, fn)(*a, **kw) if self.eng == "scalar" else None
        self._fin("scalar", ins, inc, dr)

    def dve(self, fn, *a, inc="dve", dr=False, **kw):
        ins = getattr(self.E, fn)(*a, **kw) if self.eng == "vector" else None
        self._fin("vector", ins, inc, dr)

    def cc(self, ins_ap, outs_ap, groups):
        if self.eng == "gpsimd":
            self.E.collective_compute(
                "AllGather", ALU.bypass, replica_groups=groups,
                ins=[ins_ap], outs=[outs_ap],
            ).then_inc(self.sems["cc"])
        self.c["cc"] += 1


def build(cfg: Cfg):
    nc = bass.Bass("TRN2", target_bir_lowering=False)
    C = cfg
    D, TOK, QB, HID = C.D, C.TOK, C.QB, C.HID
    DC, HT, HP, NP = C.DC, C.HT, C.HP, C.NP
    H, R = C.HEADS, C.RANK
    PIECE = C.PIECE
    VW = C.DH + 1
    groups = [list(range(g * C.GROUP, (g + 1) * C.GROUP))
              for g in range(C.N_CORES // C.GROUP)]

    def din(name, shape, dt):
        return nc.dram_tensor(name, shape, dt, kind="ExternalInput").ap()

    io = dict(
        xT=din("xT", [D, TOK], F32),
        ident_in=din("ident", [128, 128], BF16),
        wg_in=din("wgT_t", [HT, 128, D], I32),
        wu_in=din("wuT_t", [HT, 128, D], I32),
        wd_in=din("wdT_t", [DC, 128, HID], I32),
        outT=nc.dram_tensor("outT", [D, TOK], F32, kind="ExternalOutput").ap(),
        s_wg=nc.dram_tensor("s_wg", [HT, 128, 2 * D], BF16).ap(),
        s_wd=nc.dram_tensor("s_wd", [DC, 128, HID], BF16).ap(),
        s_wo=nc.dram_tensor("s_wo", [DC, 128, D], BF16).ap(),
        s_wq=nc.dram_tensor("s_wq", [DC, 128, D], BF16).ap(),
        s_wk=nc.dram_tensor("s_wk", [DC, 128, D], BF16).ap(),
        s_wv=nc.dram_tensor("s_wv", [DC, 128, D], BF16).ap(),
        s_qT=nc.dram_tensor("s_qT", [D, TOK], BF16).ap(),
        x1sp=nc.dram_tensor("x1sp", [D, TOK], F32).ap(),
        kp_own=nc.dram_tensor("kp_own", [NP, HP * 128, PIECE], BF16).ap(),
        vp_own=nc.dram_tensor("vp_own", [NP, PIECE, H * VW], BF16).ap(),
        kp_g=nc.dram_tensor("kp_g", [NP, C.GROUP * HP * 128, PIECE], BF16).ap(),
        vp_g=nc.dram_tensor("vp_g", [NP, C.GROUP * PIECE, H * VW], BF16).ap(),
        nrm_own=nc.dram_tensor("nrm_own", [1, H], F32).ap(),
        nrm_g=nc.dram_tensor("nrm_g", [C.GROUP, H], F32).ap(),
        qn_b=nc.dram_tensor("qn_b", [1, H], F32).ap(),
        groups=groups,
    )
    for p in "qkvo":
        io[f"wT_{p}"] = din(f"w{p}T", [DC, 128, D], F32)
        io[f"bi_{p}"] = din(f"b{p}", [D, 1], F32)
        io[f"aT_{p}"] = din(f"a{p}T", [D, R], F32)
        io[f"bT_{p}"] = din(f"b{p}T", [R, D], F32)

    with ExitStack() as top:
        TE = top.enter_context
        P = {}

        def sb(name, shape, dt, n=None, into=P, E=TE):
            if n is None:
                into[name] = E(nc.sbuf_tensor(name, shape, dt))
            else:
                into[name] = [E(nc.sbuf_tensor(f"{name}{i}", shape, dt))
                              for i in range(n)]

        sb("identS", [128, 128], BF16)
        sb("ones1", [1, 128], BF16)
        sb("onesf", [1, 128], F32)
        sb("onesc", [128, 1], BF16)
        sb("Eind", [128, DC * H], BF16)
        sb("negm", [128, H], F32)
        for p in "qkvo":
            sb(f"bias_{p}", [128, DC], F32)
            sb(f"aTs_{p}", [128, DC * R], BF16)
        sb("bTs_o", [R, D], BF16)
        sb("m_r", [1, QB], F32)
        sb("t_r", [1, QB], F32)
        sb("mr_r", [1, QB], F32)
        sb("qrow", [1, H], F32)
        sb("grow", [1, C.GROUP * H], F32)
        sb("krow", [1, H], F32)
        sb("prow", [1, H], F32)
        sb("nrow", [1, H], F32)
        sb("nrow_b", [1, H], BF16)
        sb("qn2", [H, 1], F32)
        sb("kn2", [H, 1], F32)
        sb("kn_sc", [H, C.NQB], F32)
        sb("qn_sc", [H, C.NQB], F32)
        sb("u_s", [R, QB], BF16, n=2)
        sb("sg", [128, QB], BF16, n=2)
        sb("xb", [128, QB], BF16, n=2)
        sb("xsq", [128, QB], BF16, n=2)
        sb("tmp", [128, QB], F32, n=2)
        sb("wtile", [128, D], BF16, n=4)
        sb("gtile", [128, 2 * D], BF16, n=2)
        sb("dtile", [128, HID], BF16, n=2)
        sb("epsr", [1, 1], F32)

        sems = {}
        for s in ("syc", "syx", "syp0", "syp1", "sps0", "sps1", "spd0", "spd1",
                  "syg0", "syg1", "syu0", "syu1",
                  "syd0", "syd1", "syd2", "syd3", "syw0", "syw1", "syw2", "syw3",
                  "sy1", "syq", "syk0", "syk1", "syvp0", "syvp1",
                  "sysk0", "sysk1", "sysv0", "sysv1",
                  "scs", "sco", "gp", "cc", "pe", "act", "dve"):
            sems[s] = TE(nc.semaphore(s))

        sA = ExitStack()
        A = {}
        sb("x_res", [128, DC * TOK], F32, into=A, E=sA.enter_context)
        sb("xnb", [128, DC * TOK], BF16, into=A, E=sA.enter_context)
        sb("hT_A", [128, HT * QB], BF16, into=A, E=sA.enter_context)
        sb("VT_qb", [128, DC * QB], BF16, into=A, E=sA.enter_context)
        sb("kst", [128, QB], BF16, n=2, into=A, E=sA.enter_context)
        sb("ksq", [128, QB], BF16, n=2, into=A, E=sA.enter_context)
        sb("vrow", [128, H * VW], BF16, n=2, into=A, E=sA.enter_context)
        sb("stg_i", [128, D], I32, n=2, into=A, E=sA.enter_context)
        for _p in "qkv":
            sb(f"bTs_{_p}", [R, D], BF16, into=A, E=sA.enter_context)
        sb("wstg", [128, D], F32, n=1, into=A, E=sA.enter_context)
        A["ps"] = [sA.enter_context(nc.psum_tensor(f"psA{i}", [128, QB], F32))
                   for i in range(7)]
        A["psV"] = sA.enter_context(nc.psum_tensor("psV", [128, 1024], BF16))

        B = {"done": False}

        def alloc_B():
            if B["done"]:
                return
            sA.close()
            KW = C.GROUP * PIECE
            sb("kbuf", [128, HP * KW], BF16, n=2, into=B)
            sb("vbuf", [128, C.KCP * H * VW], BF16, n=2, into=B)
            B["oT"] = [TE(nc.sbuf_tensor(f"oT{i}", [65, QB], F32))
                       for i in range(H)]
            sb("hT_B", [128, HT * QB], BF16, into=B)
            sb("xn_q", [128, DC * QB], BF16, into=B)
            sb("x1_q", [128, DC * QB], F32, into=B)
            sb("qt_q", [128, DC * QB], BF16, into=B)
            sb("Pt", [128, QB], BF16, n=4, into=B)
            B["recl"] = P["u_s"][1][0:1, :]
            B["recf"] = P["m_r"][0:1, :]
            B["ps"] = [TE(nc.psum_tensor(f"psB{i}", [128, QB], F32))
                       for i in range(8)]
            B["done"] = True

        with nc.Block() as block:
            def runner(eng_name):
                def f(E):
                    w = W(nc, eng_name, E, sems)
                    emit_all(w, C, io, P, A, B, alloc_B)
                f.__name__ = "_"
                return f
            block.sync(runner("sync"))
            block.scalar(runner("scalar"))
            block.vector(runner("vector"))
            block.gpsimd(runner("gpsimd"))
            block.tensor(runner("tensor"))

    return nc


def emit_all(w: W, C: Cfg, io, P, A, B, alloc_B):
    D, TOK, QB, HID = C.D, C.TOK, C.QB, C.HID
    DC, HT, HP, NQB, NP = C.DC, C.HT, C.HP, C.NQB, C.NP
    H, R, DH = C.HEADS, C.RANK, C.DH
    PIECE, GT, KCP = C.PIECE, C.GT, C.KCP
    VW = DH + 1
    KW = C.GROUP * PIECE
    HH = HID // 2

    def xch(c, q0=0, n=TOK):
        return A["x_res"][:, c * TOK + q0: c * TOK + q0 + n]

    def xnch(c, q0, n=QB):
        return A["xnb"][:, c * TOK + q0: c * TOK + q0 + n]

    war = {}

    for c in range(DC):
        w.dma("sync", xch(c), io["xT"][c * 128:(c + 1) * 128, :], sem="syx")
    x_tok = w.tok("syx")

    w.wait("sync", "syc")
    w.dma("sync", P["identS"][:], io["ident_in"], sem="syc")
    for p in "qkvo":
        for c in range(DC):
            w.wait("sync", "syc")
            w.dma("sync", P[f"bias_{p}"][:, c:c + 1],
                  io[f"bi_{p}"][c * 128:(c + 1) * 128, 0:1], sem="syc")
        w.wait("sync", "dve")
        w.wait("sync", "syc")
        w.dma("sync", A["wstg"][0][:, 0:DC * R].rearrange("p (c r) -> p c r", r=R),
              io[f"aT_{p}"].rearrange("(c p) r -> p c r", p=128), sem="syc")
        w.wait("vector", "syc")
        w.dve("tensor_copy", P[f"aTs_{p}"][:], A["wstg"][0][:, 0:DC * R])
        w.wait("sync", "dve")
        w.wait("sync", "syc")
        w.dma("sync", A["wstg"][0][0:R, 0:D], io[f"bT_{p}"], sem="syc")
        w.wait("vector", "syc")
        w.dve("tensor_copy", (P if p == "o" else A)[f"bTs_{p}"][:],
              A["wstg"][0][0:R, 0:D])

    w.dve("memset", P["epsr"][:], C.EPS)
    w.dve("memset", P["ones1"][:], 1.0)
    w.dve("memset", P["onesf"][:], 1.0)
    w.dve("memset", P["onesc"][:], 1.0)
    w.dve("memset", P["Eind"][:], 0.0, dr=True)
    for c in range(DC):
        w.dve("memset", P["Eind"][0:64, c * H + 2 * c: c * H + 2 * c + 1], 1.0)
        w.dve("memset", P["Eind"][64:128, c * H + 2 * c + 1: c * H + 2 * c + 2], 1.0)
    for i in range(2):
        vr = A["vrow"][i][:, :].rearrange("p (h v) -> p h v", v=VW)
        w.dve("memset", vr[:, :, DH:DH + 1], 1.0)
    p0_dve = w.tok("dve")

    idx = [0]

    def run_preps(preps, ssem="sps"):
        for nm, srct, dst, nt, ncols, nhalf, stgb, sdt, doff in preps:
            for t in range(nt):
                for half in range(nhalf):
                    sl = idx[0] % 2
                    idx[0] += 1
                    cw = ncols // nhalf
                    stg_in = A["stg_i"][sl] if sdt == I32 else A["wstg"][0]
                    w.await_tok("sync", war.get(("prep_in", id(stg_in))))
                    w.dma("sync", stg_in[:, 0:cw],
                          srct[t][:, half * cw:(half + 1) * cw], sem=f"syp{sl}")
                    w.wait("vector", f"syp{sl}")
                    w.await_tok("vector", war.get(("prep_out", id(stgb[sl]))))
                    w.dve("tensor_copy", stgb[sl][:, 0:cw], stg_in[:, 0:cw])
                    war[("prep_in", id(stg_in))] = w.tok("dve")
                    w.wait("gpsimd", "dve")
                    w.dma("gpsimd",
                          dst[t][:, doff + half * cw: doff + (half + 1) * cw],
                          stgb[sl][:, 0:cw], sem=f"{ssem}{sl}")
                    war[("prep_out", id(stgb[sl]))] = w.tok(f"{ssem}{sl}")

    run_preps([("g", io["wg_in"], io["s_wg"], HT, D, 1, P["gtile"], I32, 0),
               ("u", io["wu_in"], io["s_wg"], HT, D, 1, P["gtile"], I32, D)])
    prep_done = [w.tok(s) for s in ("syp0", "syp1", "sps0", "sps1")]
    run_preps([("d", io["wd_in"], io["s_wd"], DC, HID, HID // D,
                P["dtile"], I32, 0)], ssem="spd")
    war[("dtile", 0)] = w.tok("spd0")
    war[("dtile", 1)] = w.tok("spd1")

    pstok = {}

    def ln(src_of, xn_to, ps4, qbw=QB, src_tok=None):
        ps_sum, ps_ssq, ps_r, ps_mr = ps4
        w.await_tok("vector", src_tok)
        for c in range(DC):
            sl = c % 2
            w.await_toks("vector", war.get(("xb", sl), []))
            w.dve("tensor_copy", P["xb"][sl][:, 0:qbw], src_of(c))
            cast_t = w.tok("dve")
            w.wait("scalar", "dve", cast_t[1])
            w.await_tok("scalar", war.get(("xsq", sl)))
            w.act("square", P["xsq"][sl][:, 0:qbw], P["xb"][sl][:, 0:qbw])
            sq_t = w.tok("act")
            w.wait("tensor", "dve", cast_t[1])
            if c == 0:
                w.await_tok("tensor", pstok.get(id(ps_sum)))
                w.await_tok("tensor", pstok.get(id(ps_ssq)))
            w.mm(ps_sum[0:1, 0:qbw], P["onesc"][:, 0:1], P["xb"][sl][:, 0:qbw],
                 start=(c == 0), stop=(c == DC - 1), inc="pe")
            war[("xb", sl)] = [w.tok("pe"), sq_t]
            w.wait("tensor", "act", sq_t[1])
            w.mm(ps_ssq[0:1, 0:qbw], P["onesc"][:, 0:1], P["xsq"][sl][:, 0:qbw],
                 start=(c == 0), stop=(c == DC - 1), inc="pe")
            war[("xsq", sl)] = w.tok("pe")
        mm_t = w.tok("pe")
        w.wait("vector", "pe", mm_t[1])
        w.dve("tensor_scalar_mul", P["m_r"][0:1, 0:qbw], ps_sum[0:1, 0:qbw], 1.0 / D, dr=True)
        w.dve("tensor_mul", P["mr_r"][0:1, 0:qbw], P["m_r"][0:1, 0:qbw],
              P["m_r"][0:1, 0:qbw], dr=True)
        w.dve("scalar_tensor_tensor", P["t_r"][0:1, 0:qbw], ps_ssq[0:1, 0:qbw],
              1.0 / D, P["mr_r"][0:1, 0:qbw], op0=ALU.mult, op1=ALU.subtract, dr=True)
        pstok[id(ps_sum)] = pstok[id(ps_ssq)] = w.tok("dve")
        w.dve("tensor_scalar_add", P["tmp"][0][0:1, 0:qbw], P["t_r"][0:1, 0:qbw],
              C.EPS, dr=True)
        var_t = w.tok("dve")
        w.wait("scalar", "dve", var_t[1])
        w.act("activation", P["t_r"][0:1, 0:qbw], P["tmp"][0][0:1, 0:qbw], AF.Sqrt,
              bias=0.0, scale=1.0)
        sd_t = w.tok("act")
        w.wait("vector", "act", sd_t[1])
        w.dve("reciprocal", P["t_r"][0:1, 0:qbw], P["t_r"][0:1, 0:qbw], dr=True)
        w.dve("tensor_mul", P["mr_r"][0:1, 0:qbw], P["t_r"][0:1, 0:qbw],
              P["t_r"][0:1, 0:qbw], dr=True)
        w.dve("tensor_mul", P["mr_r"][0:1, 0:qbw], P["mr_r"][0:1, 0:qbw],
              P["tmp"][0][0:1, 0:qbw], dr=True)
        w.dve("tensor_scalar", P["mr_r"][0:1, 0:qbw], P["mr_r"][0:1, 0:qbw],
              -0.5, 1.5, op0=ALU.mult, op1=ALU.add, dr=True)
        w.dve("tensor_mul", P["t_r"][0:1, 0:qbw], P["t_r"][0:1, 0:qbw],
              P["mr_r"][0:1, 0:qbw], dr=True)
        w.dve("tensor_mul", P["mr_r"][0:1, 0:qbw], P["m_r"][0:1, 0:qbw],
              P["t_r"][0:1, 0:qbw], dr=True)
        row_t = w.tok("dve")
        w.wait("tensor", "dve", row_t[1])
        w.await_tok("tensor", pstok.get(id(ps_r)))
        w.await_tok("tensor", pstok.get(id(ps_mr)))
        w.mm(ps_r[:, 0:qbw], P["onesf"][0:1, :], P["t_r"][0:1, 0:qbw], inc="pe")
        w.mm(ps_mr[:, 0:qbw], P["onesf"][0:1, :], P["mr_r"][0:1, 0:qbw], inc="pe")
        bc_t = w.tok("pe")
        w.wait("vector", "pe", bc_t[1])
        for c in range(DC):
            sl = c % 2
            w.dve("tensor_mul", P["tmp"][sl][:, 0:qbw], src_of(c), ps_r[:, 0:qbw],
                  dr=True)
            w.dve("tensor_sub", xn_to(c), P["tmp"][sl][:, 0:qbw], ps_mr[:, 0:qbw])
        pstok[id(ps_r)] = pstok[id(ps_mr)] = w.tok("dve")
        return w.tok("dve")

    def ffn(xn_of, res_of, dst_of, ps4, hT, xn_tok, dst_cb=None,
            res_tok=None):
        ps_g, ps_u, ps_d0, ps_d1 = ps4
        s = C.S8
        last_h_t = None
        w.wait("tensor", xn_tok[0], xn_tok[1])
        w.await_tok("vector", res_tok)
        for ht in range(HT):
            sl = ht % 2
            w.await_tok("sync", war.get(("gtile", sl)))
            w.await_tok("sync", war.get(("utile", sl)))
            w.dma("sync", P["gtile"][sl][:], io["s_wg"][ht], sem=f"syg{sl}")
            w.wait("tensor", f"syg{sl}")
            w.await_tok("tensor", pstok.get(id(ps_g)))
            for c in range(DC):
                w.mm(ps_g[:, :], P["gtile"][sl][:, c * 128:(c + 1) * 128],
                     xn_of(c), start=(c == 0), stop=(c == DC - 1),
                     inc="pe" if c == DC - 1 else None)
            gmm_t = w.tok("pe")
            war[("gtile", sl)] = gmm_t
            w.await_tok("tensor", pstok.get(id(ps_u)))
            for c in range(DC):
                w.mm(ps_u[:, :], P["gtile"][sl][:, D + c * 128:D + (c + 1) * 128],
                     xn_of(c), start=(c == 0), stop=(c == DC - 1),
                     inc="pe" if c == DC - 1 else None)
            umm_t = w.tok("pe")
            war[("utile", sl)] = umm_t
            w.wait("scalar", "pe", gmm_t[1])
            w.await_tok("scalar", war.get(("sg", sl)))
            w.act("activation", P["sg"][sl][:], ps_g[:, :], AF.Silu, scale=s)
            sil_t = w.tok("act")
            pstok[id(ps_g)] = sil_t
            w.wait("vector", "pe", umm_t[1])
            w.wait("vector", "act", sil_t[1])
            w.dve("scalar_tensor_tensor", hT[:, ht * QB:(ht + 1) * QB], ps_u[:, :],
                  s, P["sg"][sl][:], op0=ALU.mult, op1=ALU.mult)
            pstok[id(ps_u)] = war[("sg", sl)] = w.tok("dve")
            last_h_t = w.tok("dve")
        for ot in range(DC):
            ps_d = ps_d0 if ot % 2 == 0 else ps_d1
            sl = ot % 2
            w.await_tok("sync", war.get(("dtile", sl)))
            w.dma("sync", P["dtile"][sl][:], io["s_wd"][ot], sem=f"syd{sl}")
            w.wait("tensor", f"syd{sl}")
            w.await_tok("tensor", pstok.get(id(ps_d)))
            w.wait("tensor", "dve", last_h_t[1])
            for h in range(HT):
                w.mm(ps_d[:, :], P["dtile"][sl][:, h * 128:(h + 1) * 128],
                     hT[:, h * QB:(h + 1) * QB], start=(h == 0),
                     stop=(h == HT - 1), inc="pe" if h == HT - 1 else None)
            dmm_t = w.tok("pe")
            war[("dtile", sl)] = dmm_t
            w.wait("vector", "pe", dmm_t[1])
            w.dve("scalar_tensor_tensor", dst_of(ot), ps_d[:, :], 0.5 * s,
                  res_of(ot), op0=ALU.mult, op1=ALU.add)
            pstok[id(ps_d)] = w.tok("dve")
            if dst_cb is not None:
                dst_cb(ot, w.tok("dve"))

    psA = A["ps"]
    w.wait("vector", "syc")
    w.wait("vector", "syx")
    w.await_toks("sync", prep_done)
    for qb in range(NQB):
        q0 = qb * QB
        ln_t = ln(lambda c: xch(c, q0, QB), lambda c: xnch(c, q0),
                  (psA[3], psA[4], psA[5], psA[6]))
        ffn(lambda c: xnch(c, q0), lambda c: xch(c, q0, QB),
            lambda c: xch(c, q0, QB), (psA[0], psA[1], psA[2], psA[3]),
            A["hT_A"], ln_t)
    x1_tok = w.tok("dve")

    run_preps([(p, io[f"wT_{p}"], io[f"s_w{p}"], DC, D, 1, P["wtile"], F32, 0)
               for p in "qkvo"])
    for sl in range(2):
        war[("wtile", sl)] = war[("prep_out", id(P["wtile"][sl]))]
    w.wait("sync", "sps0")
    w.wait("sync", "sps1")

    w.wait("scalar", "dve", x1_tok[1])
    for c in range(DC):
        w.dma("scalar", io["x1sp"][c * 128:(c + 1) * 128, :], xch(c), sem="scs")
    spill_tok = w.tok("scs")

    if KDBG == "x1":
        w.wait("scalar", "dve", x1_tok[1])
        for c in range(DC):
            w.dma("scalar", io["outT"][c * 128:(c + 1) * 128, :], xch(c),
                  sem="sco")
        w.wait("scalar", "sco")
        alloc_B()
        return

    for qb in range(NQB):
        q0 = qb * QB
        xn2_tok = ln(lambda c, q0=q0: xch(c, q0, QB),
                     lambda c, q0=q0: xnch(c, q0),
                     (psA[3], psA[4], psA[5], psA[6]), src_tok=x1_tok)
    w.wait("tensor", "dve", xn2_tok[1])

    def proj(p, qb, q0, dest_cb):
        usl = qb % 2
        deferred = [None]
        w.await_tok("tensor", pstok.get(id(psA[2])))
        for c in range(DC):
            w.mm(psA[2][0:R, :], P[f"aTs_{p}"][:, c * R:(c + 1) * R],
                 xnch(c, q0), start=(c == 0), stop=(c == DC - 1),
                 inc="pe" if c == DC - 1 else None)
        umm_t = w.tok("pe")
        w.wait("vector", "pe", umm_t[1])
        w.await_tok("vector", war.get(("u_s", usl)))
        w.dve("tensor_scalar_mul", P["u_s"][usl][:, :], psA[2][0:R, :], C.LORA)
        pstok[id(psA[2])] = w.tok("dve")
        u_t = w.tok("dve")
        for ot in range(DC):
            wsl = ot % 4
            q = "sync"
            w.await_tok(q, war.get(("wtile", wsl)))
            w.dma(q, P["wtile"][wsl][:], io[f"s_w{p}"][ot],
                  sem=f"syw{wsl}")
            ps = psA[ot % 2]
            w.wait("tensor", f"syw{wsl}")
            w.await_tok("tensor", pstok.get(id(ps)))
            for c in range(DC):
                w.mm(ps[:, :], P["wtile"][wsl][:, c * 128:(c + 1) * 128],
                     xnch(c, q0), start=(c == 0), stop=False)
            w.wait("tensor", "dve", u_t[1])
            w.mm(ps[:, :], A[f"bTs_{p}"][:, ot * 128:(ot + 1) * 128],
                 P["u_s"][usl][0:R, :], start=False, stop=True, inc="pe")
            mm_t = w.tok("pe")
            war[("wtile", wsl)] = mm_t
            war[("u_s", usl)] = mm_t
            nxt = dest_cb(ot, ps, mm_t)
            if deferred[0] is not None:
                deferred[0]()
            deferred[0] = nxt
        if deferred[0] is not None:
            deferred[0]()
            deferred[0] = None

    kv_cc = {}
    for qb in range(NQB):
        q0 = qb * QB

        def k_dest(ot, ps, mm_t, qb=qb, q0=q0):
            sl = ot % 2
            w.wait("vector", "pe", mm_t[1])
            w.await_toks("vector", war.get(("kst", sl), []))
            w.dve("tensor_scalar", A["kst"][sl][:], ps[:, :],
                  P["bias_k"][:, ot:ot + 1], None, op0=ALU.add)
            pstok[id(ps)] = ke_t = w.tok("dve")
            w.wait("scalar", "dve", ke_t[1])
            w.await_tok("scalar", war.get(("ksq", sl)))
            w.act("square", A["ksq"][sl][:], A["kst"][sl][:])
            ksq_t = w.tok("act")
            w.wait("gpsimd", "dve", ke_t[1])
            pc0 = q0 // PIECE
            nsp = QB // PIECE
            w.dma("gpsimd",
                  io["kp_own"][pc0:pc0 + nsp, ot * 128:(ot + 1) * 128, :]
                  .rearrange("s p t -> p s t"),
                  A["kst"][sl][:, :].rearrange("p (s t) -> p s t", t=PIECE),
                  sem=f"sysk{sl}")
            war[("kst", sl)] = [("act", ksq_t[1]), w.tok(f"sysk{sl}")]

            def knmm(ot=ot, sl=sl, ksq_t=ksq_t):
                w.wait("tensor", "act", ksq_t[1])
                if ot == 0:
                    w.await_tok("tensor", pstok.get(id(psA[3])))
                w.mm(psA[3][0:H, :], P["Eind"][:, ot * H:(ot + 1) * H],
                     A["ksq"][sl][:], start=(ot == 0), stop=(ot == DC - 1),
                     inc="pe")
                war[("ksq", sl)] = w.tok("pe")
            return knmm

        proj("k", qb, q0, k_dest)
        knm_t = w.tok("pe")
        w.wait("vector", "pe", knm_t[1])
        w.dve("reduce_max", P["kn_sc"][:, qb:qb + 1], psA[3][0:H, :], axis=AX.X)
        pstok[id(psA[3])] = w.tok("dve")

        def v_dest(ot, ps, mm_t, qb=qb, q0=q0):
            w.wait("vector", "pe", mm_t[1])
            w.await_tok("vector", war.get("VT"))
            w.dve("tensor_scalar", A["VT_qb"][:, ot * QB:(ot + 1) * QB], ps[:, :],
                  P["bias_v"][:, ot:ot + 1], None, op0=ALU.add)
            pstok[id(ps)] = w.tok("dve")

        proj("v", qb, q0, v_dest)
        vev_t = w.tok("dve")
        for tb in range(QB // 128):
            vsl = tb % 2
            w.wait("tensor", "dve", vev_t[1])
            w.await_tok("tensor", war.get(("vrow", vsl)))
            w.await_tok("tensor", pstok.get(id(A["psV"])))
            for c in range(DC):
                w.transp(A["psV"][:, c * 128:(c + 1) * 128],
                         A["VT_qb"][:, c * QB + tb * 128: c * QB + tb * 128 + 128],
                         P["identS"][:], inc="pe" if c == DC - 1 else None)
            tr_t = w.tok("pe")
            war["VT"] = tr_t
            w.wait("vector", "pe", tr_t[1])
            vr = A["vrow"][vsl][:, :].rearrange("p (h v) -> p h v", v=VW)
            w.await_tok("vector", p0_dve)
            w.dve("tensor_copy", vr[:, :, 0:DH],
                  A["psV"][:, 0:D].rearrange("p (h d) -> p h d", d=DH))
            pstok[id(A["psV"])] = vcp_t = w.tok("dve")
            w.wait("gpsimd", "dve", vcp_t[1])
            pc = (q0 + tb * 128) // PIECE
            pr = ((q0 + tb * 128) % PIECE) // 128
            w.dma("gpsimd", io["vp_own"][pc][pr * 128:(pr + 1) * 128, :],
                  A["vrow"][vsl][:], sem=f"sysv{vsl}")
            war[("vrow", vsl)] = w.tok(f"sysv{vsl}")

        for s in ("sysk0", "sysk1", "sysv0", "sysv1"):
            w.wait("gpsimd", s)
        for sp in range(QB // PIECE):
            pc = (q0 + sp * PIECE) // PIECE
            w.cc(io["kp_own"][pc].opt(), io["kp_g"][pc].opt(), io["groups"])
            w.cc(io["vp_own"][pc].opt(), io["vp_g"][pc].opt(), io["groups"])
            kv_cc[pc] = w.tok("cc")

        def q_dest(ot, ps, mm_t, qb=qb, q0=q0):
            sl = ot % 2
            w.wait("vector", "pe", mm_t[1])
            w.await_toks("vector", war.get(("kst", sl), []))
            w.dve("tensor_scalar", A["kst"][sl][:], ps[:, :],
                  P["bias_q"][:, ot:ot + 1], None, op0=ALU.add)
            pstok[id(ps)] = qe_t = w.tok("dve")
            w.wait("scalar", "dve", qe_t[1])
            w.await_tok("scalar", war.get(("ksq", sl)))
            w.act("square", A["ksq"][sl][:], A["kst"][sl][:])
            qsq_t = w.tok("act")
            w.wait("gpsimd", "dve", qe_t[1])
            w.dma("gpsimd", io["s_qT"][ot * 128:(ot + 1) * 128, q0:q0 + QB],
                  A["kst"][sl][:], sem=f"sysk{sl}")
            war[("kst", sl)] = [("act", qsq_t[1]), w.tok(f"sysk{sl}")]

            def qnmm(ot=ot, sl=sl, qsq_t=qsq_t):
                w.wait("tensor", "act", qsq_t[1])
                if ot == 0:
                    w.await_tok("tensor", pstok.get(id(psA[3])))
                w.mm(psA[3][0:H, :], P["Eind"][:, ot * H:(ot + 1) * H],
                     A["ksq"][sl][:], start=(ot == 0), stop=(ot == DC - 1),
                     inc="pe")
                war[("ksq", sl)] = w.tok("pe")
            return qnmm

        proj("q", qb, q0, q_dest)
        qnm_t = w.tok("pe")
        w.wait("vector", "pe", qnm_t[1])
        w.dve("reduce_max", P["qn_sc"][:, qb:qb + 1], psA[3][0:H, :], axis=AX.X)
        pstok[id(psA[3])] = w.tok("dve")

    w.wait("vector", "dve")
    w.dve("reduce_max", P["kn2"][:, 0:1], P["kn_sc"][:, :], axis=AX.X)
    w.dve("reduce_max", P["qn2"][:, 0:1], P["qn_sc"][:, :], axis=AX.X)
    n_t = w.tok("dve")
    w.wait("gpsimd", "dve", n_t[1])
    w.dma("gpsimd", io["nrm_own"].rearrange("a h -> h a"), P["kn2"][:, 0:1])
    w.dma("gpsimd", io["qn_b"].rearrange("a h -> h a"), P["qn2"][:, 0:1])
    w.wait("gpsimd", "gp")
    w.cc(io["nrm_own"].opt(), io["nrm_g"].opt(), io["groups"])
    nrm_cc = w.tok("cc")
    w.wait("gpsimd", "cc", nrm_cc[1])
    w.dma("gpsimd", P["grow"][0:1, :], io["nrm_g"].rearrange("r h -> (r h)"))
    w.dma("gpsimd", P["qrow"][0:1, :], io["qn_b"])
    row_ld = w.tok("gp")
    w.wait("vector", "gp", row_ld[1])
    w.dve("tensor_max", P["krow"][0:1, :], P["grow"][0:1, 0:H],
          P["grow"][0:1, H:2 * H], dr=True)
    for r in range(2, C.GROUP):
        w.dve("tensor_max", P["krow"][0:1, :], P["krow"][0:1, :],
              P["grow"][0:1, r * H:(r + 1) * H], dr=True)
    w.dve("tensor_mul", P["prow"][0:1, :], P["qrow"][0:1, :], P["krow"][0:1, :], dr=True)
    p_t = w.tok("dve")
    w.wait("scalar", "dve", p_t[1])
    w.act("activation", P["nrow"][0:1, :], P["prow"][0:1, :], AF.Sqrt,
          bias=0.0, scale=1.02 / 64.0)
    sq_t = w.tok("act")
    w.wait("vector", "act", sq_t[1])
    w.dve("tensor_scalar_mul", P["nrow_b"][0:1, :], P["nrow"][0:1, :], -1.0, dr=True)
    nb_t = w.tok("dve")
    w.wait("tensor", "dve", nb_t[1])
    w.await_tok("tensor", pstok.get(id(psA[3])))
    w.mm(psA[3][:, 0:H], P["ones1"][0:1, :], P["nrow_b"][0:1, :], inc="pe")
    nm_t = w.tok("pe")
    w.wait("vector", "pe", nm_t[1])
    w.dve("tensor_copy", P["negm"][:], psA[3][:, 0:H])
    negm_tok = w.tok("dve")

    for sem in list(w.c):
        if sem not in ("cc",):
            for e in ("sync", "scalar", "vector", "gpsimd", "tensor"):
                w.wait(e, sem)
    alloc_B()
    kbuf, vbuf, oT = B["kbuf"], B["vbuf"], B["oT"]
    hT_B, xn_q, x1_q, qt_q = B["hT_B"], B["xn_q"], B["x1_q"], B["qt_q"]
    aoT = hT_B
    psB = B["ps"]

    kv_war = [None, None]
    piece_ld = {}

    def load_k_hp(pc, hp, gate):
        slot = pc % 2
        q, fam = (("sync", f"syk{slot}") if hp % 2 == 0
                  else ("gpsimd", f"syvp{slot}"))
        w.await_tok(q, gate)
        w.wait(q, "cc")
        w.dma(q,
              kbuf[slot][:, hp * KW:(hp + 1) * KW]
              .rearrange("p (r t) -> p r t", t=PIECE),
              io["kp_g"][pc].rearrange("(r hh p) t -> p hh r t",
                                       p=128, hh=HP)[:, hp, :, :],
              sem=fam)

    def load_v(qb_, pc, gate):
        slot = pc % 2
        hk = KCP // 2
        for half in range(2):
            q, fam = (("sync", f"syk{slot}") if half == 0
                      else ("gpsimd", f"syvp{slot}"))
            w.await_tok(q, gate)
            w.wait(q, "cc")
            w.dma(q,
                  vbuf[slot][:, half * hk * H * VW:(half + 1) * hk * H * VW]
                  .rearrange("p (tc f) -> p tc f", f=H * VW),
                  io["vp_g"][pc].rearrange("(tc p) f -> p tc f", p=128)
                  [:, half * hk:(half + 1) * hk, :], sem=fam)
        piece_ld[(qb_, pc)] = [w.tok(f"syk{slot}"), w.tok(f"syvp{slot}")]

    def load_piece(qb_, pc):
        for hp in range(HP):
            load_k_hp(pc, hp, kv_war[pc % 2])
        load_v(qb_, pc, kv_war[pc % 2])

    for qb in range(NQB):
        q0 = qb * QB

        w.wait("sync", "scs", spill_tok[1])
        w.await_tok("sync", war.get("qt_q"))
        for c in range(DC):
            w.dma("sync", qt_q[:, c * QB:(c + 1) * QB],
                  io["s_qT"][c * 128:(c + 1) * 128, q0:q0 + QB], sem="syq")
        qt_tok = w.tok("syq")
        w.await_tok("sync", war.get("x1q"))
        for c in range(DC):
            w.dma("sync", x1_q[:, c * QB:(c + 1) * QB],
                  io["x1sp"][c * 128:(c + 1) * 128, q0:q0 + QB], sem="sy1")
        x1q_tok = w.tok("sy1")

        if qb == 0:
            load_piece(0, 0)
            if NP > 1:
                load_piece(0, 1)

        units = [(pc, hp, kc) for pc in range(NP) for hp in range(HP)
                 for kc in range(KCP)]
        S_free = [None, None]
        exp_tok = {}
        av_tok = {}
        oT_free = {}

        def emit_av(u):
            pc_, hp_, kc_ = units[u]
            slot_ = pc_ % 2
            sp_ = u % 2
            w.wait("tensor", "act", exp_tok[u][1])
            if kc_ == 0:
                w.await_tok("tensor", oT_free.get(hp_ % 2))
                w.await_tok("tensor", pstok.get(id(psB[4 + 2 * (hp_ % 2)])))
                w.await_tok("tensor", pstok.get(id(psB[5 + 2 * (hp_ % 2)])))
            vb0 = kc_ * H * VW
            w.mm(psB[4 + 2 * (hp_ % 2)][0:65, :],
                 vbuf[slot_][:, vb0 + 2 * hp_ * VW: vb0 + 2 * hp_ * VW + VW],
                 B["Pt"][2 * sp_][:], start=(kc_ == 0), stop=(kc_ == KCP - 1))
            w.mm(psB[5 + 2 * (hp_ % 2)][0:65, :],
                 vbuf[slot_][:, vb0 + (2 * hp_ + 1) * VW: vb0 + (2 * hp_ + 1) * VW + VW],
                 B["Pt"][2 * sp_ + 1][:], start=(kc_ == 0),
                 stop=(kc_ == KCP - 1), inc="pe")
            av_tok[u] = w.tok("pe")
            if kc_ == KCP - 1:
                w.wait("vector", "pe", av_tok[u][1])
                for side in range(2):
                    head = 2 * hp_ + side
                    ps_o = psB[4 + 2 * (hp_ % 2) + side]
                    w.await_tok("vector", war.get(("oT", head)))
                    if pc_ == 0:
                        w.dve("tensor_copy", oT[head][:, :], ps_o[0:65, :])
                    else:
                        w.dve("tensor_add", oT[head][:, :], oT[head][:, :],
                              ps_o[0:65, :])
                    pstok[id(ps_o)] = war[("oT", head)] = w.tok("dve")
                oT_free[hp_ % 2] = w.tok("dve")
                if hp_ == HP - 1:
                    kv_war[slot_] = w.tok("pe")
                    if pc_ + 2 < NP:
                        load_v(qb, pc_ + 2, kv_war[slot_])
                    elif qb + 1 < NQB:
                        load_v(qb + 1, pc_ + 2 - NP, kv_war[slot_])

        for ui, (pc, hp, kc) in enumerate(units):
            slot = pc % 2
            sp = ui % 2
            if kc == 0 and hp == 0:
                w.await_toks("tensor", piece_ld[(qb, pc)])
                w.await_tok("tensor", qt_tok)
            w.await_tok("tensor", S_free[sp])
            w.await_tok("tensor", pstok.get(id(psB[2 * sp])))
            w.await_tok("tensor", pstok.get(id(psB[2 * sp + 1])))
            kb0 = hp * KW + kc * 128
            w.mm(psB[2 * sp][:, :], kbuf[slot][0:64, kb0:kb0 + 128],
                 qt_q[0:64, hp * QB:(hp + 1) * QB], tp=(0, 0))
            w.mm(psB[2 * sp + 1][:, :], kbuf[slot][64:128, kb0:kb0 + 128],
                 qt_q[64:128, hp * QB:(hp + 1) * QB], tp=(64, 0), inc="pe")
            qk_t = w.tok("pe")
            if kc == KCP - 1:
                if pc + 2 < NP:
                    load_k_hp(pc + 2, hp, qk_t)
                elif qb + 1 < NQB:
                    load_k_hp(pc + 2 - NP, hp, qk_t)
            w.wait("scalar", "pe", qk_t[1])
            if ui == 0 and qb == 0:
                w.wait("scalar", "dve", negm_tok[1])
            w.await_tok("scalar", av_tok.get(ui - 2))
            w.act("activation", B["Pt"][2 * sp][:], psB[2 * sp][:, :], AF.Exp,
                  bias=P["negm"][:, 2 * hp:2 * hp + 1], scale=C.ISCALE)
            w.act("activation", B["Pt"][2 * sp + 1][:], psB[2 * sp + 1][:, :],
                  AF.Exp, bias=P["negm"][:, 2 * hp + 1:2 * hp + 2],
                  scale=C.ISCALE)
            S_free[sp] = exp_tok[ui] = w.tok("act")
            pstok[id(psB[2 * sp])] = pstok[id(psB[2 * sp + 1])] = exp_tok[ui]
            if ui >= 1:
                emit_av(ui - 1)
        emit_av(len(units) - 1)

        att_t = w.tok("dve")

        bc_war = [None, None]
        w.wait("vector", "dve", att_t[1])
        for head in range(H):
            hp, side = head // 2, head % 2
            w.dve("reciprocal", B["recf"], oT[head][64:65, :], dr=True)
            w.dve("tensor_copy", B["recl"], B["recf"])
            rc_t = w.tok("dve")
            ps_bc = psB[head % 2]
            w.wait("tensor", "dve", rc_t[1])
            w.await_tok("tensor", bc_war[head % 2])
            w.await_tok("tensor", pstok.get(id(ps_bc)))
            w.mm(ps_bc[0:64, :], P["ones1"][0:1, 0:64], B["recl"],
                 inc="pe")
            bm_t = w.tok("pe")
            w.wait("vector", "pe", bm_t[1])
            w.dve("tensor_mul",
                  aoT[side * 64:(side + 1) * 64, hp * QB:(hp + 1) * QB],
                  oT[head][0:64, :], ps_bc[0:64, :])
            bc_war[head % 2] = pstok[id(ps_bc)] = w.tok("dve")
        aoT_t = w.tok("dve")

        w.await_tok("tensor", pstok.get(id(psB[2])))
        w.wait("tensor", "dve", aoT_t[1])
        for c in range(DC):
            w.mm(psB[2][0:R, :], P["aTs_o"][:, c * R:(c + 1) * R],
                 aoT[:, c * QB:(c + 1) * QB], start=(c == 0), stop=(c == DC - 1),
                 inc="pe" if c == DC - 1 else None)
        uo_t = w.tok("pe")
        w.wait("vector", "pe", uo_t[1])
        w.await_tok("vector", war.get(("u_s", 0)))
        w.dve("tensor_scalar_mul", P["u_s"][0][:, :], psB[2][0:R, :], C.LORA)
        pstok[id(psB[2])] = uo_e = w.tok("dve")
        for ot in range(DC):
            wsl = ot % 2
            q = "sync"
            w.await_tok(q, war.get(("wtile", wsl)))
            w.dma(q, P["wtile"][wsl][:], io["s_wo"][ot], sem=f"syw{wsl}")
            ps = psB[ot % 2]
            w.wait("tensor", f"syw{wsl}")
            w.await_tok("tensor", pstok.get(id(ps)))
            for c in range(DC):
                w.mm(ps[:, :], P["wtile"][wsl][:, c * 128:(c + 1) * 128],
                     aoT[:, c * QB:(c + 1) * QB], start=(c == 0), stop=False)
            w.wait("tensor", "dve", uo_e[1])
            w.mm(ps[:, :], P["bTs_o"][:, ot * 128:(ot + 1) * 128],
                 P["u_s"][0][0:R, :], start=False, stop=True, inc="pe")
            om_t = w.tok("pe")
            war[("wtile", wsl)] = om_t
            war[("u_s", 0)] = om_t
            w.wait("vector", "pe", om_t[1])
            w.await_tok("vector", x1q_tok)
            w.dve("scalar_tensor_tensor", x1_q[:, ot * QB:(ot + 1) * QB],
                  ps[:, :], P["bias_o"][:, ot:ot + 1],
                  x1_q[:, ot * QB:(ot + 1) * QB], op0=ALU.add, op1=ALU.add)
            pstok[id(ps)] = pstok["x2t"] = w.tok("dve")
        war["qt_q"] = w.tok("pe")

        if KDBG == "x2":
            x2_t = w.tok("dve")
            for ot in range(DC):
                w.wait("scalar", "dve", x2_t[1])
                w.dma("scalar", io["outT"][ot * 128:(ot + 1) * 128, q0:q0 + QB],
                      x1_q[:, ot * QB:(ot + 1) * QB], sem="sco")
            war["x1q"] = w.tok("sco")
            continue

        ln3_t = ln(lambda c: x1_q[:, c * QB:(c + 1) * QB],
                   lambda c: xn_q[:, c * QB:(c + 1) * QB],
                   (psB[3], psB[4], psB[5], psB[6]),
                   src_tok=pstok.get("x2t"))

        st_toks = {}
        ffn(lambda c: xn_q[:, c * QB:(c + 1) * QB],
            lambda c: x1_q[:, c * QB:(c + 1) * QB],
            lambda c: x1_q[:, c * QB:(c + 1) * QB],
            (psB[0], psB[1], psB[2], psB[3]), hT_B, ln3_t,
            dst_cb=lambda ot, tok: st_toks.__setitem__(ot, tok),
            res_tok=pstok.get("x2t"))

        for ot in range(DC):
            w.wait("scalar", "dve", st_toks[ot][1])
            w.dma("scalar", io["outT"][ot * 128:(ot + 1) * 128, q0:q0 + QB],
                  x1_q[:, ot * QB:(ot + 1) * QB], sem="sco")
        war["x1q"] = w.tok("sco")

    w.wait("scalar", "sco")
    w.wait("gpsimd", "cc")


def _prep_inputs(C, inputs, prefix="L0_"):
    f = np.float32
    D, TOK, H, R = C.D, C.TOK, C.HEADS, C.RANK
    DC, HT, HID = C.DC, C.HT, C.HID

    x = np.asarray(inputs[prefix + "attn_x"], dtype=f)

    def tile_w(wmat):
        wt = np.asarray(wmat, dtype=f).T
        t = wt.reshape(DC, 128, DC, 128).transpose(2, 1, 0, 3).reshape(DC, 128, D)
        return np.ascontiguousarray(t)

    def tile_in(wmat, nt, ncols):
        wt = np.asarray(wmat).T
        ic = wt.shape[0] // 128
        t = wt.reshape(ic, 128, nt, 128).transpose(2, 1, 0, 3).reshape(nt, 128, ncols)
        return np.ascontiguousarray(t.astype(np.int32))

    common = dict(ident=np.eye(128, dtype=ml_dtypes.bfloat16))
    for p, nm in zip("qkvo", ["q", "k", "v", "o"]):
        common[f"w{p}T"] = tile_w(inputs[f"{prefix}attn_W{nm}"])
        common[f"b{p}"] = np.ascontiguousarray(
            np.asarray(inputs[f"{prefix}attn_b{nm}"], dtype=f).reshape(D, 1))
        common[f"a{p}T"] = np.ascontiguousarray(
            np.asarray(inputs[f"{prefix}attn_A{nm}"], dtype=f).T)
        common[f"b{p}T"] = np.ascontiguousarray(
            np.asarray(inputs[f"{prefix}attn_B{nm}"], dtype=f).T)
    common["wgT_t"] = tile_in(inputs[prefix + "ffn_w_gate"], HT, D)
    common["wuT_t"] = tile_in(inputs[prefix + "ffn_w_up"], HT, D)
    common["wdT_t"] = tile_in(inputs[prefix + "ffn_w_down"], DC, HID)

    B = x.shape[0]
    cores_per_b = C.N_CORES // B
    in_maps = []
    for core in range(C.N_CORES):
        b = core // cores_per_b
        t0 = (core % cores_per_b) * TOK
        m = dict(common)
        m["xT"] = np.ascontiguousarray(x[b, t0:t0 + TOK, :].T)
        in_maps.append(m)
    return in_maps, x.shape


def _gather_out(C, results, shape):
    B, S, D = shape
    cores_per_b = C.N_CORES // B
    out = np.empty(shape, dtype=np.float32)
    for core in range(C.N_CORES):
        b = core // cores_per_b
        t0 = (core % cores_per_b) * C.TOK
        out[b, t0:t0 + C.TOK, :] = results[core]["outT"].T
    return out


def kernel(**inputs):
    C = Cfg()
    in_maps, shape = _prep_inputs(C, inputs)
    nc = build(C)
    res = run_bass_kernel_spmd(nc, in_maps, core_ids=list(range(C.N_CORES)))
    return _gather_out(C, res.results, shape)
